# version 08
# baseline (speedup 1.0000x reference)
import bisect
import contextlib
import numpy as np
import ml_dtypes
import concourse.bass as bass
import concourse.mybir as mybir
from concourse.bass_utils import run_bass_kernel_spmd

F32 = mybir.dt.float32
BF16 = mybir.dt.bfloat16
U8 = mybir.dt.uint8
AF = mybir.ActivationFunctionType
ALU = mybir.AluOpType
AX = mybir.AxisListType

ENGINES = ("sync", "tensor", "vector", "scalar", "gpsimd")

B_CORE = 2
T = 2048
D = 1024
ALL_STAGES = ("s0", "s1", "s2", "s3", "s4", "s5", "s6", "s7")
S2_PHASES = 99
S2_LAT_LIMIT = 99
S2_LAT_MODE = 'full'



class _Op:
    __slots__ = ("eng", "fn", "reads", "writes", "is_dma", "pos", "dma_idx",
                 "deps", "needs_inc", "inc_value", "waits", "drain_before", "tag",
                 "gidx", "same_dep_max", "wait_ops")

    def __init__(self):
        self.deps = []
        self.needs_inc = False
        self.inc_value = None
        self.waits = []
        self.drain_before = False
        self.dma_idx = None
        self.tag = ""
        self.gidx = 0
        self.same_dep_max = -1
        self.wait_ops = []


class Emitter:

    def __init__(self, nc, stack):
        self.nc = nc
        self.stack = stack
        self.per_eng = {e: [] for e in ENGINES}
        self.last_write = {}
        self.readers = {}
        self.sem = {e: stack.enter_context(nc.semaphore(f"prog_{e}")) for e in ENGINES}
        self.dma_ring = {"sync": 8, "gpsimd": 4, "scalar": 2}
        self.dma_sem = {e: [stack.enter_context(nc.semaphore(f"dma_{e}{i}"))
                            for i in range(r)] for e, r in self.dma_ring.items()}
        self.dma_count = {e: 0 for e in ENGINES}
        self.gcount = 0
        self.nbar = 0

    def op(self, eng, fn, reads=(), writes=(), tag=""):
        o = _Op()
        o.eng, o.fn, o.reads, o.writes = eng, fn, tuple(reads), tuple(writes)
        o.is_dma = False
        o.pos = len(self.per_eng[eng])
        o.gidx = self.gcount
        self.gcount += 1
        o.tag = tag
        self._resolve(o)
        self.per_eng[eng].append(o)
        self._update(o)
        return o

    def dma(self, queue, out_ap, in_ap, reads=(), writes=(), tag="", **kw):
        assert queue in ("sync", "gpsimd", "scalar")
        o = _Op()
        o.eng = queue
        o.fn = lambda e, _o=out_ap, _i=in_ap, _kw=kw: e.dma_start(out=_o, in_=_i, **_kw)
        o.reads, o.writes = tuple(reads), tuple(writes)
        o.is_dma = True
        o.pos = len(self.per_eng[queue])
        o.gidx = self.gcount
        self.gcount += 1
        o.dma_idx = self.dma_count[queue]
        self.dma_count[queue] += 1
        o.tag = tag
        self._resolve(o)
        R = self.dma_ring[queue]
        i = o.dma_idx
        if i >= R:
            o.waits.append((("dmaslot", queue, i % R), 16 * (i // R)))
        self.per_eng[queue].append(o)
        self._update(o)
        return o

    def barrier(self):
        n = self.nbar
        self.nbar += 1
        keys = [f"__bar{n}_{e}" for e in ENGINES]
        for e, k in zip(ENGINES, keys):
            self.op(e, None, reads=(), writes=(k,), tag=f"bar{n}")
        for e in ENGINES:
            self.op(e, None, reads=keys, writes=(), tag=f"barw{n}")

    def _resolve(self, o):
        deps = {}
        for k in o.reads:
            w = self.last_write.get(k)
            if w is not None:
                deps[id(w)] = w
        for k in o.writes:
            w = self.last_write.get(k)
            if w is not None:
                deps[id(w)] = w
            for r in self.readers.get(k, ()):
                deps[id(r)] = r
        deps.pop(id(o), None)
        for d in deps.values():
            if d.eng == o.eng and not d.is_dma:
                if o.eng == "tensor":
                    continue
                if d.pos > o.same_dep_max:
                    o.same_dep_max = d.pos
            else:
                o.deps.append(d)

    def _update(self, o):
        for k in o.reads:
            self.readers.setdefault(k, []).append(o)
        for k in o.writes:
            self.last_write[k] = o
            self.readers[k] = []

    def emit(self):
        all_ops = sorted((o for e in ENGINES for o in self.per_eng[e]),
                         key=lambda o: o.gidx)
        inc_list = {e: [] for e in ENGINES}
        for o in all_ops:
            for d in o.deps:
                if d.is_dma:
                    o.wait_ops.append(d)
                    continue
                lst = inc_list[d.eng]
                j = bisect.bisect_left(lst, d.pos, key=lambda x: x.pos)
                if j < len(lst) and lst[j].gidx < o.gidx:
                    o.wait_ops.append(lst[j])
                else:
                    d.needs_inc = True
                    bisect.insort(lst, d, key=lambda x: x.pos)
                    o.wait_ops.append(d)
        for e in ENGINES:
            v = 0
            for o in self.per_eng[e]:
                if o.needs_inc and not o.is_dma:
                    v += 1
                    o.inc_value = v
        for e in ENGINES:
            if e == "tensor":
                continue
            covered = -1.0
            for o in self.per_eng[e]:
                if o.same_dep_max > covered:
                    o.drain_before = True
                    covered = o.pos - 0.5
                if o.needs_inc and not o.is_dma:
                    covered = o.pos
        for e in ENGINES:
            hw = {}
            for o in self.per_eng[e]:
                pre = list(o.waits)
                o.waits = []
                for (key, val) in pre:
                    if val > hw.get(key, 0):
                        o.waits.append((key, val))
                        hw[key] = val
                for d in o.wait_ops:
                    if d.is_dma:
                        R = self.dma_ring[d.eng]
                        key = ("dmaslot", d.eng, d.dma_idx % R)
                        val = 16 * (d.dma_idx // R + 1)
                    else:
                        key = ("prog", d.eng)
                        val = d.inc_value
                    if val > hw.get(key, 0):
                        o.waits.append((key, val))
                        hw[key] = val

        nc = self.nc
        block = self.stack.enter_context(nc.Block())

        def make_runner(eng_name):
            def run(E):
                for o in self.per_eng[eng_name]:
                    if o.drain_before:
                        E.drain()
                    for (sem_key, val) in o.waits:
                        if sem_key[0] == "prog":
                            s = self.sem[sem_key[1]]
                        else:
                            s = self.dma_sem[sem_key[1]][sem_key[2]]
                        E.wait_ge(s, val)
                    instr = o.fn(E) if o.fn is not None else None
                    if o.is_dma:
                        R = self.dma_ring[o.eng]
                        instr.then_inc(self.dma_sem[o.eng][o.dma_idx % R], 16)
                    elif o.needs_inc:
                        if o.eng == "tensor" and instr is not None:
                            instr.then_inc(self.sem[o.eng], 1)
                        else:
                            E.drain().then_inc(self.sem[o.eng], 1)
            return run

        for e in ENGINES:
            if self.per_eng[e]:
                getattr(block, e)(make_runner(e))
        return block



def build_consts():
    c = {}
    c["ident_bf"] = np.eye(128, dtype=ml_dtypes.bfloat16)
    c["ident_f32"] = np.eye(128, dtype=np.float32)

    def rot_mt(h):
        m = np.zeros((128, 128), np.float32)
        for i in range(128):
            j = i % h
            base = i - j
            if j < h // 2:
                m[i, base + j + h // 2] = -1.0
            else:
                m[i, base + j - h // 2] = 1.0
        return np.ascontiguousarray(m.T.astype(ml_dtypes.bfloat16))
    c["rot64_mt"] = rot_mt(64)
    c["rot32_mt"] = rot_mt(32)

    def rope_tab(dh):
        inv = 1.0 / (10000.0 ** (np.arange(0, dh, 2, dtype=np.float64) / dh))
        freqs = np.arange(T, dtype=np.float64)[:, None] * inv[None, :]
        emb = np.concatenate([freqs, freqs], axis=-1)
        cosF = np.tile(np.cos(emb).T, (128 // dh, 1)).astype(ml_dtypes.bfloat16)
        sinF = np.tile(np.sin(emb).T, (128 // dh, 1)).astype(ml_dtypes.bfloat16)
        return np.ascontiguousarray(cosF), np.ascontiguousarray(sinF)
    c["cos64"], c["sin64"] = rope_tab(64)
    c["cos32"], c["sin32"] = rope_tab(32)

    p = np.arange(128)[:, None]
    col = np.arange(1024)[None, :]
    c["stair"] = np.ascontiguousarray((col >= p + 512).astype(ml_dtypes.bfloat16))
    return c



class Ctx:
    def __init__(self, nc, stack, fused, stages):
        self.nc = nc
        self.stack = stack
        self.E = Emitter(nc, stack)
        self.fused = fused
        self.stages = stages
        self.dram = {}
        self.kinds = {}
        self._uid = 0
        self.consts = {}
        self.ps = []

    def uid(self):
        self._uid += 1
        return self._uid

    def din(self, name, shape, dtype=F32):
        if name not in self.dram:
            self.dram[name] = self.nc.dram_tensor(name, list(shape), dtype, kind="ExternalInput").ap()
            self.kinds[name] = "ExternalInput"
        return self.dram[name]

    def dout(self, name, shape, dtype=F32):
        if name not in self.dram:
            self.dram[name] = self.nc.dram_tensor(name, list(shape), dtype, kind="ExternalOutput").ap()
            self.kinds[name] = "ExternalOutput"
        return self.dram[name]

    def dio(self, name, shape, dtype=F32, producer=None, consumers=()):
        if name in self.dram:
            return self.dram[name]
        if self.fused:
            kind = "Internal"
        else:
            prod = producer in self.stages
            cons = any(s in self.stages for s in consumers)
            if prod and cons:
                kind = "Internal"
            elif prod:
                kind = "ExternalOutput"
            elif cons:
                kind = "ExternalInput"
            else:
                kind = "Internal"
        self.dram[name] = self.nc.dram_tensor(name, list(shape), dtype, kind=kind).ap()
        self.kinds[name] = kind
        return self.dram[name]

    def dscratch(self, name, shape, dtype=F32):
        if name not in self.dram:
            self.dram[name] = self.nc.dram_tensor(name, list(shape), dtype, kind="Internal").ap()
            self.kinds[name] = "Internal"
        return self.dram[name]

    def sb(self, scope, name, shape, dtype=F32):
        return scope.enter_context(self.nc.sbuf_tensor(f"{name}_{self.uid()}", list(shape), dtype))

    def init_psum(self):
        self.ps_all = self.stack.enter_context(self.nc.psum_tensor("psall", [128, 4096], F32))
        for i in range(8):
            self.ps.append(self.ps_all[:, i * 512:(i + 1) * 512])

    def psbf(self, i):
        return self.ps[i][:, 0:64].bitcast(BF16)

    def psbf6(self, i, slot):
        return self.ps[i][:, slot * 64:(slot + 1) * 64].bitcast(BF16)

    def load_consts(self, host_consts, names=None, scope=None):
        E = self.E
        for name, arr in host_consts.items():
            if names is not None and name not in names:
                continue
            dt = BF16 if arr.dtype == ml_dtypes.bfloat16 else F32
            d = self.din("c_" + name, arr.shape, dt)
            t = self.sb(scope or self.stack, "c_" + name, list(arr.shape), dt)
            E.dma("sync", t[:], d[:, :], reads=(), writes=("c_" + name,), tag="const")
            self.consts[name] = t
        if not hasattr(self, "_sconsts"):
            self._sconsts = {}
            for v in (1e-5, 1.0):
                self.sconst(v)

    def sconst(self, val):
        if val not in self._sconsts:
            t = self.sb(self.stack, f"sc_{len(self._sconsts)}", [128, 1], F32)
            self.E.op("vector", lambda e, t=t, val=val: e.memset(t[:], float(val)),
                      reads=(), writes=(f"sconst_{val}",))
            self._sconsts[val] = (t, f"sconst_{val}")
        return self._sconsts[val]

    @contextlib.contextmanager
    def stage(self, name):
        if name not in self.stages:
            yield None
            return
        with contextlib.ExitStack() as s:
            yield s
        self.E.barrier()



def tile_slice(i, w=128):
    return slice(i * w, (i + 1) * w)


def emit_ln_tile(ctx, sc, x_ap, kx, out_ap, kout, tag, d_model=D, eps=1e-5):
    E = ctx.E
    sq, ssq, sm, var, m_t, r_t = sc
    E.op("scalar", lambda e: e.activation(sq[:], x_ap, AF.Square, accum_out=ssq[:, :]),
         reads=[kx], writes=[tag + "sq", tag + "ssq"])
    E.op("vector", lambda e: e.tensor_reduce(sm[:].unsqueeze(2), x_ap.unsqueeze(1), AX.X, ALU.add),
         reads=[kx], writes=[tag + "sm"])
    E.op("vector", lambda e: e.tensor_scalar(m_t[:], sm[:], 1.0 / d_model, None, ALU.mult),
         reads=[tag + "sm"], writes=[tag + "m"])
    E.op("vector", lambda e: e.tensor_tensor(sm[:], m_t[:], m_t[:], ALU.mult),
         reads=[tag + "m"], writes=[tag + "sm"])
    E.op("vector", lambda e: e.tensor_scalar(var[:], ssq[:], 1.0 / d_model, None, ALU.mult),
         reads=[tag + "ssq"], writes=[tag + "v"])
    E.op("vector", lambda e: e.tensor_tensor(var[:], var[:], sm[:], ALU.subtract),
         reads=[tag + "v", tag + "sm"], writes=[tag + "v"])
    epst, keps = ctx.sconst(eps)
    E.op("scalar", lambda e: e.activation(var[:], var[:], AF.Ln, bias=epst[:, :]),
         reads=[tag + "v", keps], writes=[tag + "v"])
    E.op("scalar", lambda e: e.activation(r_t[:], var[:], AF.Exp, scale=-0.5),
         reads=[tag + "v"], writes=[tag + "r"])
    E.op("vector", lambda e: e.tensor_scalar(out_ap, x_ap, m_t[:], r_t[:], ALU.subtract, ALU.mult),
         reads=[kx, tag + "m", tag + "r"], writes=[kout])


def make_ln_scratch(ctx, S, pfx):
    return (ctx.sb(S, pfx + "sq", [128, D], F32),
            ctx.sb(S, pfx + "ssq", [128, 1], F32),
            ctx.sb(S, pfx + "sm", [128, 1], F32),
            ctx.sb(S, pfx + "var", [128, 1], F32),
            ctx.sb(S, pfx + "m", [128, 1], F32),
            ctx.sb(S, pfx + "r", [128, 1], F32))


def emit_transpose_tile(ctx, dst_ap, kdst, src_ap, ksrc, dtype, psum_bank, kpsum, scale_ap=None, out_dtype=None):
    E = ctx.E
    ident = ctx.consts["ident_bf"] if dtype == BF16 else ctx.consts["ident_f32"]
    ps = psum_bank
    pview = ps[:, 0:128] if dtype != BF16 else ps[:, 0:128]
    E.op("tensor", lambda e: e.transpose(pview, src_ap, ident[:]),
         reads=[ksrc], writes=[kpsum])
    if scale_ap is None:
        E.op("scalar", lambda e: e.copy(dst_ap, pview), reads=[kpsum], writes=[kdst])
    else:
        E.op("scalar", lambda e: e.activation(dst_ap, pview, AF.Copy, scale=scale_ap),
             reads=[kpsum, "gain"], writes=[kdst])



def emit_s0(ctx):
    E = ctx.E
    x_d = ctx.din("x", (B_CORE, T, D), F32)
    xn_d = ctx.dio("xn", (B_CORE, T, D), F32, producer="s0", consumers=("s2", "s3"))
    xnT_d = ctx.dio("xnT", (B_CORE, D, T), BF16, producer="s0", consumers=("s1", "s2", "s3"))

    with ctx.stage("s0") as S:
        if S is None:
            return
        xt = [ctx.sb(S, "s0x", [128, D], F32) for _ in range(2)]
        xn_t = [ctx.sb(S, "s0n", [128, D], F32) for _ in range(2)]
        stg = [ctx.sb(S, "s0stg", [128, 512], BF16) for _ in range(2)]
        sc = make_ln_scratch(ctx, S, "s0")

        for b in range(B_CORE):
            for tt in range(T // 128):
                p = (b * 16 + tt) % 2
                kx = f"s0x{p}"
                kn = f"s0n{p}"
                E.dma("sync", xt[p][:], x_d[b, tile_slice(tt), :], writes=[kx], tag="s0ld")
                emit_ln_tile(ctx, sc, xt[p][:], kx, xn_t[p][:], kn, "s0ln")
                E.dma("sync", xn_d[b, tile_slice(tt), :], xn_t[p][:], reads=[kn],
                      writes=[f"d_xn_{b}_{tt}"], tag="s0st")
                for dt_i in range(D // 128):
                    bk = 6 + dt_i // 4
                    E.op("tensor", lambda e, p=p, dt_i=dt_i, bk=bk: e.transpose(
                        ctx.ps[bk][:, (dt_i % 4) * 128:(dt_i % 4 + 1) * 128],
                        xn_t[p][:, tile_slice(dt_i)], ctx.consts["ident_f32"][:]),
                        reads=[kn, "c_ident_f32"], writes=[f"ps{bk}"])
                for gg in range(2):
                    bk = 6 + gg
                    E.op("scalar", lambda e, gg=gg, bk=bk: e.copy(stg[gg][:], ctx.ps[bk][:, :]),
                         reads=[f"ps{bk}"], writes=[f"s0stg{gg}"])
                    E.dma("sync",
                          xnT_d[b, gg * 512:(gg + 1) * 512, tile_slice(tt)]
                          .rearrange("(a p) c -> p a c", p=128),
                          stg[gg][:].rearrange("p (a c) -> p a c", c=128),
                          reads=[f"s0stg{gg}"], writes=[f"d_xnT_{b}_{gg}_{tt}"], tag="s0stT")



def emit_rope_pair(ctx, S, out_h, kout, src_h, ksrc, rot_mt_name, cos_name, sin_name,
                   rtmp, ps_bank, kps, tagu):
    E = ctx.E
    krt = f"rtmp_{id(rtmp)}"
    rot = ctx.consts[rot_mt_name]
    cosc = ctx.consts[cos_name]
    sinc = ctx.consts[sin_name]
    for tc in range(4):
        E.op("tensor", lambda e, tc=tc: e.matmul(
            ps_bank[:, 0:512], rot[:], src_h[:, tile_slice(tc, 512)], start=True, stop=True,
            skip_group_check=True),
            reads=[ksrc, "c_" + rot_mt_name], writes=[kps], tag=tagu + "rot")
        E.op("vector", lambda e, tc=tc: e.tensor_tensor(
            rtmp[:, tile_slice(tc, 512)], ps_bank[:, 0:512], sinc[:, tile_slice(tc, 512)], ALU.mult),
            reads=[kps, "c_" + sin_name], writes=[krt], tag=tagu + "sin")
    E.op("vector", lambda e: e.tensor_tensor(out_h[:], src_h[:], cosc[:], ALU.mult),
         reads=[ksrc, "c_" + cos_name], writes=[kout], tag=tagu + "cos")
    E.op("vector", lambda e: e.tensor_tensor(out_h[:], out_h[:], rtmp[:], ALU.add),
         reads=[kout, krt], writes=[kout], tag=tagu + "add")


def emit_s1(ctx):
    E = ctx.E
    nc = ctx.nc
    xnT_d = ctx.dio("xnT", (B_CORE, D, T), BF16, producer="s0", consumers=("s1", "s2", "s3"))
    s0_d = ctx.dio("s0", (B_CORE, T, D), F32, producer="s1", consumers=("s4",))
    zs_d = ctx.dscratch("s1_zs", (B_CORE, 16, T), F32)
    zr_d = ctx.dscratch("s1_zr", (B_CORE, 16, T), F32)

    w_dk = ctx.din("w_L0_attn_w_dk", (D, 128), F32)
    w_dv = ctx.din("w_L0_attn_w_dv", (D, 128), F32)
    w_uk = ctx.din("w_L0_attn_w_uk", (128, D), F32)
    w_uv = ctx.din("w_L0_attn_w_uv", (128, D), F32)
    w_o = ctx.din("w_L0_attn_w_o", (D, D), F32)
    b_dk = ctx.din("w_L0_attn_b_dk", (128,), F32)
    b_dv = ctx.din("w_L0_attn_b_dv", (128,), F32)
    b_uk = ctx.din("w_L0_attn_b_uk", (D,), F32)
    b_uv = ctx.din("w_L0_attn_b_uv", (D,), F32)
    b_o = ctx.din("w_L0_attn_b_o", (D,), F32)

    with ctx.stage("s1") as S:
        if S is None:
            return
        wdk = ctx.sb(S, "wdk", [128, 8 * 128], BF16)
        wdv = ctx.sb(S, "wdv", [128, 8 * 128], BF16)
        wuk = ctx.sb(S, "wuk", [128, D], BF16)
        wuv = ctx.sb(S, "wuv", [128, D], BF16)
        wo = ctx.sb(S, "wo", [128, 8 * D], BF16)
        E.dma("gpsimd", wdk[:].rearrange("p (h c) -> p h c", h=8),
              w_dk.rearrange("(h p) c -> p h c", p=128), writes=["wdk"])
        E.dma("gpsimd", wdv[:].rearrange("p (h c) -> p h c", h=8),
              w_dv.rearrange("(h p) c -> p h c", p=128), writes=["wdv"])
        E.dma("gpsimd", wuk[:], w_uk[:, :], writes=["wuk"])
        E.dma("gpsimd", wuv[:], w_uv[:, :], writes=["wuv"])
        E.dma("gpsimd", wo[:].rearrange("p (h o) -> p h o", h=8),
              w_o.rearrange("(h p) o -> p h o", p=128), writes=["wo"])
        bdk = ctx.sb(S, "bdk", [128, 1], F32)
        bdv = ctx.sb(S, "bdv", [128, 1], F32)
        buk = ctx.sb(S, "buk", [128, 8], F32)
        buv = ctx.sb(S, "buv", [128, 8], F32)
        bo = ctx.sb(S, "bo", [128, 8], F32)
        E.dma("sync", bdk[:], b_dk.rearrange("(a p) -> p a", p=128), writes=["bdk"], allow_slow_non_contiguous=True)
        E.dma("sync", bdv[:], b_dv.rearrange("(a p) -> p a", p=128), writes=["bdv"], allow_slow_non_contiguous=True)
        E.dma("sync", buk[:], b_uk.rearrange("(a p) -> p a", p=128), writes=["buk"], allow_slow_non_contiguous=True)
        E.dma("sync", buv[:], b_uv.rearrange("(a p) -> p a", p=128), writes=["buv"], allow_slow_non_contiguous=True)
        E.dma("sync", bo[:], b_o.rearrange("(a p) -> p a", p=128), writes=["bo"], allow_slow_non_contiguous=True)

        xnt = [ctx.sb(S, "s1xnt", [128, T], BF16) for _ in range(8)]
        ck = ctx.sb(S, "s1ck", [128, T], BF16)
        cv = ctx.sb(S, "s1cv", [128, T], BF16)
        vsb = [ctx.sb(S, "s1v", [128, 16 * 65], BF16) for _ in range(16)]
        keff = ctx.sb(S, "s1ke", [128, T], BF16)
        qeff = ctx.sb(S, "s1qe", [128, T], BF16)
        ktmp = ctx.sb(S, "s1kt", [128, T], BF16)
        rtmp = ctx.sb(S, "s1rt", [128, T], BF16)
        psb = [ctx.sb(S, "s1p", [128, 1024], BF16) for _ in range(3)]
        avsb = [ctx.sb(S, "s1av", [128, T], BF16) for _ in range(8)]
        avn = avsb
        zstg = ctx.sb(S, "s1zstg", [1, 512], F32)
        zall = ctx.sb(S, "s1zall", [16, T], F32)
        zbc = ctx.sb(S, "s1zbc", [128, T], F32)
        sostg = ctx.sb(S, "s1so", [128, 512], F32)
        trstg = [ctx.sb(S, "s1tr", [128, 512], F32) for _ in range(2)]

        for b in range(B_CORE):
            kb = f"b{b}"
            for ht in range(8):
                E.dma("sync", xnt[ht][:], xnT_d[b, tile_slice(ht), :], writes=[f"xnt{ht}"], tag="s1ldx")
            for (csb, wsb, bias, kc, kw, kbias) in ((ck, wdk, bdk, "ck", "wdk", "bdk"),
                                                    (cv, wdv, bdv, "cv", "wdv", "bdv")):
                for tc in range(4):
                    bk = 6 + tc % 2
                    for ht in range(8):
                        E.op("tensor", lambda e, tc=tc, ht=ht, wsb=wsb, bk=bk: e.matmul(
                            ctx.ps[bk][:, :], wsb[:, tile_slice(ht)],
                            xnt[ht][:, tile_slice(tc, 512)], start=(ht == 0), stop=(ht == 7), skip_group_check=True),
                            reads=[f"xnt{ht}", kw], writes=[f"ps{bk}"], tag="s1c")
                    E.op("scalar", lambda e, tc=tc, csb=csb, bias=bias, bk=bk: e.activation(
                        csb[:, tile_slice(tc, 512)], ctx.ps[bk][:, :], AF.Identity, bias=bias[:, :]),
                        reads=[f"ps{bk}", kbias], writes=[kc], tag="s1cev")

            for tp in range(16):
                E.op("vector", lambda e, tp=tp: e.memset(vsb[tp][:], 1.0),
                     writes=[f"v{tp}"], tag="s1vm")
                for ch in range(2):
                    bk = 6 + ch
                    E.op("tensor", lambda e, tp=tp, ch=ch, bk=bk: e.matmul(
                        ctx.ps[bk][:, :], cv[:, tile_slice(tp)],
                        wuv[:, tile_slice(ch, 512)], start=True, stop=True, skip_group_check=True),
                        reads=["cv", "wuv"], writes=[f"ps{bk}"], tag="s1vmm")
                    for hh in range(8):
                        h = ch * 8 + hh
                        E.op("scalar", lambda e, tp=tp, h=h, hh=hh, bk=bk: e.copy(
                            vsb[tp][:, h * 65:h * 65 + 64], ctx.ps[bk][:, hh * 64:(hh + 1) * 64]),
                            reads=[f"ps{bk}"], writes=[f"v{tp}"], tag="s1vev")

            for pair in range(8):
                for tc in range(4):
                    bk = 6 + tc % 2
                    E.op("tensor", lambda e, tc=tc, pair=pair, bk=bk: e.matmul(
                        ctx.ps[bk][:, :], wuk[:, tile_slice(pair)],
                        ck[:, tile_slice(tc, 512)], start=True, stop=True, skip_group_check=True),
                        reads=["ck", "wuk"], writes=[f"ps{bk}"], tag="s1kmm")
                    E.op("scalar", lambda e, tc=tc, pair=pair, bk=bk: e.activation(
                        ktmp[:, tile_slice(tc, 512)], ctx.ps[bk][:, :], AF.Identity,
                        bias=buk[:, pair:pair + 1]),
                        reads=[f"ps{bk}", "buk"], writes=["ktmp"], tag="s1kev")
                emit_rope_pair(ctx, S, keff, "keff", ktmp, "ktmp",
                               "rot64_mt", "cos64", "sin64", rtmp, ctx.ps[6], "ps6", "s1kr")
                emit_rope_pair(ctx, S, qeff, "qeff", xnt[pair], f"xnt{pair}",
                               "rot64_mt", "cos64", "sin64", rtmp, ctx.ps[6], "ps6", "s1qr")

                for hh in range(2):
                    h = pair * 2 + hh
                    r0 = hh * 64
                    for qq in range(4):
                        avbank = 4 + (qq % 2)
                        PBMAP = ((0, 1), (2, 3), (6, 7))
                        for kp in range(8):
                            b0, b1 = PBMAP[kp % 3]
                            for half in range(2):
                                kk = kp * 2 + half
                                bb = (b0, b1)[half]
                                E.op("tensor", lambda e, kk=kk, qq=qq, r0=r0, bb=bb: e.matmul(
                                    ctx.ps[bb][:, :], keff[r0:r0 + 64, tile_slice(kk)],
                                    qeff[r0:r0 + 64, tile_slice(qq, 512)], start=True, stop=True, skip_group_check=True),
                                    reads=["keff", "qeff"], writes=[f"ps{bb}"], tag="s1smm")
                            E.op("scalar", lambda e, b0=b0, kp=kp: e.activation(
                                psb[kp % 3][:], ctx.ps_all[:, b0 * 512:(b0 + 2) * 512], AF.Exp, scale=0.125),
                                reads=[f"ps{b0}", f"ps{b0 + 1}"], writes=[f"p{kp % 3}"], tag="s1exp")
                            for half in range(2):
                                kk = kp * 2 + half
                                E.op("tensor", lambda e, kk=kk, h=h, kp=kp, half=half, avbank=avbank: e.matmul(
                                    ctx.ps[avbank][0:65, :], vsb[kk][:, h * 65:(h + 1) * 65],
                                    psb[kp % 3][:, half * 512:(half + 1) * 512],
                                    start=(kk == 0), stop=(kk == 15),
                                    skip_group_check=True),
                                    reads=[f"p{kp % 3}", f"v{kk}"], writes=[f"ps{avbank}"], tag="s1avmm")
                        E.op("scalar", lambda e, qq=qq, r0=r0, pair=pair, avbank=avbank: e.copy(
                            avsb[pair][r0:r0 + 64, tile_slice(qq, 512)], ctx.ps[avbank][0:64, :]),
                            reads=[f"ps{avbank}"], writes=[f"av{pair}"], tag="s1avev")
                        E.op("scalar", lambda e, avbank=avbank: e.copy(
                            zstg[:, :], ctx.ps[avbank][64:65, :]),
                            reads=[f"ps{avbank}"], writes=["zstg"], tag="s1zev")
                        E.dma("sync", zs_d[b, h, tile_slice(qq, 512)], zstg[:, :],
                              reads=["zstg"], writes=[f"zs{h}{qq}"], tag="s1zst")

            E.dma("sync", zall[:], zs_d[b],
                  reads=[f"zs{h}{qq}" for h in range(16) for qq in range(4)],
                  writes=["zall"], tag="s1zld")
            E.op("vector", lambda e: e.reciprocal(zall[:], zall[:]), reads=["zall"], writes=["zall"])
            E.dma("sync", zr_d[b], zall[:], reads=["zall"], writes=["zr"], tag="s1zrst")

            for pair in range(8):
                for hh in range(2):
                    E.dma("sync", zbc[hh * 64:(hh + 1) * 64, :],
                          zr_d[b, pair * 2 + hh:pair * 2 + hh + 1, :].broadcast_to([64, T]),
                          reads=["zr"], writes=["zbc"], tag="s1zbc")
                E.op("vector", lambda e, pair=pair: e.tensor_tensor(
                    avn[pair][:], avsb[pair][:], zbc[:], ALU.mult),
                    reads=[f"av{pair}", "zbc"], writes=[f"av{pair}", f"avn{pair}"], tag="s1nrm")
                E.op("vector", lambda e, pair=pair: e.tensor_scalar(
                    avn[pair][:], avn[pair][:], buv[:, pair:pair + 1], None, ALU.add),
                    reads=[f"avn{pair}", "buv"], writes=[f"avn{pair}"], tag="s1buv")

            for ot in range(8):
                for tc in range(4):
                    bk = 6 + tc % 2
                    E.op("tensor", lambda e, ot=ot, tc=tc, bk=bk: e.matmul(
                        ctx.ps[bk][:, :], wo[:, 0 * D + ot * 128: 0 * D + (ot + 1) * 128],
                        avn[0][:, tile_slice(tc, 512)], start=True, stop=False, skip_group_check=True),
                        reads=["avn0", "wo"], writes=[f"ps{bk}"], tag="s1omm")
                    for ht in range(1, 8):
                        E.op("tensor", lambda e, ot=ot, tc=tc, ht=ht, bk=bk: e.matmul(
                            ctx.ps[bk][:, :], wo[:, ht * D + ot * 128: ht * D + (ot + 1) * 128],
                            avn[ht][:, tile_slice(tc, 512)], start=False, stop=(ht == 7), skip_group_check=True),
                            reads=[f"avn{ht}", "wo"], writes=[f"ps{bk}"], tag="s1omm")
                    E.op("scalar", lambda e, ot=ot, tc=tc, bk=bk: e.activation(
                        sostg[:], ctx.ps[bk][:, :], AF.Identity, bias=bo[:, ot:ot + 1]),
                        reads=[f"ps{bk}", "bo"], writes=["sostg"], tag="s1oev")
                    tb = 2 + tc % 2
                    for t2 in range(4):
                        E.op("tensor", lambda e, t2=t2, tb=tb: e.transpose(
                            ctx.ps[tb][:, t2 * 128:(t2 + 1) * 128], sostg[:, tile_slice(t2)],
                            ctx.consts["ident_f32"][:]),
                            reads=["sostg", "c_ident_f32"], writes=[f"ps{tb}"], tag="s1otr")
                    pr = tc % 2
                    E.op("scalar", lambda e, tb=tb, pr=pr: e.copy(trstg[pr][:], ctx.ps[tb][:, :]),
                         reads=[f"ps{tb}"], writes=[f"trstg{pr}"], tag="s1otrev")
                    E.dma("sync", s0_d[b, tc * 512:(tc + 1) * 512, tile_slice(ot)]
                          .rearrange("(a p) c -> p a c", p=128),
                          trstg[pr][:].rearrange("p (a c) -> p a c", c=128),
                          reads=[f"trstg{pr}"], writes=[f"d_s0_{b}_{ot}_{tc}"], tag="s1ost")



def emit_load_wT(ctx, S, dram_ap, rows, cols, name, head_sliced=None, nat_buf=None):
    E = ctx.E
    nrt, nct = rows // 128, cols // 128
    wT = ctx.sb(S, name + "T", [128, nct * rows], BF16)
    if True:
        nat = nat_buf
        if head_sliced is None:
            E.dma("gpsimd", nat[:, 0:nrt * cols].rearrange("p (rt c) -> p rt c", rt=nrt),
                  dram_ap.rearrange("(rt p) c -> p rt c", p=128), writes=["prepnat"])
        else:
            src_full, nh, lo, tw = head_sliced
            hstride = src_full.shape[0] // nh
            per_tile = 128 // tw
            for h in range(nh):
                rt = h // per_tile
                p0 = (h % per_tile) * tw
                E.dma("gpsimd", nat[p0:p0 + tw, rt * cols:(rt + 1) * cols],
                      src_full[h * hstride + lo: h * hstride + lo + tw, :],
                      writes=["prepnat"])
        for ht in range(nct):
            bk = 6 + ht % 2
            for rt in range(nrt):
                E.op("tensor", lambda e, ht=ht, rt=rt, bk=bk: e.transpose(
                    ctx.psbf6(bk, rt), nat[:, rt * cols + ht * 128: rt * cols + (ht + 1) * 128],
                    ctx.consts["ident_bf"][:]),
                    reads=["prepnat", "c_ident_bf"], writes=[f"ps{bk}"])
            E.op("vector", lambda e, ht=ht, bk=bk, nrt=nrt: e.tensor_copy(
                wT[:, ht * rows: ht * rows + nrt * 128], ctx.ps[bk][:, 0:nrt * 64].bitcast(BF16)),
                reads=[f"ps{bk}"], writes=[name], tag="wTev")
    return wT


def emit_s2(ctx):
    E = ctx.E
    NH, HD, RD, L = 16, 64, 32, 256
    xn_d = ctx.dio("xn", (B_CORE, T, D), F32, producer="s0", consumers=("s2", "s3"))
    t1_d = ctx.dio("t1", (B_CORE, T, D), F32, producer="s2", consumers=("s3",))
    zT_d = ctx.dscratch("s2_zT", (B_CORE, D, T), BF16)
    avn_d = ctx.dscratch("s2_avn", (B_CORE, D, T), BF16)
    zs_d = ctx.dscratch("s2_zs", (B_CORE, 4, T), F32)

    ln1_w = ctx.din("w_L0_ffn_ln1_w", (D,), F32)
    kv_d = ctx.din("w_L0_ffn_kv_d", (L, D), F32)
    q_d = ctx.din("w_L0_ffn_q_d", (L, D), F32)
    k_u = ctx.din("w_L0_ffn_k_u", (D, L), F32)
    q_u = ctx.din("w_L0_ffn_q_u", (D, L), F32)
    v_u = ctx.din("w_L0_ffn_v_u", (D, L), F32)
    rope_q_w = ctx.din("w_L0_ffn_rope_q_w", (512, L), F32)
    rope_k_w = ctx.din("w_L0_ffn_rope_k_w", (512, D), F32)
    o_w = ctx.din("w_L0_ffn_o_w", (D, D), F32)

    with ctx.stage("s2") as S:
        if S is None:
            return
        prep_nat = ctx.sb(S, "prepnat", [128, 8 * 1024], BF16)
        kvdT = emit_load_wT(ctx, S, kv_d, L, D, "kvd", nat_buf=prep_nat)
        qdT = emit_load_wT(ctx, S, q_d, L, D, "qd", nat_buf=prep_nat)
        kuT = emit_load_wT(ctx, S, None, 512, L, "ku", head_sliced=(k_u, 16, 32, 32), nat_buf=prep_nat)
        quT = emit_load_wT(ctx, S, None, 512, L, "qu", head_sliced=(q_u, 16, 32, 32), nat_buf=prep_nat)
        vuT = emit_load_wT(ctx, S, v_u, D, L, "vu", nat_buf=prep_nat)
        rqT = emit_load_wT(ctx, S, rope_q_w, 512, L, "rq", nat_buf=prep_nat)
        rkT = emit_load_wT(ctx, S, rope_k_w, 512, D, "rk", nat_buf=prep_nat)
        owT = emit_load_wT(ctx, S, o_w, D, D, "ow", nat_buf=prep_nat)
        ln1w = ctx.sb(S, "ln1w", [128, 8], F32)
        E.dma("sync", ln1w[:], ln1_w.rearrange("(a p) -> p a", p=128), writes=["ln1w"],
              allow_slow_non_contiguous=True)

        xrow = ctx.sb(S, "s2xr", [128, D], F32)
        znorm = xrow
        sc = make_ln_scratch(ctx, S, "s2")
        stg = [ctx.sb(S, "s2stg", [128, 512], BF16) for _ in range(2)]
        zt = [ctx.sb(S, "s2zt", [128, T], BF16) for _ in range(2)]
        kvlat = [ctx.sb(S, "s2kvl", [128, T], BF16) for _ in range(2)]
        qlat = [ctx.sb(S, "s2ql", [128, T], BF16) for _ in range(2)]
        vsb = [ctx.sb(S, "s2v", [128, 16 * 65], BF16) for _ in range(16)]
        rot_t = [ctx.sb(S, "s2rot", [128, T], BF16) for _ in range(2)]
        keff = [ctx.sb(S, "s2ke", [128, T], BF16) for _ in range(2)]
        qeff = [ctx.sb(S, "s2qe", [128, T], BF16) for _ in range(2)]
        rtmp = ctx.sb(S, "s2rt", [128, T], BF16)
        tmpe = ctx.sb(S, "s2te", [128, T], BF16)
        psb = [ctx.sb(S, "s2p", [128, 1024], BF16) for _ in range(3)]
        avsb = [ctx.sb(S, "s2av", [128, T], BF16) for _ in range(2)]
        zstg = ctx.sb(S, "s2zstg", [1, 512], F32)
        zall = ctx.sb(S, "s2zall", [4, T], F32)
        zbc = ctx.sb(S, "s2zbc", [128, T // 2], F32)
        avst = zt
        sostg = tmpe[:, 0:1024].bitcast(F32)
        t1row = [keff[0][:].bitcast(F32), keff[1][:].bitcast(F32),
                 qeff[0][:].bitcast(F32), qeff[1][:].bitcast(F32)]
        t1alias = ["s2ke0", "s2ke1", "s2qe0", "s2qe1"]

        for b in range(B_CORE):
            if S2_PHASES < 1:
                break
            for tt in range(T // 128):
                p = tt % 2
                kx = f"s2x{p}"
                E.dma("sync", xrow[:], xn_d[b, tile_slice(tt), :], writes=[kx], tag="s2ldx")
                emit_ln_tile(ctx, sc, xrow[:], kx, znorm[:], "s2zn", "s2ln")
                for dt_i in range(8):
                    bk = 6 + dt_i // 4
                    E.op("tensor", lambda e, dt_i=dt_i, bk=bk: e.transpose(
                        ctx.ps[bk][:, (dt_i % 4) * 128:(dt_i % 4 + 1) * 128],
                        znorm[:, tile_slice(dt_i)], ctx.consts["ident_f32"][:]),
                        reads=["s2zn" if znorm is xrow else "s2zn", "s2xr", "c_ident_f32"],
                        writes=[f"ps{bk}"])
                for dt_i in range(8):
                    bk = 6 + dt_i // 4
                    sp = dt_i // 4
                    E.op("scalar", lambda e, dt_i=dt_i, bk=bk, sp=sp: e.activation(
                        stg[sp][:, (dt_i % 4) * 128:(dt_i % 4 + 1) * 128],
                        ctx.ps[bk][:, (dt_i % 4) * 128:(dt_i % 4 + 1) * 128], AF.Identity,
                        scale=ln1w[:, dt_i:dt_i + 1]),
                        reads=[f"ps{bk}", "ln1w"], writes=[f"s2stg{sp}"])
                for sp in range(2):
                    E.dma("sync",
                          zT_d[b, sp * 512:(sp + 1) * 512, tile_slice(tt)]
                          .rearrange("(a p) c -> p a c", p=128),
                          stg[sp][:].rearrange("p (a c) -> p a c", c=128),
                          reads=[f"s2stg{sp}"], writes=[f"zT_{sp}_{tt}"], tag="s2stz")

            zT_keys = [f"zT_{dt_i}_{tt}" for dt_i in range(8) for tt in range(16)]

            if S2_PHASES < 2:
                continue
            lat_n = 0
            for (latbuf, wTb, kw, kl) in ((kvlat, kvdT, "kvd", "kvl"), (qlat, qdT, "qd", "ql")):
                for ct in range(2):
                    for tc in range(4):
                        lat_n += 1
                        if lat_n > S2_LAT_LIMIT:
                            continue
                        for ht in range(8):
                            p = ht % 2
                            E.dma("sync", zt[p][:], zT_d[b, tile_slice(ht), :],
                                  reads=[f"zT_{ht // 4}_{tt}" for tt in range(16)],
                                  writes=[f"s2zt{p}"], tag="s2ldz")
                            E.op("tensor", lambda e, ct=ct, tc=tc, ht=ht, p=p, wTb=wTb: e.matmul(
                                ctx.ps[0][:, :], wTb[:, ht * L + ct * 128: ht * L + (ct + 1) * 128],
                                zt[p][:, tile_slice(tc, 512)], start=(ht == 0), stop=(ht == 7),
                                skip_group_check=True),
                                reads=[f"s2zt{p}", kw], writes=["ps0"], tag="s2lat")
                        E.op("scalar", lambda e, ct=ct, tc=tc, latbuf=latbuf: e.copy(
                            latbuf[ct][:, tile_slice(tc, 512)], ctx.ps[0][:, :]),
                            reads=["ps0"], writes=[kl + str(ct)], tag="s2latev")

            if S2_PHASES < 3:
                continue
            for tp in range(16):
                E.op("vector", lambda e, tp=tp: e.memset(vsb[tp][:], 1.0),
                     writes=[f"v{tp}"], tag="s2vm")
                for ch in range(2):
                    bk = 6 + ch
                    for ct in range(2):
                        E.op("tensor", lambda e, tp=tp, ch=ch, ct=ct, bk=bk: e.matmul(
                            ctx.ps[bk][:, :], kvlat[ct][:, tile_slice(tp)],
                            vuT[:, ct * D + ch * 512: ct * D + (ch + 1) * 512],
                            start=(ct == 0), stop=(ct == 1), skip_group_check=True),
                            reads=["kvl0", "kvl1", "vu"], writes=[f"ps{bk}"], tag="s2vmm")
                    for hh in range(8):
                        h = ch * 8 + hh
                        E.op("scalar", lambda e, tp=tp, h=h, hh=hh, bk=bk: e.copy(
                            vsb[tp][:, h * 65:h * 65 + 64], ctx.ps[bk][:, hh * 64:(hh + 1) * 64]),
                            reads=[f"ps{bk}"], writes=[f"v{tp}"], tag="s2vev")

            if S2_PHASES < 4:
                continue
            for rt in range(4):
                for (dst, which) in ((0, "k"), (1, "q")):
                    if which == "k":
                        for tc in range(4):
                            for ht in range(8):
                                p = ht % 2
                                E.dma("sync", zt[p][:, 0:512], zT_d[b, tile_slice(ht), tile_slice(tc, 512)],
                                      reads=[f"zT_{ht // 4}_{tt}" for tt in range(16)],
                                      writes=[f"s2zt{p}"], tag="s2ldz2")
                                E.op("tensor", lambda e, tc=tc, ht=ht, p=p, rt=rt: e.matmul(
                                    ctx.ps[7][:, :], rkT[:, ht * 512 + rt * 128: ht * 512 + (rt + 1) * 128],
                                    zt[p][:, 0:512], start=(ht == 0), stop=(ht == 7),
                                    skip_group_check=True),
                                    reads=[f"s2zt{p}", "rk"], writes=["ps7"], tag="s2krmm")
                            E.op("scalar", lambda e, tc=tc: e.copy(
                                tmpe[:, tile_slice(tc, 512)], ctx.ps[7][:, :]),
                                reads=["ps7"], writes=["s2te"], tag="s2krev")
                    else:
                        for tc in range(4):
                            for ct in range(2):
                                E.op("tensor", lambda e, tc=tc, ct=ct, rt=rt: e.matmul(
                                    ctx.ps[7][:, :], rqT[:, ct * 512 + rt * 128: ct * 512 + (rt + 1) * 128],
                                    qlat[ct][:, tile_slice(tc, 512)], start=(ct == 0), stop=(ct == 1),
                                    skip_group_check=True),
                                    reads=["ql0", "ql1", "rq"], writes=["ps7"], tag="s2qrmm")
                            E.op("scalar", lambda e, tc=tc: e.copy(
                                tmpe[:, tile_slice(tc, 512)], ctx.ps[7][:, :]),
                                reads=["ps7"], writes=["s2te"], tag="s2qrev")
                    emit_rope_pair(ctx, S, rot_t[dst], f"s2rot{dst}", tmpe, "s2te",
                                   "rot32_mt", "cos32", "sin32", rtmp, ctx.ps[6], "ps6", f"s2r{which}")
                    lat = kvlat if which == "k" else qlat
                    lwT = kuT if which == "k" else quT
                    lk = ("kvl0", "kvl1", "ku") if which == "k" else ("ql0", "ql1", "qu")
                    effarr = keff if which == "k" else qeff
                    keffpfx = "s2ke" if which == "k" else "s2qe"
                    for tc in range(4):
                        for ct in range(2):
                            E.op("tensor", lambda e, tc=tc, ct=ct, rt=rt, lat=lat, lwT=lwT: e.matmul(
                                ctx.ps[7][:, :], lwT[:, ct * 512 + rt * 128: ct * 512 + (rt + 1) * 128],
                                lat[ct][:, tile_slice(tc, 512)], start=(ct == 0), stop=(ct == 1),
                                skip_group_check=True),
                                reads=list(lk), writes=["ps7"], tag="s2upmm")
                        for hh in range(4):
                            E.op("scalar", lambda e, tc=tc, hh=hh, effarr=effarr: e.copy(
                                effarr[hh // 2][(hh % 2) * 64 + 32:(hh % 2) * 64 + 64, tile_slice(tc, 512)],
                                ctx.ps[7][hh * 32:(hh + 1) * 32, :]),
                                reads=["ps7"], writes=[keffpfx + str(hh // 2)], tag="s2upev")

                for pp in range(2):
                    for (effbuf, keffk, srcrot, krot) in (
                            (keff[pp], f"s2ke{pp}", rot_t[0], "s2rot0"),
                            (qeff[pp], f"s2qe{pp}", rot_t[1], "s2rot1")):
                        for hh2 in range(2):
                            hh = pp * 2 + hh2
                            E.op("vector", lambda e, effbuf=effbuf, srcrot=srcrot, hh=hh, hh2=hh2: e.tensor_copy(
                                effbuf[hh2 * 64:hh2 * 64 + 32, :], srcrot[hh * 32:(hh + 1) * 32, :]),
                                reads=[krot], writes=[keffk], tag="s2asm")

                for pp in range(2):
                    for hh2 in range(2):
                        h = rt * 4 + pp * 2 + hh2
                        r0 = hh2 * 64
                        for qq in range(4):
                            avbank = 4 + (qq % 2)
                            kk_max = qq * 4 + 3
                            npairs = (kk_max + 1) // 2
                            PBMAP = ((0, 1), (2, 3), (6, 7))
                            for kp in range(npairs):
                                b0, b1 = PBMAP[kp % 3]
                                for half in range(2):
                                    kk = kp * 2 + half
                                    bb = (b0, b1)[half]
                                    E.op("tensor", lambda e, kk=kk, qq=qq, r0=r0, bb=bb, pp=pp: e.matmul(
                                        ctx.ps[bb][:, :], keff[pp][r0:r0 + 64, tile_slice(kk)],
                                        qeff[pp][r0:r0 + 64, tile_slice(qq, 512)], start=True, stop=True,
                                        skip_group_check=True),
                                        reads=[f"s2ke{pp}", f"s2qe{pp}"], writes=[f"ps{bb}"], tag="s2smm")
                                E.op("scalar", lambda e, b0=b0, kp=kp: e.activation(
                                    psb[kp % 3][:], ctx.ps_all[:, b0 * 512:(b0 + 2) * 512], AF.Exp, scale=0.125),
                                    reads=[f"ps{b0}", f"ps{b0 + 1}"], writes=[f"p{kp % 3}"], tag="s2exp")
                                for half in range(2):
                                    kk = kp * 2 + half
                                    if kk >= qq * 4:
                                        delta = kk * 128 - qq * 512
                                        E.op("vector", lambda e, kp=kp, half=half, delta=delta: e.tensor_tensor(
                                            psb[kp % 3][:, half * 512:(half + 1) * 512],
                                            psb[kp % 3][:, half * 512:(half + 1) * 512],
                                            ctx.consts["stair"][:, 512 - delta:1024 - delta], ALU.mult),
                                            reads=[f"p{kp % 3}", "c_stair"], writes=[f"p{kp % 3}"], tag="s2msk")
                                for half in range(2):
                                    kk = kp * 2 + half
                                    E.op("tensor", lambda e, kk=kk, h=h, kp=kp, half=half, avbank=avbank, kk_max=kk_max: e.matmul(
                                        ctx.ps[avbank][0:65, :], vsb[kk][:, h * 65:(h + 1) * 65],
                                        psb[kp % 3][:, half * 512:(half + 1) * 512],
                                        start=(kk == 0), stop=(kk == kk_max),
                                        skip_group_check=True),
                                        reads=[f"p{kp % 3}", f"v{kk}"], writes=[f"ps{avbank}"], tag="s2avmm")
                            E.op("scalar", lambda e, qq=qq, r0=r0, pp=pp, avbank=avbank: e.copy(
                                avsb[pp][r0:r0 + 64, tile_slice(qq, 512)], ctx.ps[avbank][0:64, :]),
                                reads=[f"ps{avbank}"], writes=[f"s2avb{pp}"], tag="s2avev")
                            E.op("scalar", lambda e, avbank=avbank: e.copy(
                                zstg[:, :], ctx.ps[avbank][64:65, :]),
                                reads=[f"ps{avbank}"], writes=["zstg"], tag="s2zev")
                            hloc = pp * 2 + hh2
                            E.dma("sync", zs_d[b, hloc, tile_slice(qq, 512)], zstg[:, :],
                                  reads=["zstg"], writes=[f"zs{hloc}{qq}"], tag="s2zst")

                E.dma("sync", zall[:], zs_d[b],
                      reads=[f"zs{hl}{qq}" for hl in range(4) for qq in range(4)],
                      writes=["zall"], tag="s2zld")
                E.op("vector", lambda e: e.reciprocal(zall[:], zall[:]), reads=["zall"], writes=["zall"])
                E.dma("sync", zs_d[b], zall[:], reads=["zall"], writes=["zrb"], tag="s2zrst")
                for pp in range(2):
                    for hh2 in range(2):
                        hloc = pp * 2 + hh2
                        for th in range(2):
                            tsl = slice(th * (T // 2), (th + 1) * (T // 2))
                            E.dma("sync", zbc[hh2 * 64:(hh2 + 1) * 64, :],
                                  zs_d[b, hloc:hloc + 1, tsl].broadcast_to([64, T // 2]),
                                  reads=["zrb"], writes=["zbc"], tag="s2zbc")
                            E.op("vector", lambda e, pp=pp, hh2=hh2, tsl=tsl: e.tensor_tensor(
                                avsb[pp][hh2 * 64:(hh2 + 1) * 64, tsl],
                                avsb[pp][hh2 * 64:(hh2 + 1) * 64, tsl],
                                zbc[hh2 * 64:(hh2 + 1) * 64, :], ALU.mult),
                                reads=[f"s2avb{pp}", "zbc"], writes=[f"s2avb{pp}"], tag="s2nrm")
                    pair_global = rt * 2 + pp
                    E.dma("sync", avn_d[b, tile_slice(pair_global), :], avsb[pp][:],
                          reads=[f"s2avb{pp}"], writes=[f"avn{pair_global}"], tag="s2avst")

            if S2_PHASES < 5:
                continue
            for tc in range(4):
                for ot in range(8):
                    for ht in range(8):
                        p = ht % 2
                        E.dma("sync", avst[p][:], avn_d[b, tile_slice(ht), :],
                              reads=[f"avn{ht}"], writes=[f"s2avs{p}"], tag="s2ldav")
                        E.op("tensor", lambda e, tc=tc, ot=ot, ht=ht, p=p: e.matmul(
                            ctx.ps[7][:, :], owT[:, ht * D + ot * 128: ht * D + (ot + 1) * 128],
                            avst[p][:, tile_slice(tc, 512)], start=(ht == 0), stop=(ht == 7),
                            skip_group_check=True),
                            reads=[f"s2avs{p}", "ow"], writes=["ps7"], tag="s2omm")
                    E.op("scalar", lambda e: e.copy(sostg, ctx.ps[7][:, :]),
                         reads=["ps7"], writes=["s2te"], tag="s2oev")
                    for t2 in range(4):
                        tr = tc * 4 + t2
                        if ot == 0:
                            E.dma("sync", t1row[t2], xn_d[b, tile_slice(tr), :],
                                  writes=[f"t1r{t2}", t1alias[t2]], tag="s2ldxn")
                        E.op("tensor", lambda e, t2=t2: e.transpose(
                            ctx.ps[6][:, t2 * 128:(t2 + 1) * 128], sostg[:, tile_slice(t2)],
                            ctx.consts["ident_f32"][:]),
                            reads=["s2te", "c_ident_f32"], writes=["ps6"], tag="s2otr")
                    for t2 in range(4):
                        tr = tc * 4 + t2
                        osl = slice(ot * 128, (ot + 1) * 128)
                        E.op("vector", lambda e, t2=t2, osl=osl: e.tensor_tensor(
                            t1row[t2][:, osl], t1row[t2][:, osl],
                            ctx.ps[6][:, t2 * 128:(t2 + 1) * 128], ALU.add),
                            reads=["ps6", f"t1r{t2}"], writes=[f"t1r{t2}", t1alias[t2]], tag="s2res")
                        if ot == 7:
                            E.dma("sync", t1_d[b, tile_slice(tr), :], t1row[t2],
                                  reads=[f"t1r{t2}"], writes=[f"d_t1_{b}_{tr}"], tag="s2stt1")



def emit_s3(ctx):
    E = ctx.E
    I_ = 2048
    t1_d = ctx.dio("t1", (B_CORE, T, D), F32, producer="s2", consumers=("s3",))
    t2_d = ctx.dio("t2", (B_CORE, T, D), F32, producer="s3", consumers=("s4",))
    wgtT_d = ctx.dscratch("s3_wgtT", (B_CORE, 7, T), F32)

    ln2_w = ctx.din("w_L0_ffn_ln2_w", (D,), F32)
    router_w = ctx.din("w_L0_ffn_router_w", (7, D), F32)
    routing_bias = ctx.din("w_L0_ffn_routing_bias", (7,), F32)
    sh_g = ctx.din("w_L0_ffn_shared_gate", (1, I_, D), F32)
    sh_u = ctx.din("w_L0_ffn_shared_up", (1, I_, D), F32)
    sh_d = ctx.din("w_L0_ffn_shared_down", (1, D, I_), F32)
    ro_g = ctx.din("w_L0_ffn_routed_gate", (7, I_, D), F32)
    ro_u = ctx.din("w_L0_ffn_routed_up", (7, I_, D), F32)
    ro_d = ctx.din("w_L0_ffn_routed_down", (7, D, I_), F32)

    with ctx.stage("s3") as S:
        if S is None:
            return
        ln2w = ctx.sb(S, "ln2w", [128, 8], F32)
        E.dma("sync", ln2w[:], ln2_w.rearrange("(a p) -> p a", p=128), writes=["ln2w"],
              allow_slow_non_contiguous=True)
        rb = ctx.sb(S, "rb", [7, 1], F32)
        E.dma("sync", rb[:], routing_bias.rearrange("(a p) -> p a", p=7), writes=["rb"],
              allow_slow_non_contiguous=True)
        routerF = ctx.sb(S, "routerF", [128, 8 * 7], F32)
        for ht in range(8):
            E.dma("sync", routerF[:, ht * 7:(ht + 1) * 7],
                  router_w[:, tile_slice(ht)].rearrange("e p -> p e"), writes=["routerF"],
                  allow_slow_non_contiguous=True)
        rb_bc = ctx.sb(S, "rb_bc", [128, 7], F32)
        E.dma("sync", rb_bc[:], routing_bias.rearrange("(p a) -> p a", p=1).broadcast_to([128, 7]),
              writes=["rb_bc"])

        xrow = ctx.sb(S, "s3xr", [128, D], F32)
        sc = make_ln_scratch(ctx, "s3") if False else make_ln_scratch(ctx, S, "s3")
        xn2 = [ctx.sb(S, "s3x2", [128, T], BF16) for _ in range(8)]
        xstgT = ctx.sb(S, "s3xT", [128, 8 * 128], F32)
        prb = ctx.sb(S, "s3prb", [128, 7], F32)
        tp1 = ctx.sb(S, "s3tp1", [128, 7], F32)
        tp2 = ctx.sb(S, "s3tp2", [128, 7], F32)
        m1 = ctx.sb(S, "s3m1", [128, 1], F32)
        m2 = ctx.sb(S, "s3m2", [128, 1], F32)
        rden = ctx.sb(S, "s3rd", [128, 1], F32)
        wstg = ctx.sb(S, "s3wstg", [7, 128], F32)
        wgt_bc = ctx.sb(S, "s3wbc", [128, T], F32)
        wg_nat = ctx.sb(S, "s3wgn", [128, D], BF16)
        wgTi = ctx.sb(S, "s3wgT", [128, 2048], BF16)
        wuTi = ctx.sb(S, "s3wuT", [128, 2048], BF16)
        wdT = ctx.sb(S, "s3wdT", [128, 16 * 128], BF16)
        g_sb = ctx.sb(S, "s3g", [128, T], BF16)
        wu_nat = wg_nat
        wd_nat = g_sb
        hst = [ctx.sb(S, "s3h", [128, T], BF16) for _ in range(16)]
        acc = [ctx.sb(S, "s3acc", [128, T], F32) for _ in range(8)]
        t2row = [hst[i][:].bitcast(F32) for i in range(4)]
        t2alias = ["h0", "h1", "h2", "h3"]

        for b in range(B_CORE):
            for tt in range(T // 128):
                E.dma("sync", xrow[:], t1_d[b, tile_slice(tt), :], writes=["s3xr"], tag="s3ldx")
                emit_ln_tile(ctx, sc, xrow[:], "s3xr", xrow[:], "s3xr", "s3ln")
                for dt_i in range(8):
                    E.op("tensor", lambda e, dt_i=dt_i: e.transpose(
                        ctx.ps[6][:, 0:128], xrow[:, tile_slice(dt_i)], ctx.consts["ident_f32"][:]),
                        reads=["s3xr", "c_ident_f32"], writes=["ps6"])
                    E.op("scalar", lambda e, dt_i=dt_i, tt=tt: e.activation(
                        xn2[dt_i][:, tile_slice(tt)], ctx.ps[6][:, 0:128], AF.Identity,
                        scale=ln2w[:, dt_i:dt_i + 1]),
                        reads=["ps6", "ln2w"], writes=[f"x2_{dt_i}"])
                    E.op("scalar", lambda e, dt_i=dt_i: e.activation(
                        xstgT[:, tile_slice(dt_i)], ctx.ps[6][:, 0:128], AF.Identity,
                        scale=ln2w[:, dt_i:dt_i + 1]),
                        reads=["ps6", "ln2w"], writes=["xstgT"])
                for ht in range(8):
                    E.op("tensor", lambda e, ht=ht: e.matmul(
                        ctx.ps[0][:, 0:7], xstgT[:, tile_slice(ht)],
                        routerF[:, ht * 7:(ht + 1) * 7], start=(ht == 0), stop=(ht == 7),
                        skip_group_check=True),
                        reads=["xstgT", "routerF"], writes=["ps0"], tag="s3rmm")
                E.op("vector", lambda e: e.tensor_tensor(
                    prb[:], ctx.ps[0][:, 0:7], rb_bc[:], ALU.add),
                    reads=["ps0", "rb_bc"], writes=["prb"], tag="s3rb")
                E.op("scalar", lambda e: e.activation(
                    prb[:], prb[:], AF.Sigmoid),
                    reads=["prb"], writes=["prb"], tag="s3sig")

                E.op("vector", lambda e: e.tensor_reduce(m1[:].unsqueeze(2), prb[:].unsqueeze(1), AX.X, ALU.max),
                     reads=["prb"], writes=["m1"])
                E.op("vector", lambda e: e.tensor_scalar(tp1[:], prb[:], m1[:], None, ALU.is_ge),
                     reads=["prb", "m1"], writes=["tp1"])
                E.op("vector", lambda e: e.tensor_scalar(tp2[:], tp1[:], -1.0, 1.0, ALU.mult, ALU.add),
                     reads=["tp1"], writes=["tp2"])
                E.op("vector", lambda e: e.tensor_tensor(tp2[:], prb[:], tp2[:], ALU.mult),
                     reads=["prb", "tp2"], writes=["tp2"])
                E.op("vector", lambda e: e.tensor_reduce(m2[:].unsqueeze(2), tp2[:].unsqueeze(1), AX.X, ALU.max),
                     reads=["tp2"], writes=["m2"])
                E.op("vector", lambda e: e.tensor_scalar(tp2[:], tp2[:], m2[:], None, ALU.is_ge),
                     reads=["tp2", "m2"], writes=["tp2"])
                E.op("vector", lambda e: e.tensor_scalar(tp1[:], tp1[:], m1[:], None, ALU.mult),
                     reads=["tp1", "m1"], writes=["tp1"])
                E.op("vector", lambda e: e.scalar_tensor_tensor(
                    tp1[:], tp2[:], m2[:], tp1[:], ALU.mult, ALU.add),
                    reads=["tp1", "tp2", "m2"], writes=["tp1"])
                E.op("vector", lambda e: e.tensor_tensor(rden[:], m1[:], m2[:], ALU.add),
                     reads=["m1", "m2"], writes=["rden"])
                E.op("vector", lambda e: e.reciprocal(rden[:], rden[:]),
                     reads=["rden"], writes=["rden"])
                E.op("vector", lambda e: e.tensor_scalar(tp1[:], tp1[:], rden[:], None, ALU.mult),
                     reads=["tp1", "rden"], writes=["tp1"])
                E.op("tensor", lambda e: e.transpose(
                    ctx.ps[5][0:7, 0:128], tp1[:], ctx.consts["ident_f32"][:]),
                    reads=["tp1", "c_ident_f32"], writes=["ps5"], tag="s3wtr")
                E.op("scalar", lambda e: e.copy(wstg[:], ctx.ps[5][0:7, 0:128]),
                     reads=["ps5"], writes=["wstg"])
                E.dma("sync", wgtT_d[b, :, tile_slice(tt)], wstg[:],
                      reads=["wstg"], writes=[f"wgtT_{tt}"], tag="s3wst")

            for e_i in range(8):
                shared = (e_i == 7)
                if not shared:
                    E.dma("sync", wgt_bc[:],
                          wgtT_d[b, e_i:e_i + 1, :].broadcast_to([128, T]),
                          reads=[f"wgtT_{tt}" for tt in range(16)], writes=["wbc"], tag="s3wbc")
                gsrc = sh_g[0] if shared else ro_g[e_i]
                usrc = sh_u[0] if shared else ro_u[e_i]
                dsrc = sh_d[0] if shared else ro_d[e_i]
                for ii in range(16):
                    tb = 6 - 2 * (ii % 2)
                    wgT_b = wgTi if ii % 2 == 0 else wuTi
                    kwg = "wgTi" if ii % 2 == 0 else "wuTi"
                    E.dma("gpsimd", wg_nat[:], gsrc[tile_slice(ii), :], writes=["wgn"], tag="s3ldg")
                    for ht in range(8):
                        E.op("tensor", lambda e, ht=ht, tb=tb: e.transpose(
                            ctx.psbf6(tb, ht), wg_nat[:, tile_slice(ht)], ctx.consts["ident_bf"][:]),
                            reads=["wgn", "c_ident_bf"], writes=[f"ps{tb}"])
                    E.op("vector", lambda e, tb=tb, wgT_b=wgT_b: e.tensor_copy(
                        wgT_b[:, 0:1024], ctx.ps[tb].bitcast(BF16)),
                        reads=[f"ps{tb}"], writes=[kwg])
                    E.dma("gpsimd", wu_nat[:], usrc[tile_slice(ii), :], writes=["wgn"], tag="s3ldu")
                    for ht in range(8):
                        E.op("tensor", lambda e, ht=ht, tb=tb: e.transpose(
                            ctx.psbf6(tb + 1, ht), wu_nat[:, tile_slice(ht)], ctx.consts["ident_bf"][:]),
                            reads=["wgn", "c_ident_bf"], writes=[f"ps{tb + 1}"])
                    E.op("vector", lambda e, tb=tb, wgT_b=wgT_b: e.tensor_copy(
                        wgT_b[:, 1024:2048], ctx.ps[tb + 1].bitcast(BF16)),
                        reads=[f"ps{tb + 1}"], writes=[kwg])
                    for tc in range(4):
                        gb = tc % 2
                        ub = 2 + tc % 2
                        for ht in range(8):
                            E.op("tensor", lambda e, tc=tc, ht=ht, gb=gb, wgT_b=wgT_b: e.matmul(
                                ctx.ps[gb][:, :], wgT_b[:, tile_slice(ht)],
                                xn2[ht][:, tile_slice(tc, 512)], start=(ht == 0), stop=(ht == 7),
                                skip_group_check=True),
                                reads=[kwg, f"x2_{ht}"], writes=[f"ps{gb}"], tag="s3gmm")
                            E.op("tensor", lambda e, tc=tc, ht=ht, ub=ub, wgT_b=wgT_b: e.matmul(
                                ctx.ps[ub][:, :], wgT_b[:, 1024 + ht * 128:1024 + (ht + 1) * 128],
                                xn2[ht][:, tile_slice(tc, 512)], start=(ht == 0), stop=(ht == 7),
                                skip_group_check=True),
                                reads=[kwg, f"x2_{ht}"], writes=[f"ps{ub}"], tag="s3umm")
                        E.op("scalar", lambda e, tc=tc, gb=gb: e.activation(
                            g_sb[:, tile_slice(tc, 512)], ctx.ps[gb][:, :], AF.Silu),
                            reads=[f"ps{gb}"], writes=["g_sb"], tag="s3silu")
                        E.op("vector", lambda e, ii=ii, tc=tc, ub=ub: e.tensor_tensor(
                            hst[ii][:, tile_slice(tc, 512)], g_sb[:, tile_slice(tc, 512)],
                            ctx.ps[ub][:, :], ALU.mult),
                            reads=["g_sb", f"ps{ub}"], writes=[f"h{ii}"], tag="s3gu")
                    if not shared:
                        E.op("vector", lambda e, ii=ii: e.tensor_tensor(
                            hst[ii][:], hst[ii][:], wgt_bc[:], ALU.mult),
                            reads=[f"h{ii}", "wbc"], writes=[f"h{ii}"], tag="s3hw")
                for ot in range(8):
                    E.dma("gpsimd", wd_nat[:], dsrc[tile_slice(ot), :], writes=["g_sb"], tag="s3ldd")
                    for ii in range(16):
                        bk = 2 + ii // 8
                        E.op("tensor", lambda e, ii=ii, bk=bk: e.transpose(
                            ctx.psbf6(bk, ii % 8), wd_nat[:, tile_slice(ii)], ctx.consts["ident_bf"][:]),
                            reads=["g_sb", "c_ident_bf"], writes=[f"ps{bk}"])
                    for bk in (2, 3):
                        E.op("vector", lambda e, bk=bk: e.tensor_copy(
                            wdT[:, (bk - 2) * 1024:(bk - 1) * 1024], ctx.ps[bk].bitcast(BF16)),
                            reads=[f"ps{bk}"], writes=["wdT"])
                    for tc in range(4):
                        bank = tc % 2
                        for ii in range(16):
                            E.op("tensor", lambda e, tc=tc, ii=ii, bank=bank: e.matmul(
                                ctx.ps[bank][:, :], wdT[:, tile_slice(ii)],
                                hst[ii][:, tile_slice(tc, 512)], start=(ii == 0), stop=(ii == 15),
                                skip_group_check=True),
                                reads=["wdT", f"h{ii}"], writes=[f"ps{bank}"], tag="s3dmm")
                        if e_i == 0:
                            E.op("scalar", lambda e, ot=ot, tc=tc, bank=bank: e.copy(
                                acc[ot][:, tile_slice(tc, 512)], ctx.ps[bank][:, :]),
                                reads=[f"ps{bank}"], writes=[f"acc{ot}"], tag="s3acc0")
                        else:
                            E.op("vector", lambda e, ot=ot, tc=tc, bank=bank: e.tensor_tensor(
                                acc[ot][:, tile_slice(tc, 512)], acc[ot][:, tile_slice(tc, 512)],
                                ctx.ps[bank][:, :], ALU.add),
                                reads=[f"ps{bank}", f"acc{ot}"], writes=[f"acc{ot}"], tag="s3acc")

            for tc in range(4):
                for ot in range(8):
                    bk = 4 + ot % 2
                    for t2_ in range(4):
                        tr = tc * 4 + t2_
                        if ot == 0:
                            E.dma("sync", t2row[t2_], t1_d[b, tile_slice(tr), :],
                                  writes=[f"t2r{t2_}", t2alias[t2_]], tag="s3ldt1")
                        E.op("tensor", lambda e, ot=ot, tc=tc, t2_=t2_, bk=bk: e.transpose(
                            ctx.ps[bk][:, t2_ * 128:(t2_ + 1) * 128],
                            acc[ot][:, tc * 512 + t2_ * 128: tc * 512 + (t2_ + 1) * 128],
                            ctx.consts["ident_f32"][:]),
                            reads=[f"acc{ot}", "c_ident_f32"], writes=[f"ps{bk}"], tag="s3ttr")
                    for t2_ in range(4):
                        tr = tc * 4 + t2_
                        E.op("vector", lambda e, ot=ot, t2_=t2_, bk=bk: e.tensor_tensor(
                            t2row[t2_][:, tile_slice(ot)], t2row[t2_][:, tile_slice(ot)],
                            ctx.ps[bk][:, t2_ * 128:(t2_ + 1) * 128], ALU.add),
                            reads=[f"ps{bk}", f"t2r{t2_}"], writes=[f"t2r{t2_}", t2alias[t2_]], tag="s3res")
                        if ot == 7:
                            E.dma("sync", t2_d[b, tile_slice(tr), :], t2row[t2_],
                                  reads=[f"t2r{t2_}"], writes=[f"d_t2_{b}_{tr}"], tag="s3stt2")



def emit_s4(ctx):
    E = ctx.E
    x_d = ctx.din("x", (B_CORE, T, D), F32)
    s0_d = ctx.dio("s0", (B_CORE, T, D), F32, producer="s1", consumers=("s4",))
    t2_d = ctx.dio("t2", (B_CORE, T, D), F32, producer="s3", consumers=("s4",))
    x1_d = ctx.dio("x1", (B_CORE, T, D), F32, producer="s4", consumers=("s7",))
    xn1_d = ctx.dio("xn1", (B_CORE, T, D), F32, producer="s4", consumers=("s6",))
    xn1T_d = ctx.dio("xn1T", (B_CORE, D, T), BF16, producer="s4", consumers=("s5",))

    with ctx.stage("s4") as S:
        if S is None:
            return
        xt = ctx.sb(S, "s4x", [128, D], F32)
        st_ = ctx.sb(S, "s4s", [128, D], F32)
        gt = ctx.sb(S, "s4g", [128, D], F32)
        stg = [ctx.sb(S, "s4stg", [128, 512], BF16) for _ in range(2)]
        sc = make_ln_scratch(ctx, S, "s4")

        for b in range(B_CORE):
            for tt in range(T // 128):
                E.dma("sync", xt[:], x_d[b, tile_slice(tt), :], writes=["s4x"], tag="s4ldx")
                E.dma("sync", st_[:], s0_d[b, tile_slice(tt), :], writes=["s4s"], tag="s4lds")
                E.dma("sync", gt[:], t2_d[b, tile_slice(tt), :], writes=["s4g"], tag="s4ldt")
                E.op("scalar", lambda e: e.activation(gt[:], gt[:], AF.Silu),
                     reads=["s4g"], writes=["s4g"], tag="s4silu")
                E.op("vector", lambda e: e.tensor_tensor(gt[:], gt[:], st_[:], ALU.mult),
                     reads=["s4g", "s4s"], writes=["s4g"], tag="s4sg")
                E.op("vector", lambda e: e.tensor_tensor(xt[:], xt[:], gt[:], ALU.add),
                     reads=["s4x", "s4g"], writes=["s4x"], tag="s4add")
                E.dma("sync", x1_d[b, tile_slice(tt), :], xt[:],
                      reads=["s4x"], writes=[f"d_x1_{b}_{tt}"], tag="s4stx1")
                emit_ln_tile(ctx, sc, xt[:], "s4x", gt[:], "s4n", "s4ln")
                E.dma("sync", xn1_d[b, tile_slice(tt), :], gt[:],
                      reads=["s4n"], writes=[f"d_xn1_{b}_{tt}"], tag="s4stn")
                for dt_i in range(8):
                    bk = 6 + dt_i // 4
                    E.op("tensor", lambda e, dt_i=dt_i, bk=bk: e.transpose(
                        ctx.ps[bk][:, (dt_i % 4) * 128:(dt_i % 4 + 1) * 128],
                        gt[:, tile_slice(dt_i)], ctx.consts["ident_f32"][:]),
                        reads=["s4n", "c_ident_f32"], writes=[f"ps{bk}"])
                for gg in range(2):
                    bk = 6 + gg
                    E.op("scalar", lambda e, gg=gg, bk=bk: e.copy(stg[gg][:], ctx.ps[bk][:, :]),
                         reads=[f"ps{bk}"], writes=[f"s4stg{gg}"])
                    E.dma("sync",
                          xn1T_d[b, gg * 512:(gg + 1) * 512, tile_slice(tt)]
                          .rearrange("(a p) c -> p a c", p=128),
                          stg[gg][:].rearrange("p (a c) -> p a c", c=128),
                          reads=[f"s4stg{gg}"], writes=[f"d_xn1T_{b}_{gg}_{tt}"], tag="s4stT")



def emit_s5(ctx):
    E = ctx.E
    DI = 2048
    xn1T_d = ctx.dio("xn1T", (B_CORE, D, T), BF16, producer="s4", consumers=("s5",))
    s1_d = ctx.dio("s1", (B_CORE, T, D), F32, producer="s5", consumers=("s7",))
    xbc_d = ctx.dscratch("s5_xbc", (B_CORE, DI, T), BF16)
    z_d = ctx.dscratch("s5_z", (B_CORE, DI, T), BF16)
    dt_d = ctx.dscratch("s5_dt", (B_CORE, DI, T), F32)
    dtu_d = ctx.dscratch("s5_dtu", (B_CORE, DI, T), BF16)
    bc_d = ctx.dscratch("s5_bc", (B_CORE, 32, T), BF16)
    yg_d = ctx.dscratch("s5_yg", (B_CORE, DI, T), BF16)

    w_ip = ctx.din("w_L1_attn_in_proj_w", (D, 2 * DI), F32)
    w_cv = ctx.din("w_L1_attn_conv_w", (DI, 4), F32)
    b_cv = ctx.din("w_L1_attn_conv_b", (DI,), F32)
    w_xp = ctx.din("w_L1_attn_x_proj_w", (DI, 96), F32)
    w_dt = ctx.din("w_L1_attn_dt_proj_w", (64, DI), F32)
    b_dt = ctx.din("w_L1_attn_dt_proj_b", (DI,), F32)
    a_log = ctx.din("w_L1_attn_A_log", (DI, 16), F32)
    d_par = ctx.din("w_L1_attn_D_param", (DI,), F32)
    w_op = ctx.din("w_L1_attn_out_proj_w", (DI, D), F32)

    with ctx.stage("s5") as S:
        if S is None:
            return
        cvw = ctx.sb(S, "cvw", [128, 16 * 4], F32)
        E.dma("sync", cvw[:].rearrange("p (d c) -> p d c", d=16),
              w_cv.rearrange("(d p) c -> p d c", p=128), writes=["cvw"])
        cvb = ctx.sb(S, "cvb", [128, 16], F32)
        E.dma("sync", cvb[:], b_cv.rearrange("(d p) -> p d", p=128), writes=["cvb"],
              allow_slow_non_contiguous=True)
        dtb = ctx.sb(S, "dtb", [128, 16], F32)
        E.dma("sync", dtb[:], b_dt.rearrange("(d p) -> p d", p=128), writes=["dtb"],
              allow_slow_non_contiguous=True)
        dpar = ctx.sb(S, "dpar", [128, 16], F32)
        E.dma("sync", dpar[:], d_par.rearrange("(d p) -> p d", p=128), writes=["dpar"],
              allow_slow_non_contiguous=True)
        asb = ctx.sb(S, "asb", [128, 256], F32)
        E.dma("sync", asb[:].rearrange("p (d n) -> p d n", d=16),
              a_log.rearrange("(d p) n -> p d n", p=128), writes=["asb"])
        E.op("scalar", lambda e: e.activation(asb[:], asb[:], AF.Exp), reads=["asb"], writes=["asb"])
        E.op("vector", lambda e: e.tensor_scalar(asb[:], asb[:], -1.0, None, ALU.mult),
             reads=["asb"], writes=["asb"])
        xpw = ctx.sb(S, "xpw", [128, 16 * 96], BF16)
        E.dma("gpsimd", xpw[:].rearrange("p (d c) -> p d c", d=16),
              w_xp.rearrange("(d p) c -> p d c", p=128), writes=["xpw"])
        dtw = ctx.sb(S, "dtw", [64, DI], BF16)
        E.dma("gpsimd", dtw[:], w_dt[:, :], writes=["dtw"])

        xnt8 = [ctx.sb(S, "s5xnt", [128, T], BF16) for _ in range(8)]
        wip = [ctx.sb(S, "s5wip", [128, 8 * 128], BF16) for _ in range(2)]
        xbrow = ctx.sb(S, "s5xbr", [128, T + 4], BF16)
        cacc = ctx.sb(S, "s5cacc", [128, T], F32)
        xbc = ctx.sb(S, "s5xbc", [128, T], BF16)
        dtraw = ctx.sb(S, "s5dtr", [64, T], BF16)
        bcsb = ctx.sb(S, "s5bc", [32, T], BF16)
        etmp = ctx.sb(S, "s5et", [128, 512], F32)
        dts = ctx.sb(S, "s5dts", [128, 512], F32)
        dtu = ctx.sb(S, "s5dtu", [128, 512], BF16)
        xbcc = ctx.sb(S, "s5xbcc", [128, 512], BF16)
        SEG = 1024
        dtt = [ctx.sb(S, "s5dtt", [128, SEG], F32) for _ in range(2)]
        dut = [ctx.sb(S, "s5dut", [128, SEG], BF16) for _ in range(2)]
        bbc = [ctx.sb(S, "s5bbc", [128, SEG], BF16) for _ in range(4)]
        cbc = [ctx.sb(S, "s5cbc", [128, SEG], BF16) for _ in range(4)]
        da = [ctx.sb(S, "s5da", [128, SEG], F32) for _ in range(4)]
        dbu = [ctx.sb(S, "s5dbu", [128, SEG], BF16) for _ in range(4)]
        hbuf = [ctx.sb(S, "s5h", [128, SEG], BF16) for _ in range(4)]
        prods = [ctx.sb(S, "s5pr", [128, SEG], BF16) for _ in range(4)]
        gld = [ctx.sb(S, "s5gl", [128, SEG], BF16) for _ in range(2)]
        state = ctx.sb(S, "s5st", [128, 256], F32)
        zt_s = ctx.sb(S, "s5zt", [128, SEG], F32)
        ygt = ctx.sb(S, "s5yg", [128, SEG], BF16)
        ygl = [ctx.sb(S, "s5ygl", [128, 512], BF16) for _ in range(16)]
        sost = ctx.sb(S, "s5so", [128, 512], F32)
        trst = [ctx.sb(S, "s5tr", [128, 512], F32) for _ in range(2)]

        for b in range(B_CORE):
            for ht in range(8):
                E.dma("sync", xnt8[ht][:], xn1T_d[b, tile_slice(ht), :],
                      writes=[f"s5xnt{ht}"], tag="s5ldx")

            for dt_i in range(16):
                E.dma("gpsimd", wip[dt_i % 2][:].rearrange("p (h c) -> p h c", h=8),
                      w_ip[:, tile_slice(dt_i)].rearrange("(h p) c -> p h c", p=128),
                      writes=[f"wip{dt_i % 2}"], tag="s5ldw")
                for tc in range(4):
                    bank = tc % 2
                    for ht in range(8):
                        E.op("tensor", lambda e, dt_i=dt_i, tc=tc, ht=ht, bank=bank: e.matmul(
                            ctx.ps[bank][:, :], wip[dt_i % 2][:, tile_slice(ht)],
                            xnt8[ht][:, tile_slice(tc, 512)], start=(ht == 0), stop=(ht == 7),
                            skip_group_check=True),
                            reads=[f"s5xnt{ht}", f"wip{dt_i % 2}"], writes=[f"ps{bank}"], tag="s5ipmm")
                    E.op("scalar", lambda e, tc=tc, bank=bank: e.copy(
                        xbrow[:, 4 + tc * 512: 4 + (tc + 1) * 512], ctx.ps[bank][:, :]),
                        reads=[f"ps{bank}"], writes=["xbrow"], tag="s5ipev")
                E.op("vector", lambda e: e.memset(xbrow[:, 0:4], 0.0), writes=["xbrow"])
                E.op("vector", lambda e, dt_i=dt_i: e.tensor_scalar(
                    cacc[:], xbrow[:, 1:1 + T], cvw[:, dt_i * 4:dt_i * 4 + 1], None, ALU.mult),
                    reads=["xbrow", "cvw"], writes=["cacc"], tag="s5cv0")
                for j in range(1, 4):
                    E.op("vector", lambda e, j=j, dt_i=dt_i: e.scalar_tensor_tensor(
                        cacc[:], xbrow[:, 1 + j:1 + j + T], cvw[:, dt_i * 4 + j:dt_i * 4 + j + 1],
                        cacc[:], ALU.mult, ALU.add),
                        reads=["xbrow", "cvw", "cacc"], writes=["cacc"], tag="s5cv")
                E.op("scalar", lambda e, dt_i=dt_i: e.activation(
                    xbc[:], cacc[:], AF.Silu, bias=cvb[:, dt_i:dt_i + 1]),
                    reads=["cacc", "cvb"], writes=["xbc"], tag="s5cvs")
                E.dma("sync", xbc_d[b, tile_slice(dt_i), :], xbc[:],
                      reads=["xbc"], writes=[f"xbc_{dt_i}"], tag="s5stxbc")
                for tc in range(4):
                    E.op("tensor", lambda e, dt_i=dt_i, tc=tc: e.matmul(
                        ctx.ps[2 + tc][0:96, :], xpw[:, dt_i * 96:(dt_i + 1) * 96],
                        xbc[:, tile_slice(tc, 512)], start=(dt_i == 0), stop=(dt_i == 15),
                        skip_group_check=True),
                        reads=["xbc", "xpw"], writes=[f"ps{2 + tc}"], tag="s5xpmm")
            for dt_i in range(16):
                E.dma("gpsimd", wip[dt_i % 2][:].rearrange("p (h c) -> p h c", h=8),
                      w_ip[:, 2048 + dt_i * 128: 2048 + (dt_i + 1) * 128].rearrange("(h p) c -> p h c", p=128),
                      writes=[f"wip{dt_i % 2}"], tag="s5ldwz")
                for tc in range(4):
                    bank = tc % 2
                    for ht in range(8):
                        E.op("tensor", lambda e, tc=tc, ht=ht, bank=bank, dt_i=dt_i: e.matmul(
                            ctx.ps[bank][:, :], wip[dt_i % 2][:, tile_slice(ht)],
                            xnt8[ht][:, tile_slice(tc, 512)], start=(ht == 0), stop=(ht == 7),
                            skip_group_check=True),
                            reads=[f"s5xnt{ht}", f"wip{dt_i % 2}"], writes=[f"ps{bank}"], tag="s5zmm")
                    E.op("scalar", lambda e, tc=tc, bank=bank: e.copy(
                        xbc[:, tile_slice(tc, 512)], ctx.ps[bank][:, :]),
                        reads=[f"ps{bank}"], writes=["xbc"], tag="s5zev")
                E.dma("sync", z_d[b, tile_slice(dt_i), :], xbc[:],
                      reads=["xbc"], writes=[f"z_{dt_i}"], tag="s5stz")
            for tc in range(4):
                E.op("scalar", lambda e, tc=tc: e.copy(
                    dtraw[:, tile_slice(tc, 512)], ctx.ps[2 + tc][0:64, :]),
                    reads=[f"ps{2 + tc}"], writes=["dtraw"], tag="s5xpev")
                E.op("scalar", lambda e, tc=tc: e.copy(
                    bcsb[:, tile_slice(tc, 512)], ctx.ps[2 + tc][64:96, :]),
                    reads=[f"ps{2 + tc}"], writes=["bcsb"], tag="s5bcev")
            E.dma("sync", bc_d[b], bcsb[:], reads=["bcsb"], writes=["bc_dram"], tag="s5stbc")

            for dt_i in range(16):
                for tc in range(4):
                    E.op("tensor", lambda e, dt_i=dt_i, tc=tc: e.matmul(
                        ctx.ps[0][:, :], dtw[:, tile_slice(dt_i)],
                        dtraw[:, tile_slice(tc, 512)], start=True, stop=True,
                        skip_group_check=True),
                        reads=["dtraw", "dtw"], writes=["ps0"], tag="s5dtmm")
                    E.op("scalar", lambda e, dt_i=dt_i: e.activation(
                        etmp[:], ctx.ps[0][:, :], AF.Exp, bias=dtb[:, dt_i:dt_i + 1]),
                        reads=["ps0", "dtb"], writes=["etmp"], tag="s5sp1")
                    one_t, kone = ctx.sconst(1.0)
                    E.op("scalar", lambda e: e.activation(
                        dts[:], etmp[:], AF.Ln, bias=one_t[:, :]),
                        reads=["etmp", kone], writes=["dts"], tag="s5sp2")
                    E.dma("sync", dt_d[b, tile_slice(dt_i), tile_slice(tc, 512)], dts[:],
                          reads=["dts"], writes=[f"dt_{dt_i}"], tag="s5stdt")
                    E.dma("sync", xbcc[:], xbc_d[b, tile_slice(dt_i), tile_slice(tc, 512)],
                          reads=[f"xbc_{dt_i}"], writes=["xbcc"], tag="s5ldxbc")
                    E.op("vector", lambda e: e.tensor_tensor(dtu[:], dts[:], xbcc[:], ALU.mult),
                         reads=["dts", "xbcc"], writes=["dtu"], tag="s5dtu")
                    E.dma("sync", dtu_d[b, tile_slice(dt_i), tile_slice(tc, 512)], dtu[:],
                          reads=["dtu"], writes=[f"dtu_{dt_i}"], tag="s5stdtu")

            E.op("vector", lambda e: e.memset(state[:], 0.0),
                 writes=["state"] + [f"state{i}" for i in range(256)])
            for seg in range(2):
                sgsl = slice(seg * SEG, (seg + 1) * SEG)
                for dtg in range(4):
                    for nblk in range(4):
                        for nl in range(4):
                            n = nblk * 4 + nl
                            E.dma("sync", bbc[nl][:], bc_d[b, n:n + 1, sgsl].broadcast_to([128, SEG]),
                                  reads=["bc_dram"], writes=[f"bbc{nl}"], tag="s5bbc")
                            E.dma("sync", cbc[nl][:], bc_d[b, 16 + n:17 + n, sgsl].broadcast_to([128, SEG]),
                                  reads=["bc_dram"], writes=[f"cbc{nl}"], tag="s5cbc")
                        for dtl in range(4):
                            dt_i = dtg * 4 + dtl
                            pb_ = dt_i % 2
                            E.dma("sync", dtt[pb_][:], dt_d[b, tile_slice(dt_i), sgsl],
                                  reads=[f"dt_{dt_i}"], writes=[f"dtt{pb_}"], tag="s5lddt")
                            E.dma("sync", dut[pb_][:], dtu_d[b, tile_slice(dt_i), sgsl],
                                  reads=[f"dtu_{dt_i}"], writes=[f"dut{pb_}"], tag="s5lddtu")
                            for nl in range(4):
                                n = nblk * 4 + nl
                                pp = n % 4
                                scol = dt_i * 16 + n
                                E.op("scalar", lambda e, pb_=pb_, pp=pp, scol=scol: e.activation(
                                    da[pp][:], dtt[pb_][:], AF.Exp, scale=asb[:, scol:scol + 1]),
                                    reads=[f"dtt{pb_}", "asb"], writes=[f"da{pp}"], tag="s5da")
                                E.op("vector", lambda e, pb_=pb_, pp=pp, nl=nl: e.tensor_tensor(
                                    dbu[pp][:], dut[pb_][:], bbc[nl][:], ALU.mult),
                                    reads=[f"dut{pb_}", f"bbc{nl}"], writes=[f"dbu{pp}"], tag="s5dbu")
                                E.op("vector", lambda e, pp=pp, scol=scol: e.tensor_tensor_scan(
                                    hbuf[pp][:], da[pp][:], dbu[pp][:], state[:, scol:scol + 1],
                                    ALU.mult, ALU.add),
                                    reads=[f"da{pp}", f"dbu{pp}", f"state{scol}"], writes=[f"h{pp}"], tag="s5tts")
                                E.op("vector", lambda e, pp=pp, scol=scol: e.tensor_copy(
                                    state[:, scol:scol + 1], hbuf[pp][:, SEG - 1:SEG]),
                                    reads=[f"h{pp}"], writes=[f"state{scol}"], tag="s5stsv")
                                E.op("vector", lambda e, pp=pp, nl=nl: e.tensor_tensor(
                                    prods[pp][:], hbuf[pp][:], cbc[nl][:], ALU.mult),
                                    reads=[f"h{pp}", f"cbc{nl}"], writes=[f"pr{pp}"], tag="s5prod")
                                first = (nblk == 0 and nl == 0)
                                last = (nblk == 3 and nl == 3)
                                for tc2 in range(2):
                                    bank = dtl * 2 + tc2
                                    E.op("tensor", lambda e, pp=pp, tc2=tc2, bank=bank, first=first, last=last: e.matmul(
                                        ctx.ps[bank][:, :], ctx.consts["ident_bf"][:],
                                        prods[pp][:, tile_slice(tc2, 512)], start=first, stop=last,
                                        skip_group_check=True),
                                        reads=[f"pr{pp}", "c_ident_bf"], writes=[f"ps{bank}"], tag="s5yacc")
                        if nblk == 3:
                            for dtl in range(4):
                                dt_i = dtg * 4 + dtl
                                E.dma("sync", gld[0][:], xbc_d[b, tile_slice(dt_i), sgsl],
                                      reads=[f"xbc_{dt_i}"], writes=["gld0"], tag="s5gxbc")
                                E.dma("sync", gld[1][:], z_d[b, tile_slice(dt_i), sgsl],
                                      reads=[f"z_{dt_i}"], writes=["gld1"], tag="s5gz")
                                E.op("scalar", lambda e: e.activation(zt_s[:], gld[1][:], AF.Silu),
                                     reads=["gld1"], writes=["zt_s"], tag="s5gsilu")
                                E.op("vector", lambda e, dtl=dtl, dt_i=dt_i: e.scalar_tensor_tensor(
                                    zt_s[:] if False else gld[0][:], gld[0][:], dpar[:, dt_i:dt_i + 1],
                                    ctx.ps_all[:, dtl * 1024:(dtl + 1) * 1024], ALU.mult, ALU.add),
                                    reads=["gld0", "dpar", f"ps{dtl * 2}", f"ps{dtl * 2 + 1}"],
                                    writes=["gld0"], tag="s5gyd")
                                E.op("vector", lambda e: e.tensor_tensor(ygt[:], gld[0][:], zt_s[:], ALU.mult),
                                     reads=["gld0", "zt_s"], writes=["ygt"], tag="s5gmul")
                                E.dma("sync", yg_d[b, tile_slice(dt_i), sgsl], ygt[:],
                                      reads=["ygt"], writes=[f"yg_{dt_i}"], tag="s5styg")

            for tc in range(4):
                for dt_i in range(16):
                    E.dma("sync", ygl[dt_i][:], yg_d[b, tile_slice(dt_i), tile_slice(tc, 512)],
                          reads=[f"yg_{dt_i}"], writes=[f"ygl{dt_i}"], tag="s5ldyg")
                for ot in range(8):
                    wop = dtt[ot % 2][:].bitcast(BF16)
                    E.dma("gpsimd", wop.rearrange("p (d c) -> p d c", d=16),
                          w_op[:, tile_slice(ot)].rearrange("(d p) c -> p d c", p=128),
                          writes=[f"dtt{ot % 2}"], tag="s5ldop")
                    for dt_i in range(16):
                        E.op("tensor", lambda e, ot=ot, dt_i=dt_i, tc=tc, wop=wop: e.matmul(
                            ctx.ps[0][:, :], wop[:, tile_slice(dt_i)],
                            ygl[dt_i][:], start=(dt_i == 0), stop=(dt_i == 15),
                            skip_group_check=True),
                            reads=[f"ygl{dt_i}", f"dtt{ot % 2}"], writes=["ps0"], tag="s5opmm")
                    E.op("scalar", lambda e: e.copy(sost[:], ctx.ps[0][:, :]),
                         reads=["ps0"], writes=["sost"], tag="s5opev")
                    for t2 in range(4):
                        E.op("tensor", lambda e, t2=t2: e.transpose(
                            ctx.ps[1][:, t2 * 128:(t2 + 1) * 128], sost[:, tile_slice(t2)],
                            ctx.consts["ident_f32"][:]),
                            reads=["sost", "c_ident_f32"], writes=["ps1"], tag="s5str")
                    pr_ = ot % 2
                    E.op("scalar", lambda e, pr_=pr_: e.copy(trst[pr_][:], ctx.ps[1][:, :]),
                         reads=["ps1"], writes=[f"trst{pr_}"], tag="s5strev")
                    E.dma("sync", s1_d[b, tc * 512:(tc + 1) * 512, tile_slice(ot)]
                          .rearrange("(a p) c -> p a c", p=128),
                          trst[pr_][:].rearrange("p (a c) -> p a c", c=128),
                          reads=[f"trst{pr_}"], writes=[f"d_s1_{b}_{ot}_{tc}"], tag="s5sts1")



FP4_MIDS = (0.25, 0.75, 1.25, 1.75, 2.5, 3.5, 5.0)
FP4_DELT = (0.5, 0.5, 0.5, 0.5, 1.0, 1.0, 2.0)


def emit_fp4_quant(ctx, out_ap, kout, in_ap, kin, width, qs, tag):
    E = ctx.E
    nb = width // 16
    mx, scl, ybuf, acc, term, sgn = qs["mx"], qs["scl"], qs["y"], qs["acc"], qs["term"], qs["sgn"]
    kq = qs["key"]
    E.op("vector", lambda e: e.tensor_reduce(
        mx[:, 0:nb].unsqueeze(2), in_ap.rearrange("p (a b) -> p a b", b=16), AX.X, ALU.max,
        apply_absolute_value=True),
        reads=[kin], writes=[kq + "mx"], tag=tag + "mx")
    E.op("vector", lambda e: e.tensor_scalar(
        scl[:, 0:nb], mx[:, 0:nb], 1.0 / 6.0, 1e-12, ALU.mult, ALU.max),
        reads=[kq + "mx"], writes=[kq + "scl"], tag=tag + "scl")
    E.op("vector", lambda e: e.reciprocal(mx[:, 0:nb], scl[:, 0:nb]),
         reads=[kq + "scl"], writes=[kq + "mx"], tag=tag + "rcp")
    rcp_view = mx[:, 0:nb].unsqueeze(2).broadcast_to([128, nb, 16])
    E.op("scalar", lambda e: e.activation(
        ybuf[:, 0:width], in_ap, AF.Abs),
        reads=[kin], writes=[kq + "y"], tag=tag + "abs")
    E.op("vector", lambda e: e.tensor_tensor(
        ybuf[:, 0:width].rearrange("p (a b) -> p a b", b=16),
        ybuf[:, 0:width].rearrange("p (a b) -> p a b", b=16), rcp_view, ALU.mult),
        reads=[kq + "y", kq + "mx"], writes=[kq + "y"], tag=tag + "y")
    E.op("vector", lambda e: e.tensor_scalar(
        acc[:, 0:width], ybuf[:, 0:width], FP4_MIDS[0], FP4_DELT[0], ALU.is_gt, ALU.mult),
        reads=[kq + "y"], writes=[kq + "acc"], tag=tag + "q0")
    for j in range(1, 7):
        E.op("vector", lambda e, j=j: e.tensor_scalar(
            term[:, 0:width], ybuf[:, 0:width], FP4_MIDS[j], FP4_DELT[j], ALU.is_gt, ALU.mult),
            reads=[kq + "y"], writes=[kq + "t"], tag=tag + f"q{j}")
        E.op("vector", lambda e: e.tensor_tensor(
            acc[:, 0:width], acc[:, 0:width], term[:, 0:width], ALU.add),
            reads=[kq + "acc", kq + "t"], writes=[kq + "acc"], tag=tag + f"a{j}")
    E.op("scalar", lambda e: e.activation(sgn[:, 0:width], in_ap, AF.Sign),
         reads=[kin], writes=[kq + "sgn"], tag=tag + "sgn")
    scl_view = scl[:, 0:nb].unsqueeze(2).broadcast_to([128, nb, 16])
    E.op("vector", lambda e: e.tensor_tensor(
        sgn[:, 0:width].rearrange("p (a b) -> p a b", b=16),
        sgn[:, 0:width].rearrange("p (a b) -> p a b", b=16), scl_view, ALU.mult),
        reads=[kq + "sgn", kq + "scl"], writes=[kq + "sgn"], tag=tag + "ss")
    E.op("vector", lambda e: e.tensor_tensor(
        out_ap, acc[:, 0:width], sgn[:, 0:width], ALU.mult),
        reads=[kq + "acc", kq + "sgn"], writes=[kout], tag=tag + "out")


def make_fp4_scratch(ctx, S, pfx, width):
    return {"mx": ctx.sb(S, pfx + "mx", [128, width // 16], F32),
            "scl": ctx.sb(S, pfx + "scl", [128, width // 16], F32),
            "y": ctx.sb(S, pfx + "y", [128, width], F32),
            "acc": ctx.sb(S, pfx + "acc", [128, width], BF16),
            "term": ctx.sb(S, pfx + "term", [128, width], BF16),
            "sgn": ctx.sb(S, pfx + "sgn", [128, width], BF16),
            "key": pfx}


def emit_s6(ctx):
    E = ctx.E
    I_ = 2816
    NI = I_ // 128
    xn1_d = ctx.dio("xn1", (B_CORE, T, D), F32, producer="s4", consumers=("s6",))
    f1_d = ctx.dio("f1", (B_CORE, T, D), F32, producer="s6", consumers=("s7",))
    wgT_d = ctx.dscratch("s6_wgT", (D, I_), BF16)
    wuT_d = ctx.dscratch("s6_wuT", (D, I_), BF16)
    wdT_d = ctx.dscratch("s6_wdT", (I_, D), BF16)
    hqT_d = ctx.dscratch("s6_hqT", (B_CORE, I_, T), BF16)

    w_g = ctx.din("w_L1_ffn_w_gate", (I_, D), F32)
    w_u = ctx.din("w_L1_ffn_w_up", (I_, D), F32)
    w_d = ctx.din("w_L1_ffn_w_down", (D, I_), F32)

    with ctx.stage("s6") as S:
        if S is None:
            return
        qs = make_fp4_scratch(ctx, S, "s6q", I_)
        wnat = ctx.sb(S, "s6wn", [128, I_], F32)
        wq = ctx.sb(S, "s6wq", [128, I_], BF16)
        piece = [ctx.sb(S, "s6pc", [128, 512], BF16) for _ in range(2)]
        for (wsrc, wdst, nr, ncol, kw) in ((w_g, wgT_d, NI, 8, "wg"), (w_u, wuT_d, NI, 8, "wu"),
                                           (w_d, wdT_d, 8, NI, "wd")):
            width = ncol * 128
            for rc in range(nr):
                E.dma("sync", wnat[:, 0:width], wsrc[tile_slice(rc), :],
                      writes=["s6wn"], tag="s6ldw")
                emit_fp4_quant(ctx, wq[:, 0:width], "s6wq", wnat[:, 0:width], "s6wn",
                               width, qs, "s6qw")
                for cc0 in range(0, ncol, 4):
                    ng = min(4, ncol - cc0)
                    bk = 6 + (cc0 // 4) % 2
                    for cc in range(cc0, cc0 + ng):
                        E.op("tensor", lambda e, cc=cc, bk=bk, cc0=cc0: e.transpose(
                            ctx.psbf6(bk, cc - cc0), wq[:, tile_slice(cc)], ctx.consts["ident_bf"][:]),
                            reads=["s6wq", "c_ident_bf"], writes=[f"ps{bk}"])
                    pp = (cc0 // 4) % 2
                    E.op("vector", lambda e, pp=pp, ng=ng, bk=bk: e.tensor_copy(
                        piece[pp][:, 0:ng * 128], ctx.ps[bk][:, 0:ng * 64].bitcast(BF16)),
                        reads=[f"ps{bk}"], writes=[f"s6pc{pp}"])
                    E.dma("sync",
                          wdst[cc0 * 128:(cc0 + ng) * 128, tile_slice(rc)]
                          .rearrange("(a p) c -> p a c", p=128),
                          piece[pp][:, 0:ng * 128].rearrange("p (a c) -> p a c", c=128),
                          reads=[f"s6pc{pp}"], writes=[kw], tag="s6stw")

        xrow = ctx.sb(S, "s6xr", [128, D], F32)
        xrb = ctx.sb(S, "s6xrb", [128, D], BF16)
        xqT = [ctx.sb(S, "s6xqT", [128, T], BF16) for _ in range(8)]
        wgc = [ctx.sb(S, "s6wgc", [128, 512], BF16) for _ in range(8)]
        wuc = [ctx.sb(S, "s6wuc", [128, 512], BF16) for _ in range(8)]
        gs2 = [ctx.sb(S, "s6gs", [128, 512], BF16) for _ in range(2)]
        hf2 = [ctx.sb(S, "s6hf", [128, 512], F32) for _ in range(2)]
        hq2 = [ctx.sb(S, "s6hq", [128, 512], BF16) for _ in range(2)]
        hqt = [ctx.sb(S, "s6hqt", [128, 512], BF16) for _ in range(NI)]
        wds = ctx.sb(S, "s6wds", [128, 4 * NI * 128], BF16)
        fstg = ctx.sb(S, "s6fs", [128, 512], F32)
        trs = [ctx.sb(S, "s6tr", [128, 512], F32) for _ in range(2)]

        IC = [(0, 512), (512, 512), (1024, 512), (1536, 512), (2048, 512), (2560, 256)]

        for b in range(B_CORE):
            for tt in range(16):
                E.dma("sync", xrow[:], xn1_d[b, tile_slice(tt), :], writes=["s6xr"], tag="s6ldx")
                emit_fp4_quant(ctx, xrb[:], "s6xrb", xrow[:], "s6xr", D, qs, "s6qx")
                for ht in range(8):
                    E.op("tensor", lambda e, ht=ht: e.transpose(
                        ctx.psbf6(6, ht), xrb[:, tile_slice(ht)], ctx.consts["ident_bf"][:]),
                        reads=["s6xrb", "c_ident_bf"], writes=["ps6"])
                for ht in range(8):
                    E.op("vector", lambda e, ht=ht, tt=tt: e.tensor_copy(
                        xqT[ht][:, tile_slice(tt)], ctx.psbf6(6, ht)),
                        reads=["ps6"], writes=[f"xq{ht}"])
            for (ic0, icw) in IC:
                for ht in range(8):
                    E.dma("sync", wgc[ht][:, 0:icw], wgT_d[tile_slice(ht), ic0:ic0 + icw],
                          reads=["wg"], writes=[f"wgc{ht}"], tag="s6ldg")
                    E.dma("sync", wuc[ht][:, 0:icw], wuT_d[tile_slice(ht), ic0:ic0 + icw],
                          reads=["wu"], writes=[f"wuc{ht}"], tag="s6ldu")
                for tt in range(16):
                    b_ = tt % 2
                    for ht in range(8):
                        E.op("tensor", lambda e, ht=ht, tt=tt, ic0=ic0, icw=icw, b_=b_: e.matmul(
                            ctx.ps[b_][:, 0:icw], xqT[ht][:, tile_slice(tt)],
                            wgc[ht][:, 0:icw], start=(ht == 0), stop=(ht == 7),
                            skip_group_check=True),
                            reads=[f"xq{ht}", f"wgc{ht}"], writes=[f"ps{b_}"], tag="s6gmm")
                        E.op("tensor", lambda e, ht=ht, tt=tt, ic0=ic0, icw=icw, b_=b_: e.matmul(
                            ctx.ps[2 + b_][:, 0:icw], xqT[ht][:, tile_slice(tt)],
                            wuc[ht][:, 0:icw], start=(ht == 0), stop=(ht == 7),
                            skip_group_check=True),
                            reads=[f"xq{ht}", f"wuc{ht}"], writes=[f"ps{2 + b_}"], tag="s6umm")
                    gs = gs2[b_]
                    hf = hf2[b_]
                    hq = hq2[b_]
                    E.op("scalar", lambda e, icw=icw, b_=b_, gs=gs: e.activation(
                        gs[:, 0:icw], ctx.ps[b_][:, 0:icw], AF.Silu),
                        reads=[f"ps{b_}"], writes=[f"s6gs{b_}"], tag="s6silu")
                    E.op("vector", lambda e, icw=icw, b_=b_, gs=gs, hf=hf: e.tensor_tensor(
                        hf[:, 0:icw], gs[:, 0:icw], ctx.ps[2 + b_][:, 0:icw], ALU.mult),
                        reads=[f"s6gs{b_}", f"ps{2 + b_}"], writes=[f"s6hf{b_}"], tag="s6gu")
                    emit_fp4_quant(ctx, hq[:, 0:icw], f"s6hq{b_}", hf[:, 0:icw], f"s6hf{b_}",
                                   icw, qs, "s6qh")
                    nk2 = icw // 128
                    for k2 in range(nk2):
                        E.op("tensor", lambda e, k2=k2, hq=hq: e.transpose(
                            ctx.psbf6(6, k2), hq[:, tile_slice(k2)], ctx.consts["ident_bf"][:]),
                            reads=[f"s6hq{b_}", "c_ident_bf"], writes=["ps6"])
                    pp = tt % 2
                    E.op("vector", lambda e, pp=pp, nk2=nk2: e.tensor_copy(
                        piece[pp][:, 0:nk2 * 128], ctx.ps[6][:, 0:nk2 * 64].bitcast(BF16)),
                        reads=["ps6"], writes=[f"s6pc{pp}"])
                    E.dma("sync",
                          hqT_d[b, ic0:ic0 + nk2 * 128, tile_slice(tt)]
                          .rearrange("(a p) c -> p a c", p=128),
                          piece[pp][:, 0:nk2 * 128].rearrange("p (a c) -> p a c", c=128),
                          reads=[f"s6pc{pp}"], writes=[f"hqT_{b}"], tag="s6sth")
            for oth in range(2):
                for tc in range(4):
                    for ii in range(NI):
                        E.dma("sync", hqt[ii][:], hqT_d[b, tile_slice(ii), tile_slice(tc, 512)],
                              reads=[f"hqT_{b}"], writes=[f"hqt{ii}"], tag="s6ldh")
                    for ol in range(4):
                        ot = oth * 4 + ol
                        if tc == 0:
                            E.dma("sync", wds[:, ol * NI * 128:(ol + 1) * NI * 128]
                                  .rearrange("p (a c) -> p a c", c=128),
                                  wdT_d.rearrange("(a p) c -> p a c", p=128)[:, :, tile_slice(ot)],
                                  reads=["wd"], writes=[f"wds{ol}"], tag="s6ldwd")
                        for ii in range(NI):
                            E.op("tensor", lambda e, ii=ii, ol=ol, tc=tc: e.matmul(
                                ctx.ps[4 + (tc % 2)][:, :], wds[:, (ol * NI + ii) * 128:(ol * NI + ii + 1) * 128],
                                hqt[ii][:], start=(ii == 0), stop=(ii == NI - 1),
                                skip_group_check=True),
                                reads=[f"hqt{ii}", f"wds{ol}"], writes=[f"ps{4 + (tc % 2)}"], tag="s6dmm")
                        E.op("scalar", lambda e, tc=tc: e.copy(fstg[:], ctx.ps[4 + (tc % 2)][:, :]),
                             reads=[f"ps{4 + (tc % 2)}"], writes=["s6fs"], tag="s6dev")
                        for t2 in range(4):
                            E.op("tensor", lambda e, t2=t2: e.transpose(
                                ctx.ps[1][:, t2 * 128:(t2 + 1) * 128], fstg[:, tile_slice(t2)],
                                ctx.consts["ident_f32"][:]),
                                reads=["s6fs", "c_ident_f32"], writes=["ps1"], tag="s6ftr")
                        pp = ot % 2
                        E.op("scalar", lambda e, pp=pp: e.copy(trs[pp][:], ctx.ps[1][:, :]),
                             reads=["ps1"], writes=[f"s6tr{pp}"])
                        E.dma("sync", f1_d[b, tc * 512:(tc + 1) * 512, tile_slice(ot)]
                              .rearrange("(a p) c -> p a c", p=128),
                              trs[pp][:].rearrange("p (a c) -> p a c", c=128),
                              reads=[f"s6tr{pp}"], writes=[f"d_f1_{b}"], tag="s6stf")


def emit_s7(ctx):
    E = ctx.E
    x1_d = ctx.dio("x1", (B_CORE, T, D), F32, producer="s4", consumers=("s7",))
    s1_d = ctx.dio("s1", (B_CORE, T, D), F32, producer="s5", consumers=("s7",))
    f1_d = ctx.dio("f1", (B_CORE, T, D), F32, producer="s6", consumers=("s7",))
    out_d = ctx.dout("out", (B_CORE, T, D), F32)

    with ctx.stage("s7") as S:
        if S is None:
            return
        xt = ctx.sb(S, "s7x", [128, D], F32)
        st = ctx.sb(S, "s7s", [128, D], F32)
        ft = ctx.sb(S, "s7f", [128, D], F32)
        for b in range(B_CORE):
            for tt in range(16):
                E.dma("sync", xt[:], x1_d[b, tile_slice(tt), :], writes=["s7x"], tag="s7ldx")
                E.dma("sync", st[:], s1_d[b, tile_slice(tt), :], writes=["s7s"], tag="s7lds")
                E.dma("sync", ft[:], f1_d[b, tile_slice(tt), :], writes=["s7f"], tag="s7ldf")
                E.op("scalar", lambda e: e.activation(ft[:], ft[:], AF.Silu),
                     reads=["s7f"], writes=["s7f"], tag="s7silu")
                E.op("vector", lambda e: e.tensor_tensor(ft[:], ft[:], st[:], ALU.mult),
                     reads=["s7f", "s7s"], writes=["s7f"], tag="s7sg")
                E.op("vector", lambda e: e.tensor_tensor(xt[:], xt[:], ft[:], ALU.add),
                     reads=["s7x", "s7f"], writes=["s7x"], tag="s7add")
                E.dma("sync", out_d[b, tile_slice(tt), :], xt[:],
                      reads=["s7x"], writes=[f"d_out_{b}_{tt}"], tag="s7st")



def build_program(stages, fused, host_consts):
    nc = bass.Bass("TRN2", target_bir_lowering=False)
    stack = contextlib.ExitStack()
    with stack:
        ctx = Ctx(nc, stack, fused, stages)
        ctx.init_psum()
        ctx.load_consts(host_consts, names=("ident_bf", "ident_f32"))
        need_rope = any(st_ in stages for st_ in ("s1", "s2"))
        with contextlib.ExitStack() as rope_scope:
            if need_rope:
                ctx.load_consts(host_consts,
                                names=("rot64_mt", "rot32_mt", "cos64", "sin64",
                                       "cos32", "sin32", "stair"),
                                scope=rope_scope)
            emit_s0(ctx)
            emit_s1(ctx)
            emit_s2(ctx)
        ctx.E.barrier()
        emit_s3(ctx)
        emit_s4(ctx)
        emit_s5(ctx)
        emit_s6(ctx)
        emit_s7(ctx)
        ctx.E.barrier()
        ctx.E.emit()
    return nc, ctx


def kernel(**inputs):
    host_consts = build_consts()
    x_full = np.ascontiguousarray(inputs["L0_attn_x"], dtype=np.float32)
    nc, ctx = build_program(ALL_STAGES, True, host_consts)

    weight_arrays = {}
    for k, v in inputs.items():
        if k == "L0_attn_x":
            continue
        name = "w_" + k
        if name in ctx.dram:
            weight_arrays[name] = np.ascontiguousarray(v, dtype=np.float32)
    const_arrays = {("c_" + n): np.asarray(a) for n, a in host_consts.items()
                    if ("c_" + n) in ctx.dram}

    in_maps = []
    for core in range(8):
        m = {"x": x_full[core * B_CORE:(core + 1) * B_CORE]}
        m.update(const_arrays)
        m.update(weight_arrays)
        in_maps.append(m)
    res = run_bass_kernel_spmd(nc, in_maps, core_ids=list(range(8)))
    out = np.concatenate([np.asarray(r["out"]) for r in res.results], axis=0)
    return out


# version 09
# speedup vs baseline: 1.1458x; 1.1458x over previous
import bisect
import contextlib
import numpy as np
import ml_dtypes
import concourse.bass as bass
import concourse.mybir as mybir
from concourse.bass_utils import run_bass_kernel_spmd

F32 = mybir.dt.float32
BF16 = mybir.dt.bfloat16
U8 = mybir.dt.uint8
AF = mybir.ActivationFunctionType
ALU = mybir.AluOpType
AX = mybir.AxisListType

ENGINES = ("sync", "tensor", "vector", "scalar", "gpsimd")

B_CORE = 2
T = 2048
D = 1024
ALL_STAGES = ("s0", "s1", "s2", "s3", "s4", "s5", "s6", "s7")
S2_PHASES = 99
S2_LAT_LIMIT = 99
S2_LAT_MODE = 'full'



class _Op:
    __slots__ = ("eng", "fn", "reads", "writes", "is_dma", "pos", "dma_idx",
                 "deps", "needs_inc", "inc_value", "waits", "drain_before", "tag",
                 "gidx", "same_dep_max", "wait_ops")

    def __init__(self):
        self.deps = []
        self.needs_inc = False
        self.inc_value = None
        self.waits = []
        self.drain_before = False
        self.dma_idx = None
        self.tag = ""
        self.gidx = 0
        self.same_dep_max = -1
        self.wait_ops = []


class Emitter:

    def __init__(self, nc, stack):
        self.nc = nc
        self.stack = stack
        self.per_eng = {e: [] for e in ENGINES}
        self.last_write = {}
        self.readers = {}
        self.sem = {e: stack.enter_context(nc.semaphore(f"prog_{e}")) for e in ENGINES}
        self.dma_ring = {"sync": 8, "gpsimd": 4, "scalar": 2}
        self.dma_sem = {e: [stack.enter_context(nc.semaphore(f"dma_{e}{i}"))
                            for i in range(r)] for e, r in self.dma_ring.items()}
        self.dma_count = {e: 0 for e in ENGINES}
        self.gcount = 0
        self.nbar = 0

    def op(self, eng, fn, reads=(), writes=(), tag=""):
        o = _Op()
        o.eng, o.fn, o.reads, o.writes = eng, fn, tuple(reads), tuple(writes)
        o.is_dma = False
        o.pos = len(self.per_eng[eng])
        o.gidx = self.gcount
        self.gcount += 1
        o.tag = tag
        self._resolve(o)
        self.per_eng[eng].append(o)
        self._update(o)
        return o

    def dma(self, queue, out_ap, in_ap, reads=(), writes=(), tag="", **kw):
        assert queue in ("sync", "gpsimd", "scalar")
        o = _Op()
        o.eng = queue
        o.fn = lambda e, _o=out_ap, _i=in_ap, _kw=kw: e.dma_start(out=_o, in_=_i, **_kw)
        o.reads, o.writes = tuple(reads), tuple(writes)
        o.is_dma = True
        o.pos = len(self.per_eng[queue])
        o.gidx = self.gcount
        self.gcount += 1
        o.dma_idx = self.dma_count[queue]
        self.dma_count[queue] += 1
        o.tag = tag
        self._resolve(o)
        R = self.dma_ring[queue]
        i = o.dma_idx
        if i >= R:
            o.waits.append((("dmaslot", queue, i % R), 16 * (i // R)))
        self.per_eng[queue].append(o)
        self._update(o)
        return o

    def barrier(self):
        n = self.nbar
        self.nbar += 1
        keys = [f"__bar{n}_{e}" for e in ENGINES]
        for e, k in zip(ENGINES, keys):
            self.op(e, None, reads=(), writes=(k,), tag=f"bar{n}")
        for e in ENGINES:
            self.op(e, None, reads=keys, writes=(), tag=f"barw{n}")

    def _resolve(self, o):
        deps = {}
        for k in o.reads:
            w = self.last_write.get(k)
            if w is not None:
                deps[id(w)] = w
        for k in o.writes:
            w = self.last_write.get(k)
            if w is not None:
                deps[id(w)] = w
            for r in self.readers.get(k, ()):
                deps[id(r)] = r
        deps.pop(id(o), None)
        for d in deps.values():
            if d.eng == o.eng and not d.is_dma:
                if o.eng == "tensor":
                    continue
                if d.pos > o.same_dep_max:
                    o.same_dep_max = d.pos
            else:
                o.deps.append(d)

    def _update(self, o):
        for k in o.reads:
            self.readers.setdefault(k, []).append(o)
        for k in o.writes:
            self.last_write[k] = o
            self.readers[k] = []

    def emit(self):
        all_ops = sorted((o for e in ENGINES for o in self.per_eng[e]),
                         key=lambda o: o.gidx)
        inc_list = {e: [] for e in ENGINES}
        for o in all_ops:
            for d in o.deps:
                if d.is_dma:
                    o.wait_ops.append(d)
                    continue
                lst = inc_list[d.eng]
                j = bisect.bisect_left(lst, d.pos, key=lambda x: x.pos)
                if j < len(lst) and lst[j].gidx < o.gidx:
                    o.wait_ops.append(lst[j])
                else:
                    d.needs_inc = True
                    bisect.insort(lst, d, key=lambda x: x.pos)
                    o.wait_ops.append(d)
        for e in ENGINES:
            v = 0
            for o in self.per_eng[e]:
                if o.needs_inc and not o.is_dma:
                    v += 1
                    o.inc_value = v
        for e in ENGINES:
            if e == "tensor":
                continue
            covered = -1.0
            for o in self.per_eng[e]:
                if o.same_dep_max > covered:
                    o.drain_before = True
                    covered = o.pos - 0.5
                if o.needs_inc and not o.is_dma:
                    covered = o.pos
        for e in ENGINES:
            hw = {}
            for o in self.per_eng[e]:
                pre = list(o.waits)
                o.waits = []
                for (key, val) in pre:
                    if val > hw.get(key, 0):
                        o.waits.append((key, val))
                        hw[key] = val
                for d in o.wait_ops:
                    if d.is_dma:
                        R = self.dma_ring[d.eng]
                        key = ("dmaslot", d.eng, d.dma_idx % R)
                        val = 16 * (d.dma_idx // R + 1)
                    else:
                        key = ("prog", d.eng)
                        val = d.inc_value
                    if val > hw.get(key, 0):
                        o.waits.append((key, val))
                        hw[key] = val

        nc = self.nc
        block = self.stack.enter_context(nc.Block())

        def make_runner(eng_name):
            def run(E):
                for o in self.per_eng[eng_name]:
                    if o.drain_before:
                        E.drain()
                    for (sem_key, val) in o.waits:
                        if sem_key[0] == "prog":
                            s = self.sem[sem_key[1]]
                        else:
                            s = self.dma_sem[sem_key[1]][sem_key[2]]
                        E.wait_ge(s, val)
                    instr = o.fn(E) if o.fn is not None else None
                    if o.is_dma:
                        R = self.dma_ring[o.eng]
                        instr.then_inc(self.dma_sem[o.eng][o.dma_idx % R], 16)
                    elif o.needs_inc:
                        if o.eng == "tensor" and instr is not None:
                            instr.then_inc(self.sem[o.eng], 1)
                        else:
                            E.drain().then_inc(self.sem[o.eng], 1)
            return run

        for e in ENGINES:
            if self.per_eng[e]:
                getattr(block, e)(make_runner(e))
        return block



def build_consts():
    c = {}
    c["ident_bf"] = np.eye(128, dtype=ml_dtypes.bfloat16)
    c["ident_f32"] = np.eye(128, dtype=np.float32)

    def rot_mt(h):
        m = np.zeros((128, 128), np.float32)
        for i in range(128):
            j = i % h
            base = i - j
            if j < h // 2:
                m[i, base + j + h // 2] = -1.0
            else:
                m[i, base + j - h // 2] = 1.0
        return np.ascontiguousarray(m.T.astype(ml_dtypes.bfloat16))
    c["rot64_mt"] = rot_mt(64)
    c["rot32_mt"] = rot_mt(32)

    def rope_tab(dh):
        inv = 1.0 / (10000.0 ** (np.arange(0, dh, 2, dtype=np.float64) / dh))
        freqs = np.arange(T, dtype=np.float64)[:, None] * inv[None, :]
        emb = np.concatenate([freqs, freqs], axis=-1)
        cosF = np.tile(np.cos(emb).T, (128 // dh, 1)).astype(ml_dtypes.bfloat16)
        sinF = np.tile(np.sin(emb).T, (128 // dh, 1)).astype(ml_dtypes.bfloat16)
        return np.ascontiguousarray(cosF), np.ascontiguousarray(sinF)
    c["cos64"], c["sin64"] = rope_tab(64)
    c["cos32"], c["sin32"] = rope_tab(32)

    p = np.arange(128)[:, None]
    col = np.arange(1024)[None, :]
    c["stair"] = np.ascontiguousarray((col >= p + 512).astype(ml_dtypes.bfloat16))
    return c



class Ctx:
    def __init__(self, nc, stack, fused, stages):
        self.nc = nc
        self.stack = stack
        self.E = Emitter(nc, stack)
        self.fused = fused
        self.stages = stages
        self.dram = {}
        self.kinds = {}
        self._uid = 0
        self.consts = {}
        self.ps = []

    def uid(self):
        self._uid += 1
        return self._uid

    def din(self, name, shape, dtype=F32):
        if name not in self.dram:
            self.dram[name] = self.nc.dram_tensor(name, list(shape), dtype, kind="ExternalInput").ap()
            self.kinds[name] = "ExternalInput"
        return self.dram[name]

    def dout(self, name, shape, dtype=F32):
        if name not in self.dram:
            self.dram[name] = self.nc.dram_tensor(name, list(shape), dtype, kind="ExternalOutput").ap()
            self.kinds[name] = "ExternalOutput"
        return self.dram[name]

    def dio(self, name, shape, dtype=F32, producer=None, consumers=()):
        if name in self.dram:
            return self.dram[name]
        if self.fused:
            kind = "Internal"
        else:
            prod = producer in self.stages
            cons = any(s in self.stages for s in consumers)
            if prod and cons:
                kind = "Internal"
            elif prod:
                kind = "ExternalOutput"
            elif cons:
                kind = "ExternalInput"
            else:
                kind = "Internal"
        self.dram[name] = self.nc.dram_tensor(name, list(shape), dtype, kind=kind).ap()
        self.kinds[name] = kind
        return self.dram[name]

    def dscratch(self, name, shape, dtype=F32):
        if name not in self.dram:
            self.dram[name] = self.nc.dram_tensor(name, list(shape), dtype, kind="Internal").ap()
            self.kinds[name] = "Internal"
        return self.dram[name]

    def sb(self, scope, name, shape, dtype=F32):
        return scope.enter_context(self.nc.sbuf_tensor(f"{name}_{self.uid()}", list(shape), dtype))

    def init_psum(self):
        self.ps_all = self.stack.enter_context(self.nc.psum_tensor("psall", [128, 4096], F32))
        for i in range(8):
            self.ps.append(self.ps_all[:, i * 512:(i + 1) * 512])

    def psbf(self, i):
        return self.ps[i][:, 0:64].bitcast(BF16)

    def psbf6(self, i, slot):
        return self.ps[i][:, slot * 64:(slot + 1) * 64].bitcast(BF16)

    def load_consts(self, host_consts, names=None, scope=None):
        E = self.E
        for name, arr in host_consts.items():
            if names is not None and name not in names:
                continue
            dt = BF16 if arr.dtype == ml_dtypes.bfloat16 else F32
            d = self.din("c_" + name, arr.shape, dt)
            t = self.sb(scope or self.stack, "c_" + name, list(arr.shape), dt)
            E.dma("sync", t[:], d[:, :], reads=(), writes=("c_" + name,), tag="const")
            self.consts[name] = t
        if not hasattr(self, "_sconsts"):
            self._sconsts = {}
            for v in (1e-5, 1.0):
                self.sconst(v)

    def sconst(self, val):
        if val not in self._sconsts:
            t = self.sb(self.stack, f"sc_{len(self._sconsts)}", [128, 1], F32)
            self.E.op("vector", lambda e, t=t, val=val: e.memset(t[:], float(val)),
                      reads=(), writes=(f"sconst_{val}",))
            self._sconsts[val] = (t, f"sconst_{val}")
        return self._sconsts[val]

    @contextlib.contextmanager
    def stage(self, name):
        if name not in self.stages:
            yield None
            return
        with contextlib.ExitStack() as s:
            yield s
        self.E.barrier()



def tile_slice(i, w=128):
    return slice(i * w, (i + 1) * w)


def emit_ln_tile(ctx, sc, x_ap, kx, out_ap, kout, tag, d_model=D, eps=1e-5):
    E = ctx.E
    sq, ssq, sm, var, m_t, r_t = sc
    E.op("scalar", lambda e: e.activation(sq[:], x_ap, AF.Square, accum_out=ssq[:, :]),
         reads=[kx], writes=[tag + "sq", tag + "ssq"])
    E.op("vector", lambda e: e.tensor_reduce(sm[:].unsqueeze(2), x_ap.unsqueeze(1), AX.X, ALU.add),
         reads=[kx], writes=[tag + "sm"])
    E.op("vector", lambda e: e.tensor_scalar(m_t[:], sm[:], 1.0 / d_model, None, ALU.mult),
         reads=[tag + "sm"], writes=[tag + "m"])
    E.op("vector", lambda e: e.tensor_tensor(sm[:], m_t[:], m_t[:], ALU.mult),
         reads=[tag + "m"], writes=[tag + "sm"])
    E.op("vector", lambda e: e.tensor_scalar(var[:], ssq[:], 1.0 / d_model, None, ALU.mult),
         reads=[tag + "ssq"], writes=[tag + "v"])
    E.op("vector", lambda e: e.tensor_tensor(var[:], var[:], sm[:], ALU.subtract),
         reads=[tag + "v", tag + "sm"], writes=[tag + "v"])
    epst, keps = ctx.sconst(eps)
    E.op("scalar", lambda e: e.activation(var[:], var[:], AF.Ln, bias=epst[:, :]),
         reads=[tag + "v", keps], writes=[tag + "v"])
    E.op("scalar", lambda e: e.activation(r_t[:], var[:], AF.Exp, scale=-0.5),
         reads=[tag + "v"], writes=[tag + "r"])
    E.op("vector", lambda e: e.tensor_scalar(out_ap, x_ap, m_t[:], r_t[:], ALU.subtract, ALU.mult),
         reads=[kx, tag + "m", tag + "r"], writes=[kout])


def make_ln_scratch(ctx, S, pfx):
    return (ctx.sb(S, pfx + "sq", [128, D], F32),
            ctx.sb(S, pfx + "ssq", [128, 1], F32),
            ctx.sb(S, pfx + "sm", [128, 1], F32),
            ctx.sb(S, pfx + "var", [128, 1], F32),
            ctx.sb(S, pfx + "m", [128, 1], F32),
            ctx.sb(S, pfx + "r", [128, 1], F32))


def emit_transpose_tile(ctx, dst_ap, kdst, src_ap, ksrc, dtype, psum_bank, kpsum, scale_ap=None, out_dtype=None):
    E = ctx.E
    ident = ctx.consts["ident_bf"] if dtype == BF16 else ctx.consts["ident_f32"]
    ps = psum_bank
    pview = ps[:, 0:128] if dtype != BF16 else ps[:, 0:128]
    E.op("tensor", lambda e: e.transpose(pview, src_ap, ident[:]),
         reads=[ksrc], writes=[kpsum])
    if scale_ap is None:
        E.op("scalar", lambda e: e.copy(dst_ap, pview), reads=[kpsum], writes=[kdst])
    else:
        E.op("scalar", lambda e: e.activation(dst_ap, pview, AF.Copy, scale=scale_ap),
             reads=[kpsum, "gain"], writes=[kdst])



def emit_s0(ctx):
    E = ctx.E
    x_d = ctx.din("x", (B_CORE, T, D), F32)
    xn_d = ctx.dio("xn", (B_CORE, T, D), F32, producer="s0", consumers=("s2", "s3"))
    xnT_d = ctx.dio("xnT", (B_CORE, D, T), BF16, producer="s0", consumers=("s1", "s2", "s3"))

    with ctx.stage("s0") as S:
        if S is None:
            return
        xt = [ctx.sb(S, "s0x", [128, D], F32) for _ in range(2)]
        xn_t = [ctx.sb(S, "s0n", [128, D], F32) for _ in range(2)]
        stg = [ctx.sb(S, "s0stg", [128, 512], BF16) for _ in range(2)]
        sc = make_ln_scratch(ctx, S, "s0")

        for b in range(B_CORE):
            for tt in range(T // 128):
                p = (b * 16 + tt) % 2
                kx = f"s0x{p}"
                kn = f"s0n{p}"
                E.dma("sync", xt[p][:], x_d[b, tile_slice(tt), :], writes=[kx], tag="s0ld")
                emit_ln_tile(ctx, sc, xt[p][:], kx, xn_t[p][:], kn, "s0ln")
                E.dma("sync", xn_d[b, tile_slice(tt), :], xn_t[p][:], reads=[kn],
                      writes=[f"d_xn_{b}_{tt}"], tag="s0st")
                for dt_i in range(D // 128):
                    bk = 6 + dt_i // 4
                    E.op("tensor", lambda e, p=p, dt_i=dt_i, bk=bk: e.transpose(
                        ctx.ps[bk][:, (dt_i % 4) * 128:(dt_i % 4 + 1) * 128],
                        xn_t[p][:, tile_slice(dt_i)], ctx.consts["ident_f32"][:]),
                        reads=[kn, "c_ident_f32"], writes=[f"ps{bk}"])
                for gg in range(2):
                    bk = 6 + gg
                    E.op("scalar", lambda e, gg=gg, bk=bk: e.copy(stg[gg][:], ctx.ps[bk][:, :]),
                         reads=[f"ps{bk}"], writes=[f"s0stg{gg}"])
                    E.dma("sync",
                          xnT_d[b, gg * 512:(gg + 1) * 512, tile_slice(tt)]
                          .rearrange("(a p) c -> p a c", p=128),
                          stg[gg][:].rearrange("p (a c) -> p a c", c=128),
                          reads=[f"s0stg{gg}"], writes=[f"d_xnT_{b}_{gg}_{tt}"], tag="s0stT")



def emit_rope_pair(ctx, S, out_h, kout, src_h, ksrc, rot_mt_name, cos_name, sin_name,
                   rtmp, ps_bank, kps, tagu):
    E = ctx.E
    krt = f"rtmp_{id(rtmp)}"
    rot = ctx.consts[rot_mt_name]
    cosc = ctx.consts[cos_name]
    sinc = ctx.consts[sin_name]
    for tc in range(4):
        E.op("tensor", lambda e, tc=tc: e.matmul(
            ps_bank[:, 0:512], rot[:], src_h[:, tile_slice(tc, 512)], start=True, stop=True,
            skip_group_check=True),
            reads=[ksrc, "c_" + rot_mt_name], writes=[kps], tag=tagu + "rot")
        E.op("vector", lambda e, tc=tc: e.tensor_tensor(
            rtmp[:, tile_slice(tc, 512)], ps_bank[:, 0:512], sinc[:, tile_slice(tc, 512)], ALU.mult),
            reads=[kps, "c_" + sin_name], writes=[krt], tag=tagu + "sin")
    E.op("vector", lambda e: e.tensor_tensor(out_h[:], src_h[:], cosc[:], ALU.mult),
         reads=[ksrc, "c_" + cos_name], writes=[kout], tag=tagu + "cos")
    E.op("vector", lambda e: e.tensor_tensor(out_h[:], out_h[:], rtmp[:], ALU.add),
         reads=[kout, krt], writes=[kout], tag=tagu + "add")


def emit_s1(ctx):
    E = ctx.E
    nc = ctx.nc
    xnT_d = ctx.dio("xnT", (B_CORE, D, T), BF16, producer="s0", consumers=("s1", "s2", "s3"))
    s0_d = ctx.dio("s0", (B_CORE, T, D), F32, producer="s1", consumers=("s4",))
    zs_d = ctx.dscratch("s1_zs", (B_CORE, 16, T), F32)
    zr_d = ctx.dscratch("s1_zr", (B_CORE, 16, T), F32)

    w_dk = ctx.din("w_L0_attn_w_dk", (D, 128), F32)
    w_dv = ctx.din("w_L0_attn_w_dv", (D, 128), F32)
    w_uk = ctx.din("w_L0_attn_w_uk", (128, D), F32)
    w_uv = ctx.din("w_L0_attn_w_uv", (128, D), F32)
    w_o = ctx.din("w_L0_attn_w_o", (D, D), F32)
    b_dk = ctx.din("w_L0_attn_b_dk", (128,), F32)
    b_dv = ctx.din("w_L0_attn_b_dv", (128,), F32)
    b_uk = ctx.din("w_L0_attn_b_uk", (D,), F32)
    b_uv = ctx.din("w_L0_attn_b_uv", (D,), F32)
    b_o = ctx.din("w_L0_attn_b_o", (D,), F32)

    with ctx.stage("s1") as S:
        if S is None:
            return
        wdk = ctx.sb(S, "wdk", [128, 8 * 128], BF16)
        wdv = ctx.sb(S, "wdv", [128, 8 * 128], BF16)
        wuk = ctx.sb(S, "wuk", [128, D], BF16)
        wuv = ctx.sb(S, "wuv", [128, D], BF16)
        wo = ctx.sb(S, "wo", [128, 8 * D], BF16)
        E.dma("gpsimd", wdk[:].rearrange("p (h c) -> p h c", h=8),
              w_dk.rearrange("(h p) c -> p h c", p=128), writes=["wdk"])
        E.dma("gpsimd", wdv[:].rearrange("p (h c) -> p h c", h=8),
              w_dv.rearrange("(h p) c -> p h c", p=128), writes=["wdv"])
        E.dma("gpsimd", wuk[:], w_uk[:, :], writes=["wuk"])
        E.dma("gpsimd", wuv[:], w_uv[:, :], writes=["wuv"])
        E.dma("gpsimd", wo[:].rearrange("p (h o) -> p h o", h=8),
              w_o.rearrange("(h p) o -> p h o", p=128), writes=["wo"])
        bdk = ctx.sb(S, "bdk", [128, 1], F32)
        bdv = ctx.sb(S, "bdv", [128, 1], F32)
        buk = ctx.sb(S, "buk", [128, 8], F32)
        buv = ctx.sb(S, "buv", [128, 8], F32)
        bo = ctx.sb(S, "bo", [128, 8], F32)
        E.dma("sync", bdk[:], b_dk.rearrange("(a p) -> p a", p=128), writes=["bdk"], allow_slow_non_contiguous=True)
        E.dma("sync", bdv[:], b_dv.rearrange("(a p) -> p a", p=128), writes=["bdv"], allow_slow_non_contiguous=True)
        E.dma("sync", buk[:], b_uk.rearrange("(a p) -> p a", p=128), writes=["buk"], allow_slow_non_contiguous=True)
        E.dma("sync", buv[:], b_uv.rearrange("(a p) -> p a", p=128), writes=["buv"], allow_slow_non_contiguous=True)
        E.dma("sync", bo[:], b_o.rearrange("(a p) -> p a", p=128), writes=["bo"], allow_slow_non_contiguous=True)

        xnt = [ctx.sb(S, "s1xnt", [128, T], BF16) for _ in range(8)]
        ck = ctx.sb(S, "s1ck", [128, T], BF16)
        cv = ctx.sb(S, "s1cv", [128, T], BF16)
        vsb = [ctx.sb(S, "s1v", [128, 16 * 65], BF16) for _ in range(16)]
        keff = ctx.sb(S, "s1ke", [128, T], BF16)
        qeff = ctx.sb(S, "s1qe", [128, T], BF16)
        ktmp = ctx.sb(S, "s1kt", [128, T], BF16)
        rtmp = ctx.sb(S, "s1rt", [128, T], BF16)
        psb = [ctx.sb(S, "s1p", [128, 1024], BF16) for _ in range(3)]
        avsb = [ctx.sb(S, "s1av", [128, T], BF16) for _ in range(8)]
        avn = avsb
        zstg = ctx.sb(S, "s1zstg", [1, 512], F32)
        zall = ctx.sb(S, "s1zall", [16, T], F32)
        zbc = ctx.sb(S, "s1zbc", [128, T], F32)
        sostg = ctx.sb(S, "s1so", [128, 512], F32)
        trstg = [ctx.sb(S, "s1tr", [128, 512], F32) for _ in range(2)]

        for b in range(B_CORE):
            kb = f"b{b}"
            for ht in range(8):
                E.dma("sync", xnt[ht][:], xnT_d[b, tile_slice(ht), :], writes=[f"xnt{ht}"], tag="s1ldx")
            for (csb, wsb, bias, kc, kw, kbias) in ((ck, wdk, bdk, "ck", "wdk", "bdk"),
                                                    (cv, wdv, bdv, "cv", "wdv", "bdv")):
                for tc in range(4):
                    bk = 6 + tc % 2
                    for ht in range(8):
                        E.op("tensor", lambda e, tc=tc, ht=ht, wsb=wsb, bk=bk: e.matmul(
                            ctx.ps[bk][:, :], wsb[:, tile_slice(ht)],
                            xnt[ht][:, tile_slice(tc, 512)], start=(ht == 0), stop=(ht == 7), skip_group_check=True),
                            reads=[f"xnt{ht}", kw], writes=[f"ps{bk}"], tag="s1c")
                    E.op("scalar", lambda e, tc=tc, csb=csb, bias=bias, bk=bk: e.activation(
                        csb[:, tile_slice(tc, 512)], ctx.ps[bk][:, :], AF.Identity, bias=bias[:, :]),
                        reads=[f"ps{bk}", kbias], writes=[kc], tag="s1cev")

            for tp in range(16):
                E.op("vector", lambda e, tp=tp: e.memset(vsb[tp][:], 1.0),
                     writes=[f"v{tp}"], tag="s1vm")
                for ch in range(2):
                    bk = 6 + ch
                    E.op("tensor", lambda e, tp=tp, ch=ch, bk=bk: e.matmul(
                        ctx.ps[bk][:, :], cv[:, tile_slice(tp)],
                        wuv[:, tile_slice(ch, 512)], start=True, stop=True, skip_group_check=True),
                        reads=["cv", "wuv"], writes=[f"ps{bk}"], tag="s1vmm")
                    for hh in range(8):
                        h = ch * 8 + hh
                        E.op("scalar", lambda e, tp=tp, h=h, hh=hh, bk=bk: e.copy(
                            vsb[tp][:, h * 65:h * 65 + 64], ctx.ps[bk][:, hh * 64:(hh + 1) * 64]),
                            reads=[f"ps{bk}"], writes=[f"v{tp}"], tag="s1vev")

            for pair in range(8):
                for tc in range(4):
                    bk = 6 + tc % 2
                    E.op("tensor", lambda e, tc=tc, pair=pair, bk=bk: e.matmul(
                        ctx.ps[bk][:, :], wuk[:, tile_slice(pair)],
                        ck[:, tile_slice(tc, 512)], start=True, stop=True, skip_group_check=True),
                        reads=["ck", "wuk"], writes=[f"ps{bk}"], tag="s1kmm")
                    E.op("scalar", lambda e, tc=tc, pair=pair, bk=bk: e.activation(
                        ktmp[:, tile_slice(tc, 512)], ctx.ps[bk][:, :], AF.Identity,
                        bias=buk[:, pair:pair + 1]),
                        reads=[f"ps{bk}", "buk"], writes=["ktmp"], tag="s1kev")
                emit_rope_pair(ctx, S, keff, "keff", ktmp, "ktmp",
                               "rot64_mt", "cos64", "sin64", rtmp, ctx.ps[6], "ps6", "s1kr")
                emit_rope_pair(ctx, S, qeff, "qeff", xnt[pair], f"xnt{pair}",
                               "rot64_mt", "cos64", "sin64", rtmp, ctx.ps[6], "ps6", "s1qr")

                PBMAP = ((0, 1), (2, 3), (6, 7))
                for qq in range(4):
                    for kp in range(8):
                        for hh in range(2):
                            h = pair * 2 + hh
                            r0 = hh * 64
                            avbank = 4 + hh
                            pidx = (kp * 2 + hh) % 3
                            b0, b1 = PBMAP[pidx]
                            for half in range(2):
                                kk = kp * 2 + half
                                bb = (b0, b1)[half]
                                E.op("tensor", lambda e, kk=kk, qq=qq, r0=r0, bb=bb: e.matmul(
                                    ctx.ps[bb][:, :], keff[r0:r0 + 64, tile_slice(kk)],
                                    qeff[r0:r0 + 64, tile_slice(qq, 512)], start=True, stop=True, skip_group_check=True),
                                    reads=["keff", "qeff"], writes=[f"ps{bb}"], tag="s1smm")
                            E.op("scalar", lambda e, b0=b0, pidx=pidx: e.activation(
                                psb[pidx][:], ctx.ps_all[:, b0 * 512:(b0 + 2) * 512], AF.Exp, scale=0.125),
                                reads=[f"ps{b0}", f"ps{b0 + 1}"], writes=[f"p{pidx}"], tag="s1exp")
                            for half in range(2):
                                kk = kp * 2 + half
                                E.op("tensor", lambda e, kk=kk, h=h, pidx=pidx, half=half, avbank=avbank: e.matmul(
                                    ctx.ps[avbank][0:65, :], vsb[kk][:, h * 65:(h + 1) * 65],
                                    psb[pidx][:, half * 512:(half + 1) * 512],
                                    start=(kk == 0), stop=(kk == 15),
                                    skip_group_check=True),
                                    reads=[f"p{pidx}", f"v{kk}"], writes=[f"ps{avbank}"], tag="s1avmm")
                    for hh in range(2):
                        h = pair * 2 + hh
                        r0 = hh * 64
                        avbank = 4 + hh
                        E.op("scalar", lambda e, qq=qq, r0=r0, pair=pair, avbank=avbank: e.copy(
                            avsb[pair][r0:r0 + 64, tile_slice(qq, 512)], ctx.ps[avbank][0:64, :]),
                            reads=[f"ps{avbank}"], writes=[f"av{pair}"], tag="s1avev")
                        E.op("scalar", lambda e, avbank=avbank: e.copy(
                            zstg[:, :], ctx.ps[avbank][64:65, :]),
                            reads=[f"ps{avbank}"], writes=["zstg"], tag="s1zev")
                        E.dma("sync", zs_d[b, h, tile_slice(qq, 512)], zstg[:, :],
                              reads=["zstg"], writes=[f"zs{h}{qq}"], tag="s1zst")

            E.dma("sync", zall[:], zs_d[b],
                  reads=[f"zs{h}{qq}" for h in range(16) for qq in range(4)],
                  writes=["zall"], tag="s1zld")
            E.op("vector", lambda e: e.reciprocal(zall[:], zall[:]), reads=["zall"], writes=["zall"])
            E.dma("sync", zr_d[b], zall[:], reads=["zall"], writes=["zr"], tag="s1zrst")

            for pair in range(8):
                for hh in range(2):
                    E.dma("sync", zbc[hh * 64:(hh + 1) * 64, :],
                          zr_d[b, pair * 2 + hh:pair * 2 + hh + 1, :].broadcast_to([64, T]),
                          reads=["zr"], writes=["zbc"], tag="s1zbc")
                E.op("vector", lambda e, pair=pair: e.tensor_tensor(
                    avn[pair][:], avsb[pair][:], zbc[:], ALU.mult),
                    reads=[f"av{pair}", "zbc"], writes=[f"av{pair}", f"avn{pair}"], tag="s1nrm")
                E.op("vector", lambda e, pair=pair: e.tensor_scalar(
                    avn[pair][:], avn[pair][:], buv[:, pair:pair + 1], None, ALU.add),
                    reads=[f"avn{pair}", "buv"], writes=[f"avn{pair}"], tag="s1buv")

            for ot in range(8):
                for tc in range(4):
                    bk = 6 + tc % 2
                    E.op("tensor", lambda e, ot=ot, tc=tc, bk=bk: e.matmul(
                        ctx.ps[bk][:, :], wo[:, 0 * D + ot * 128: 0 * D + (ot + 1) * 128],
                        avn[0][:, tile_slice(tc, 512)], start=True, stop=False, skip_group_check=True),
                        reads=["avn0", "wo"], writes=[f"ps{bk}"], tag="s1omm")
                    for ht in range(1, 8):
                        E.op("tensor", lambda e, ot=ot, tc=tc, ht=ht, bk=bk: e.matmul(
                            ctx.ps[bk][:, :], wo[:, ht * D + ot * 128: ht * D + (ot + 1) * 128],
                            avn[ht][:, tile_slice(tc, 512)], start=False, stop=(ht == 7), skip_group_check=True),
                            reads=[f"avn{ht}", "wo"], writes=[f"ps{bk}"], tag="s1omm")
                    E.op("scalar", lambda e, ot=ot, tc=tc, bk=bk: e.activation(
                        sostg[:], ctx.ps[bk][:, :], AF.Identity, bias=bo[:, ot:ot + 1]),
                        reads=[f"ps{bk}", "bo"], writes=["sostg"], tag="s1oev")
                    tb = 2 + tc % 2
                    for t2 in range(4):
                        E.op("tensor", lambda e, t2=t2, tb=tb: e.transpose(
                            ctx.ps[tb][:, t2 * 128:(t2 + 1) * 128], sostg[:, tile_slice(t2)],
                            ctx.consts["ident_f32"][:]),
                            reads=["sostg", "c_ident_f32"], writes=[f"ps{tb}"], tag="s1otr")
                    pr = tc % 2
                    E.op("scalar", lambda e, tb=tb, pr=pr: e.copy(trstg[pr][:], ctx.ps[tb][:, :]),
                         reads=[f"ps{tb}"], writes=[f"trstg{pr}"], tag="s1otrev")
                    E.dma("sync", s0_d[b, tc * 512:(tc + 1) * 512, tile_slice(ot)]
                          .rearrange("(a p) c -> p a c", p=128),
                          trstg[pr][:].rearrange("p (a c) -> p a c", c=128),
                          reads=[f"trstg{pr}"], writes=[f"d_s0_{b}_{ot}_{tc}"], tag="s1ost")



def emit_load_wT(ctx, S, dram_ap, rows, cols, name, head_sliced=None, nat_buf=None):
    E = ctx.E
    nrt, nct = rows // 128, cols // 128
    wT = ctx.sb(S, name + "T", [128, nct * rows], BF16)
    if True:
        nat = nat_buf
        if head_sliced is None:
            E.dma("gpsimd", nat[:, 0:nrt * cols].rearrange("p (rt c) -> p rt c", rt=nrt),
                  dram_ap.rearrange("(rt p) c -> p rt c", p=128), writes=["prepnat"])
        else:
            src_full, nh, lo, tw = head_sliced
            hstride = src_full.shape[0] // nh
            per_tile = 128 // tw
            for h in range(nh):
                rt = h // per_tile
                p0 = (h % per_tile) * tw
                E.dma("gpsimd", nat[p0:p0 + tw, rt * cols:(rt + 1) * cols],
                      src_full[h * hstride + lo: h * hstride + lo + tw, :],
                      writes=["prepnat"])
        for ht in range(nct):
            bk = 6 + ht % 2
            for rt in range(nrt):
                E.op("tensor", lambda e, ht=ht, rt=rt, bk=bk: e.transpose(
                    ctx.psbf6(bk, rt), nat[:, rt * cols + ht * 128: rt * cols + (ht + 1) * 128],
                    ctx.consts["ident_bf"][:]),
                    reads=["prepnat", "c_ident_bf"], writes=[f"ps{bk}"])
            E.op("vector", lambda e, ht=ht, bk=bk, nrt=nrt: e.tensor_copy(
                wT[:, ht * rows: ht * rows + nrt * 128], ctx.ps[bk][:, 0:nrt * 64].bitcast(BF16)),
                reads=[f"ps{bk}"], writes=[name], tag="wTev")
    return wT


def emit_s2(ctx):
    E = ctx.E
    NH, HD, RD, L = 16, 64, 32, 256
    xn_d = ctx.dio("xn", (B_CORE, T, D), F32, producer="s0", consumers=("s2", "s3"))
    t1_d = ctx.dio("t1", (B_CORE, T, D), F32, producer="s2", consumers=("s3",))
    zT_d = ctx.dscratch("s2_zT", (B_CORE, D, T), BF16)
    avn_d = ctx.dscratch("s2_avn", (B_CORE, D, T), BF16)
    zs_d = ctx.dscratch("s2_zs", (B_CORE, 4, T), F32)

    ln1_w = ctx.din("w_L0_ffn_ln1_w", (D,), F32)
    kv_d = ctx.din("w_L0_ffn_kv_d", (L, D), F32)
    q_d = ctx.din("w_L0_ffn_q_d", (L, D), F32)
    k_u = ctx.din("w_L0_ffn_k_u", (D, L), F32)
    q_u = ctx.din("w_L0_ffn_q_u", (D, L), F32)
    v_u = ctx.din("w_L0_ffn_v_u", (D, L), F32)
    rope_q_w = ctx.din("w_L0_ffn_rope_q_w", (512, L), F32)
    rope_k_w = ctx.din("w_L0_ffn_rope_k_w", (512, D), F32)
    o_w = ctx.din("w_L0_ffn_o_w", (D, D), F32)

    with ctx.stage("s2") as S:
        if S is None:
            return
        prep_nat = ctx.sb(S, "prepnat", [128, 8 * 1024], BF16)
        kvdT = emit_load_wT(ctx, S, kv_d, L, D, "kvd", nat_buf=prep_nat)
        qdT = emit_load_wT(ctx, S, q_d, L, D, "qd", nat_buf=prep_nat)
        kuT = emit_load_wT(ctx, S, None, 512, L, "ku", head_sliced=(k_u, 16, 32, 32), nat_buf=prep_nat)
        quT = emit_load_wT(ctx, S, None, 512, L, "qu", head_sliced=(q_u, 16, 32, 32), nat_buf=prep_nat)
        vuT = emit_load_wT(ctx, S, v_u, D, L, "vu", nat_buf=prep_nat)
        rqT = emit_load_wT(ctx, S, rope_q_w, 512, L, "rq", nat_buf=prep_nat)
        rkT = emit_load_wT(ctx, S, rope_k_w, 512, D, "rk", nat_buf=prep_nat)
        owT = emit_load_wT(ctx, S, o_w, D, D, "ow", nat_buf=prep_nat)
        ln1w = ctx.sb(S, "ln1w", [128, 8], F32)
        E.dma("sync", ln1w[:], ln1_w.rearrange("(a p) -> p a", p=128), writes=["ln1w"],
              allow_slow_non_contiguous=True)

        xrow = ctx.sb(S, "s2xr", [128, D], F32)
        znorm = xrow
        sc = make_ln_scratch(ctx, S, "s2")
        stg = [ctx.sb(S, "s2stg", [128, 512], BF16) for _ in range(2)]
        zt = [ctx.sb(S, "s2zt", [128, T], BF16) for _ in range(2)]
        kvlat = [ctx.sb(S, "s2kvl", [128, T], BF16) for _ in range(2)]
        qlat = [ctx.sb(S, "s2ql", [128, T], BF16) for _ in range(2)]
        vsb = [ctx.sb(S, "s2v", [128, 16 * 65], BF16) for _ in range(16)]
        rot_t = [ctx.sb(S, "s2rot", [128, T], BF16) for _ in range(2)]
        keff = [ctx.sb(S, "s2ke", [128, T], BF16) for _ in range(2)]
        qeff = [ctx.sb(S, "s2qe", [128, T], BF16) for _ in range(2)]
        rtmp = ctx.sb(S, "s2rt", [128, T], BF16)
        tmpe = ctx.sb(S, "s2te", [128, T], BF16)
        psb = [ctx.sb(S, "s2p", [128, 1024], BF16) for _ in range(3)]
        avsb = [ctx.sb(S, "s2av", [128, T], BF16) for _ in range(2)]
        zstg = ctx.sb(S, "s2zstg", [1, 512], F32)
        zall = ctx.sb(S, "s2zall", [4, T], F32)
        zbc = ctx.sb(S, "s2zbc", [128, T // 2], F32)
        avst = zt
        sostg = tmpe[:, 0:1024].bitcast(F32)
        t1row = [keff[0][:].bitcast(F32), keff[1][:].bitcast(F32),
                 qeff[0][:].bitcast(F32), qeff[1][:].bitcast(F32)]
        t1alias = ["s2ke0", "s2ke1", "s2qe0", "s2qe1"]

        for b in range(B_CORE):
            if S2_PHASES < 1:
                break
            for tt in range(T // 128):
                p = tt % 2
                kx = f"s2x{p}"
                E.dma("sync", xrow[:], xn_d[b, tile_slice(tt), :], writes=[kx], tag="s2ldx")
                emit_ln_tile(ctx, sc, xrow[:], kx, znorm[:], "s2zn", "s2ln")
                for dt_i in range(8):
                    bk = 6 + dt_i // 4
                    E.op("tensor", lambda e, dt_i=dt_i, bk=bk: e.transpose(
                        ctx.ps[bk][:, (dt_i % 4) * 128:(dt_i % 4 + 1) * 128],
                        znorm[:, tile_slice(dt_i)], ctx.consts["ident_f32"][:]),
                        reads=["s2zn" if znorm is xrow else "s2zn", "s2xr", "c_ident_f32"],
                        writes=[f"ps{bk}"])
                for dt_i in range(8):
                    bk = 6 + dt_i // 4
                    sp = dt_i // 4
                    E.op("scalar", lambda e, dt_i=dt_i, bk=bk, sp=sp: e.activation(
                        stg[sp][:, (dt_i % 4) * 128:(dt_i % 4 + 1) * 128],
                        ctx.ps[bk][:, (dt_i % 4) * 128:(dt_i % 4 + 1) * 128], AF.Identity,
                        scale=ln1w[:, dt_i:dt_i + 1]),
                        reads=[f"ps{bk}", "ln1w"], writes=[f"s2stg{sp}"])
                for sp in range(2):
                    E.dma("sync",
                          zT_d[b, sp * 512:(sp + 1) * 512, tile_slice(tt)]
                          .rearrange("(a p) c -> p a c", p=128),
                          stg[sp][:].rearrange("p (a c) -> p a c", c=128),
                          reads=[f"s2stg{sp}"], writes=[f"zT_{sp}_{tt}"], tag="s2stz")

            zT_keys = [f"zT_{dt_i}_{tt}" for dt_i in range(8) for tt in range(16)]

            if S2_PHASES < 2:
                continue
            lat_n = 0
            for (latbuf, wTb, kw, kl) in ((kvlat, kvdT, "kvd", "kvl"), (qlat, qdT, "qd", "ql")):
                for ct in range(2):
                    for tc in range(4):
                        lat_n += 1
                        if lat_n > S2_LAT_LIMIT:
                            continue
                        for ht in range(8):
                            p = ht % 2
                            E.dma("sync", zt[p][:], zT_d[b, tile_slice(ht), :],
                                  reads=[f"zT_{ht // 4}_{tt}" for tt in range(16)],
                                  writes=[f"s2zt{p}"], tag="s2ldz")
                            E.op("tensor", lambda e, ct=ct, tc=tc, ht=ht, p=p, wTb=wTb: e.matmul(
                                ctx.ps[0][:, :], wTb[:, ht * L + ct * 128: ht * L + (ct + 1) * 128],
                                zt[p][:, tile_slice(tc, 512)], start=(ht == 0), stop=(ht == 7),
                                skip_group_check=True),
                                reads=[f"s2zt{p}", kw], writes=["ps0"], tag="s2lat")
                        E.op("scalar", lambda e, ct=ct, tc=tc, latbuf=latbuf: e.copy(
                            latbuf[ct][:, tile_slice(tc, 512)], ctx.ps[0][:, :]),
                            reads=["ps0"], writes=[kl + str(ct)], tag="s2latev")

            if S2_PHASES < 3:
                continue
            for tp in range(16):
                E.op("vector", lambda e, tp=tp: e.memset(vsb[tp][:], 1.0),
                     writes=[f"v{tp}"], tag="s2vm")
                for ch in range(2):
                    bk = 6 + ch
                    for ct in range(2):
                        E.op("tensor", lambda e, tp=tp, ch=ch, ct=ct, bk=bk: e.matmul(
                            ctx.ps[bk][:, :], kvlat[ct][:, tile_slice(tp)],
                            vuT[:, ct * D + ch * 512: ct * D + (ch + 1) * 512],
                            start=(ct == 0), stop=(ct == 1), skip_group_check=True),
                            reads=["kvl0", "kvl1", "vu"], writes=[f"ps{bk}"], tag="s2vmm")
                    for hh in range(8):
                        h = ch * 8 + hh
                        E.op("scalar", lambda e, tp=tp, h=h, hh=hh, bk=bk: e.copy(
                            vsb[tp][:, h * 65:h * 65 + 64], ctx.ps[bk][:, hh * 64:(hh + 1) * 64]),
                            reads=[f"ps{bk}"], writes=[f"v{tp}"], tag="s2vev")

            if S2_PHASES < 4:
                continue
            for rt in range(4):
                for (dst, which) in ((0, "k"), (1, "q")):
                    if which == "k":
                        for tc in range(4):
                            for ht in range(8):
                                p = ht % 2
                                E.dma("sync", zt[p][:, 0:512], zT_d[b, tile_slice(ht), tile_slice(tc, 512)],
                                      reads=[f"zT_{ht // 4}_{tt}" for tt in range(16)],
                                      writes=[f"s2zt{p}"], tag="s2ldz2")
                                E.op("tensor", lambda e, tc=tc, ht=ht, p=p, rt=rt: e.matmul(
                                    ctx.ps[7][:, :], rkT[:, ht * 512 + rt * 128: ht * 512 + (rt + 1) * 128],
                                    zt[p][:, 0:512], start=(ht == 0), stop=(ht == 7),
                                    skip_group_check=True),
                                    reads=[f"s2zt{p}", "rk"], writes=["ps7"], tag="s2krmm")
                            E.op("scalar", lambda e, tc=tc: e.copy(
                                tmpe[:, tile_slice(tc, 512)], ctx.ps[7][:, :]),
                                reads=["ps7"], writes=["s2te"], tag="s2krev")
                    else:
                        for tc in range(4):
                            for ct in range(2):
                                E.op("tensor", lambda e, tc=tc, ct=ct, rt=rt: e.matmul(
                                    ctx.ps[7][:, :], rqT[:, ct * 512 + rt * 128: ct * 512 + (rt + 1) * 128],
                                    qlat[ct][:, tile_slice(tc, 512)], start=(ct == 0), stop=(ct == 1),
                                    skip_group_check=True),
                                    reads=["ql0", "ql1", "rq"], writes=["ps7"], tag="s2qrmm")
                            E.op("scalar", lambda e, tc=tc: e.copy(
                                tmpe[:, tile_slice(tc, 512)], ctx.ps[7][:, :]),
                                reads=["ps7"], writes=["s2te"], tag="s2qrev")
                    emit_rope_pair(ctx, S, rot_t[dst], f"s2rot{dst}", tmpe, "s2te",
                                   "rot32_mt", "cos32", "sin32", rtmp, ctx.ps[6], "ps6", f"s2r{which}")
                    lat = kvlat if which == "k" else qlat
                    lwT = kuT if which == "k" else quT
                    lk = ("kvl0", "kvl1", "ku") if which == "k" else ("ql0", "ql1", "qu")
                    effarr = keff if which == "k" else qeff
                    keffpfx = "s2ke" if which == "k" else "s2qe"
                    for tc in range(4):
                        for ct in range(2):
                            E.op("tensor", lambda e, tc=tc, ct=ct, rt=rt, lat=lat, lwT=lwT: e.matmul(
                                ctx.ps[7][:, :], lwT[:, ct * 512 + rt * 128: ct * 512 + (rt + 1) * 128],
                                lat[ct][:, tile_slice(tc, 512)], start=(ct == 0), stop=(ct == 1),
                                skip_group_check=True),
                                reads=list(lk), writes=["ps7"], tag="s2upmm")
                        for hh in range(4):
                            E.op("scalar", lambda e, tc=tc, hh=hh, effarr=effarr: e.copy(
                                effarr[hh // 2][(hh % 2) * 64 + 32:(hh % 2) * 64 + 64, tile_slice(tc, 512)],
                                ctx.ps[7][hh * 32:(hh + 1) * 32, :]),
                                reads=["ps7"], writes=[keffpfx + str(hh // 2)], tag="s2upev")

                for pp in range(2):
                    for (effbuf, keffk, srcrot, krot) in (
                            (keff[pp], f"s2ke{pp}", rot_t[0], "s2rot0"),
                            (qeff[pp], f"s2qe{pp}", rot_t[1], "s2rot1")):
                        for hh2 in range(2):
                            hh = pp * 2 + hh2
                            E.op("vector", lambda e, effbuf=effbuf, srcrot=srcrot, hh=hh, hh2=hh2: e.tensor_copy(
                                effbuf[hh2 * 64:hh2 * 64 + 32, :], srcrot[hh * 32:(hh + 1) * 32, :]),
                                reads=[krot], writes=[keffk], tag="s2asm")

                PBMAP = ((0, 1), (2, 3), (6, 7))
                for pp in range(2):
                    for qq in range(4):
                        kk_max = qq * 4 + 3
                        npairs = (kk_max + 1) // 2
                        for kp in range(npairs):
                            for hh2 in range(2):
                                h = rt * 4 + pp * 2 + hh2
                                r0 = hh2 * 64
                                avbank = 4 + hh2
                                pidx = (kp * 2 + hh2) % 3
                                b0, b1 = PBMAP[pidx]
                                for half in range(2):
                                    kk = kp * 2 + half
                                    bb = (b0, b1)[half]
                                    E.op("tensor", lambda e, kk=kk, qq=qq, r0=r0, bb=bb, pp=pp: e.matmul(
                                        ctx.ps[bb][:, :], keff[pp][r0:r0 + 64, tile_slice(kk)],
                                        qeff[pp][r0:r0 + 64, tile_slice(qq, 512)], start=True, stop=True,
                                        skip_group_check=True),
                                        reads=[f"s2ke{pp}", f"s2qe{pp}"], writes=[f"ps{bb}"], tag="s2smm")
                                E.op("scalar", lambda e, b0=b0, pidx=pidx: e.activation(
                                    psb[pidx][:], ctx.ps_all[:, b0 * 512:(b0 + 2) * 512], AF.Exp, scale=0.125),
                                    reads=[f"ps{b0}", f"ps{b0 + 1}"], writes=[f"p{pidx}"], tag="s2exp")
                                for half in range(2):
                                    kk = kp * 2 + half
                                    if kk >= qq * 4:
                                        delta = kk * 128 - qq * 512
                                        E.op("vector", lambda e, pidx=pidx, half=half, delta=delta: e.tensor_tensor(
                                            psb[pidx][:, half * 512:(half + 1) * 512],
                                            psb[pidx][:, half * 512:(half + 1) * 512],
                                            ctx.consts["stair"][:, 512 - delta:1024 - delta], ALU.mult),
                                            reads=[f"p{pidx}", "c_stair"], writes=[f"p{pidx}"], tag="s2msk")
                                for half in range(2):
                                    kk = kp * 2 + half
                                    E.op("tensor", lambda e, kk=kk, h=h, pidx=pidx, half=half, avbank=avbank, kk_max=kk_max: e.matmul(
                                        ctx.ps[avbank][0:65, :], vsb[kk][:, h * 65:(h + 1) * 65],
                                        psb[pidx][:, half * 512:(half + 1) * 512],
                                        start=(kk == 0), stop=(kk == kk_max),
                                        skip_group_check=True),
                                        reads=[f"p{pidx}", f"v{kk}"], writes=[f"ps{avbank}"], tag="s2avmm")
                        for hh2 in range(2):
                            h = rt * 4 + pp * 2 + hh2
                            r0 = hh2 * 64
                            avbank = 4 + hh2
                            E.op("scalar", lambda e, qq=qq, r0=r0, pp=pp, avbank=avbank: e.copy(
                                avsb[pp][r0:r0 + 64, tile_slice(qq, 512)], ctx.ps[avbank][0:64, :]),
                                reads=[f"ps{avbank}"], writes=[f"s2avb{pp}"], tag="s2avev")
                            E.op("scalar", lambda e, avbank=avbank: e.copy(
                                zstg[:, :], ctx.ps[avbank][64:65, :]),
                                reads=[f"ps{avbank}"], writes=["zstg"], tag="s2zev")
                            hloc = pp * 2 + hh2
                            E.dma("sync", zs_d[b, hloc, tile_slice(qq, 512)], zstg[:, :],
                                  reads=["zstg"], writes=[f"zs{hloc}{qq}"], tag="s2zst")

                E.dma("sync", zall[:], zs_d[b],
                      reads=[f"zs{hl}{qq}" for hl in range(4) for qq in range(4)],
                      writes=["zall"], tag="s2zld")
                E.op("vector", lambda e: e.reciprocal(zall[:], zall[:]), reads=["zall"], writes=["zall"])
                E.dma("sync", zs_d[b], zall[:], reads=["zall"], writes=["zrb"], tag="s2zrst")
                for pp in range(2):
                    for hh2 in range(2):
                        hloc = pp * 2 + hh2
                        for th in range(2):
                            tsl = slice(th * (T // 2), (th + 1) * (T // 2))
                            E.dma("sync", zbc[hh2 * 64:(hh2 + 1) * 64, :],
                                  zs_d[b, hloc:hloc + 1, tsl].broadcast_to([64, T // 2]),
                                  reads=["zrb"], writes=["zbc"], tag="s2zbc")
                            E.op("vector", lambda e, pp=pp, hh2=hh2, tsl=tsl: e.tensor_tensor(
                                avsb[pp][hh2 * 64:(hh2 + 1) * 64, tsl],
                                avsb[pp][hh2 * 64:(hh2 + 1) * 64, tsl],
                                zbc[hh2 * 64:(hh2 + 1) * 64, :], ALU.mult),
                                reads=[f"s2avb{pp}", "zbc"], writes=[f"s2avb{pp}"], tag="s2nrm")
                    pair_global = rt * 2 + pp
                    E.dma("sync", avn_d[b, tile_slice(pair_global), :], avsb[pp][:],
                          reads=[f"s2avb{pp}"], writes=[f"avn{pair_global}"], tag="s2avst")

            if S2_PHASES < 5:
                continue
            for tc in range(4):
                for ot in range(8):
                    for ht in range(8):
                        p = ht % 2
                        E.dma("sync", avst[p][:], avn_d[b, tile_slice(ht), :],
                              reads=[f"avn{ht}"], writes=[f"s2avs{p}"], tag="s2ldav")
                        E.op("tensor", lambda e, tc=tc, ot=ot, ht=ht, p=p: e.matmul(
                            ctx.ps[7][:, :], owT[:, ht * D + ot * 128: ht * D + (ot + 1) * 128],
                            avst[p][:, tile_slice(tc, 512)], start=(ht == 0), stop=(ht == 7),
                            skip_group_check=True),
                            reads=[f"s2avs{p}", "ow"], writes=["ps7"], tag="s2omm")
                    E.op("scalar", lambda e: e.copy(sostg, ctx.ps[7][:, :]),
                         reads=["ps7"], writes=["s2te"], tag="s2oev")
                    for t2 in range(4):
                        tr = tc * 4 + t2
                        if ot == 0:
                            E.dma("sync", t1row[t2], xn_d[b, tile_slice(tr), :],
                                  writes=[f"t1r{t2}", t1alias[t2]], tag="s2ldxn")
                        E.op("tensor", lambda e, t2=t2: e.transpose(
                            ctx.ps[6][:, t2 * 128:(t2 + 1) * 128], sostg[:, tile_slice(t2)],
                            ctx.consts["ident_f32"][:]),
                            reads=["s2te", "c_ident_f32"], writes=["ps6"], tag="s2otr")
                    for t2 in range(4):
                        tr = tc * 4 + t2
                        osl = slice(ot * 128, (ot + 1) * 128)
                        E.op("vector", lambda e, t2=t2, osl=osl: e.tensor_tensor(
                            t1row[t2][:, osl], t1row[t2][:, osl],
                            ctx.ps[6][:, t2 * 128:(t2 + 1) * 128], ALU.add),
                            reads=["ps6", f"t1r{t2}"], writes=[f"t1r{t2}", t1alias[t2]], tag="s2res")
                        if ot == 7:
                            E.dma("sync", t1_d[b, tile_slice(tr), :], t1row[t2],
                                  reads=[f"t1r{t2}"], writes=[f"d_t1_{b}_{tr}"], tag="s2stt1")



def emit_s3(ctx):
    E = ctx.E
    I_ = 2048
    t1_d = ctx.dio("t1", (B_CORE, T, D), F32, producer="s2", consumers=("s3",))
    t2_d = ctx.dio("t2", (B_CORE, T, D), F32, producer="s3", consumers=("s4",))
    wgtT_d = ctx.dscratch("s3_wgtT", (B_CORE, 7, T), F32)

    ln2_w = ctx.din("w_L0_ffn_ln2_w", (D,), F32)
    router_w = ctx.din("w_L0_ffn_router_w", (7, D), F32)
    routing_bias = ctx.din("w_L0_ffn_routing_bias", (7,), F32)
    sh_g = ctx.din("w_L0_ffn_shared_gate", (1, I_, D), F32)
    sh_u = ctx.din("w_L0_ffn_shared_up", (1, I_, D), F32)
    sh_d = ctx.din("w_L0_ffn_shared_down", (1, D, I_), F32)
    ro_g = ctx.din("w_L0_ffn_routed_gate", (7, I_, D), F32)
    ro_u = ctx.din("w_L0_ffn_routed_up", (7, I_, D), F32)
    ro_d = ctx.din("w_L0_ffn_routed_down", (7, D, I_), F32)

    with ctx.stage("s3") as S:
        if S is None:
            return
        ln2w = ctx.sb(S, "ln2w", [128, 8], F32)
        E.dma("sync", ln2w[:], ln2_w.rearrange("(a p) -> p a", p=128), writes=["ln2w"],
              allow_slow_non_contiguous=True)
        rb = ctx.sb(S, "rb", [7, 1], F32)
        E.dma("sync", rb[:], routing_bias.rearrange("(a p) -> p a", p=7), writes=["rb"],
              allow_slow_non_contiguous=True)
        routerF = ctx.sb(S, "routerF", [128, 8 * 7], F32)
        for ht in range(8):
            E.dma("sync", routerF[:, ht * 7:(ht + 1) * 7],
                  router_w[:, tile_slice(ht)].rearrange("e p -> p e"), writes=["routerF"],
                  allow_slow_non_contiguous=True)
        rb_bc = ctx.sb(S, "rb_bc", [128, 7], F32)
        E.dma("sync", rb_bc[:], routing_bias.rearrange("(p a) -> p a", p=1).broadcast_to([128, 7]),
              writes=["rb_bc"])

        xrow = ctx.sb(S, "s3xr", [128, D], F32)
        sc = make_ln_scratch(ctx, "s3") if False else make_ln_scratch(ctx, S, "s3")
        xn2 = [ctx.sb(S, "s3x2", [128, T], BF16) for _ in range(8)]
        xstgT = ctx.sb(S, "s3xT", [128, 8 * 128], F32)
        prb = ctx.sb(S, "s3prb", [128, 7], F32)
        tp1 = ctx.sb(S, "s3tp1", [128, 7], F32)
        tp2 = ctx.sb(S, "s3tp2", [128, 7], F32)
        m1 = ctx.sb(S, "s3m1", [128, 1], F32)
        m2 = ctx.sb(S, "s3m2", [128, 1], F32)
        rden = ctx.sb(S, "s3rd", [128, 1], F32)
        wstg = ctx.sb(S, "s3wstg", [7, 128], F32)
        wgt_bc = ctx.sb(S, "s3wbc", [128, T], F32)
        wg_nat = ctx.sb(S, "s3wgn", [128, D], BF16)
        wgTi = ctx.sb(S, "s3wgT", [128, 2048], BF16)
        wuTi = ctx.sb(S, "s3wuT", [128, 2048], BF16)
        wdT = ctx.sb(S, "s3wdT", [128, 16 * 128], BF16)
        g_sb = ctx.sb(S, "s3g", [128, T], BF16)
        wu_nat = wg_nat
        wd_nat = g_sb
        hst = [ctx.sb(S, "s3h", [128, T], BF16) for _ in range(16)]
        acc = [ctx.sb(S, "s3acc", [128, T], F32) for _ in range(8)]
        t2row = [hst[i][:].bitcast(F32) for i in range(4)]
        t2alias = ["h0", "h1", "h2", "h3"]

        for b in range(B_CORE):
            for tt in range(T // 128):
                E.dma("sync", xrow[:], t1_d[b, tile_slice(tt), :], writes=["s3xr"], tag="s3ldx")
                emit_ln_tile(ctx, sc, xrow[:], "s3xr", xrow[:], "s3xr", "s3ln")
                for dt_i in range(8):
                    E.op("tensor", lambda e, dt_i=dt_i: e.transpose(
                        ctx.ps[6][:, 0:128], xrow[:, tile_slice(dt_i)], ctx.consts["ident_f32"][:]),
                        reads=["s3xr", "c_ident_f32"], writes=["ps6"])
                    E.op("scalar", lambda e, dt_i=dt_i, tt=tt: e.activation(
                        xn2[dt_i][:, tile_slice(tt)], ctx.ps[6][:, 0:128], AF.Identity,
                        scale=ln2w[:, dt_i:dt_i + 1]),
                        reads=["ps6", "ln2w"], writes=[f"x2_{dt_i}"])
                    E.op("scalar", lambda e, dt_i=dt_i: e.activation(
                        xstgT[:, tile_slice(dt_i)], ctx.ps[6][:, 0:128], AF.Identity,
                        scale=ln2w[:, dt_i:dt_i + 1]),
                        reads=["ps6", "ln2w"], writes=["xstgT"])
                for ht in range(8):
                    E.op("tensor", lambda e, ht=ht: e.matmul(
                        ctx.ps[0][:, 0:7], xstgT[:, tile_slice(ht)],
                        routerF[:, ht * 7:(ht + 1) * 7], start=(ht == 0), stop=(ht == 7),
                        skip_group_check=True),
                        reads=["xstgT", "routerF"], writes=["ps0"], tag="s3rmm")
                E.op("vector", lambda e: e.tensor_tensor(
                    prb[:], ctx.ps[0][:, 0:7], rb_bc[:], ALU.add),
                    reads=["ps0", "rb_bc"], writes=["prb"], tag="s3rb")
                E.op("scalar", lambda e: e.activation(
                    prb[:], prb[:], AF.Sigmoid),
                    reads=["prb"], writes=["prb"], tag="s3sig")

                E.op("vector", lambda e: e.tensor_reduce(m1[:].unsqueeze(2), prb[:].unsqueeze(1), AX.X, ALU.max),
                     reads=["prb"], writes=["m1"])
                E.op("vector", lambda e: e.tensor_scalar(tp1[:], prb[:], m1[:], None, ALU.is_ge),
                     reads=["prb", "m1"], writes=["tp1"])
                E.op("vector", lambda e: e.tensor_scalar(tp2[:], tp1[:], -1.0, 1.0, ALU.mult, ALU.add),
                     reads=["tp1"], writes=["tp2"])
                E.op("vector", lambda e: e.tensor_tensor(tp2[:], prb[:], tp2[:], ALU.mult),
                     reads=["prb", "tp2"], writes=["tp2"])
                E.op("vector", lambda e: e.tensor_reduce(m2[:].unsqueeze(2), tp2[:].unsqueeze(1), AX.X, ALU.max),
                     reads=["tp2"], writes=["m2"])
                E.op("vector", lambda e: e.tensor_scalar(tp2[:], tp2[:], m2[:], None, ALU.is_ge),
                     reads=["tp2", "m2"], writes=["tp2"])
                E.op("vector", lambda e: e.tensor_scalar(tp1[:], tp1[:], m1[:], None, ALU.mult),
                     reads=["tp1", "m1"], writes=["tp1"])
                E.op("vector", lambda e: e.scalar_tensor_tensor(
                    tp1[:], tp2[:], m2[:], tp1[:], ALU.mult, ALU.add),
                    reads=["tp1", "tp2", "m2"], writes=["tp1"])
                E.op("vector", lambda e: e.tensor_tensor(rden[:], m1[:], m2[:], ALU.add),
                     reads=["m1", "m2"], writes=["rden"])
                E.op("vector", lambda e: e.reciprocal(rden[:], rden[:]),
                     reads=["rden"], writes=["rden"])
                E.op("vector", lambda e: e.tensor_scalar(tp1[:], tp1[:], rden[:], None, ALU.mult),
                     reads=["tp1", "rden"], writes=["tp1"])
                E.op("tensor", lambda e: e.transpose(
                    ctx.ps[5][0:7, 0:128], tp1[:], ctx.consts["ident_f32"][:]),
                    reads=["tp1", "c_ident_f32"], writes=["ps5"], tag="s3wtr")
                E.op("scalar", lambda e: e.copy(wstg[:], ctx.ps[5][0:7, 0:128]),
                     reads=["ps5"], writes=["wstg"])
                E.dma("sync", wgtT_d[b, :, tile_slice(tt)], wstg[:],
                      reads=["wstg"], writes=[f"wgtT_{tt}"], tag="s3wst")

            for e_i in range(8):
                shared = (e_i == 7)
                if not shared:
                    E.dma("sync", wgt_bc[:],
                          wgtT_d[b, e_i:e_i + 1, :].broadcast_to([128, T]),
                          reads=[f"wgtT_{tt}" for tt in range(16)], writes=["wbc"], tag="s3wbc")
                gsrc = sh_g[0] if shared else ro_g[e_i]
                usrc = sh_u[0] if shared else ro_u[e_i]
                dsrc = sh_d[0] if shared else ro_d[e_i]
                for ii in range(16):
                    tb = 6 - 2 * (ii % 2)
                    wgT_b = wgTi if ii % 2 == 0 else wuTi
                    kwg = "wgTi" if ii % 2 == 0 else "wuTi"
                    E.dma("gpsimd", wg_nat[:], gsrc[tile_slice(ii), :], writes=["wgn"], tag="s3ldg")
                    for ht in range(8):
                        E.op("tensor", lambda e, ht=ht, tb=tb: e.transpose(
                            ctx.psbf6(tb, ht), wg_nat[:, tile_slice(ht)], ctx.consts["ident_bf"][:]),
                            reads=["wgn", "c_ident_bf"], writes=[f"ps{tb}"])
                    E.op("vector", lambda e, tb=tb, wgT_b=wgT_b: e.tensor_copy(
                        wgT_b[:, 0:1024], ctx.ps[tb].bitcast(BF16)),
                        reads=[f"ps{tb}"], writes=[kwg])
                    E.dma("gpsimd", wu_nat[:], usrc[tile_slice(ii), :], writes=["wgn"], tag="s3ldu")
                    for ht in range(8):
                        E.op("tensor", lambda e, ht=ht, tb=tb: e.transpose(
                            ctx.psbf6(tb + 1, ht), wu_nat[:, tile_slice(ht)], ctx.consts["ident_bf"][:]),
                            reads=["wgn", "c_ident_bf"], writes=[f"ps{tb + 1}"])
                    E.op("vector", lambda e, tb=tb, wgT_b=wgT_b: e.tensor_copy(
                        wgT_b[:, 1024:2048], ctx.ps[tb + 1].bitcast(BF16)),
                        reads=[f"ps{tb + 1}"], writes=[kwg])
                    for tc in range(4):
                        gb = tc % 2
                        ub = 2 + tc % 2
                        for ht in range(8):
                            E.op("tensor", lambda e, tc=tc, ht=ht, gb=gb, wgT_b=wgT_b: e.matmul(
                                ctx.ps[gb][:, :], wgT_b[:, tile_slice(ht)],
                                xn2[ht][:, tile_slice(tc, 512)], start=(ht == 0), stop=(ht == 7),
                                skip_group_check=True),
                                reads=[kwg, f"x2_{ht}"], writes=[f"ps{gb}"], tag="s3gmm")
                            E.op("tensor", lambda e, tc=tc, ht=ht, ub=ub, wgT_b=wgT_b: e.matmul(
                                ctx.ps[ub][:, :], wgT_b[:, 1024 + ht * 128:1024 + (ht + 1) * 128],
                                xn2[ht][:, tile_slice(tc, 512)], start=(ht == 0), stop=(ht == 7),
                                skip_group_check=True),
                                reads=[kwg, f"x2_{ht}"], writes=[f"ps{ub}"], tag="s3umm")
                        E.op("scalar", lambda e, tc=tc, gb=gb: e.activation(
                            g_sb[:, tile_slice(tc, 512)], ctx.ps[gb][:, :], AF.Silu),
                            reads=[f"ps{gb}"], writes=["g_sb"], tag="s3silu")
                        E.op("vector", lambda e, ii=ii, tc=tc, ub=ub: e.tensor_tensor(
                            hst[ii][:, tile_slice(tc, 512)], g_sb[:, tile_slice(tc, 512)],
                            ctx.ps[ub][:, :], ALU.mult),
                            reads=["g_sb", f"ps{ub}"], writes=[f"h{ii}"], tag="s3gu")
                    if not shared:
                        E.op("vector", lambda e, ii=ii: e.tensor_tensor(
                            hst[ii][:], hst[ii][:], wgt_bc[:], ALU.mult),
                            reads=[f"h{ii}", "wbc"], writes=[f"h{ii}"], tag="s3hw")
                for ot in range(8):
                    E.dma("gpsimd", wd_nat[:], dsrc[tile_slice(ot), :], writes=["g_sb"], tag="s3ldd")
                    for ii in range(16):
                        bk = 2 + ii // 8
                        E.op("tensor", lambda e, ii=ii, bk=bk: e.transpose(
                            ctx.psbf6(bk, ii % 8), wd_nat[:, tile_slice(ii)], ctx.consts["ident_bf"][:]),
                            reads=["g_sb", "c_ident_bf"], writes=[f"ps{bk}"])
                    for bk in (2, 3):
                        E.op("vector", lambda e, bk=bk: e.tensor_copy(
                            wdT[:, (bk - 2) * 1024:(bk - 1) * 1024], ctx.ps[bk].bitcast(BF16)),
                            reads=[f"ps{bk}"], writes=["wdT"])
                    for tc in range(4):
                        bank = tc % 2
                        for ii in range(16):
                            E.op("tensor", lambda e, tc=tc, ii=ii, bank=bank: e.matmul(
                                ctx.ps[bank][:, :], wdT[:, tile_slice(ii)],
                                hst[ii][:, tile_slice(tc, 512)], start=(ii == 0), stop=(ii == 15),
                                skip_group_check=True),
                                reads=["wdT", f"h{ii}"], writes=[f"ps{bank}"], tag="s3dmm")
                        if e_i == 0:
                            E.op("scalar", lambda e, ot=ot, tc=tc, bank=bank: e.copy(
                                acc[ot][:, tile_slice(tc, 512)], ctx.ps[bank][:, :]),
                                reads=[f"ps{bank}"], writes=[f"acc{ot}"], tag="s3acc0")
                        else:
                            E.op("vector", lambda e, ot=ot, tc=tc, bank=bank: e.tensor_tensor(
                                acc[ot][:, tile_slice(tc, 512)], acc[ot][:, tile_slice(tc, 512)],
                                ctx.ps[bank][:, :], ALU.add),
                                reads=[f"ps{bank}", f"acc{ot}"], writes=[f"acc{ot}"], tag="s3acc")

            for tc in range(4):
                for ot in range(8):
                    bk = 4 + ot % 2
                    for t2_ in range(4):
                        tr = tc * 4 + t2_
                        if ot == 0:
                            E.dma("sync", t2row[t2_], t1_d[b, tile_slice(tr), :],
                                  writes=[f"t2r{t2_}", t2alias[t2_]], tag="s3ldt1")
                        E.op("tensor", lambda e, ot=ot, tc=tc, t2_=t2_, bk=bk: e.transpose(
                            ctx.ps[bk][:, t2_ * 128:(t2_ + 1) * 128],
                            acc[ot][:, tc * 512 + t2_ * 128: tc * 512 + (t2_ + 1) * 128],
                            ctx.consts["ident_f32"][:]),
                            reads=[f"acc{ot}", "c_ident_f32"], writes=[f"ps{bk}"], tag="s3ttr")
                    for t2_ in range(4):
                        tr = tc * 4 + t2_
                        E.op("vector", lambda e, ot=ot, t2_=t2_, bk=bk: e.tensor_tensor(
                            t2row[t2_][:, tile_slice(ot)], t2row[t2_][:, tile_slice(ot)],
                            ctx.ps[bk][:, t2_ * 128:(t2_ + 1) * 128], ALU.add),
                            reads=[f"ps{bk}", f"t2r{t2_}"], writes=[f"t2r{t2_}", t2alias[t2_]], tag="s3res")
                        if ot == 7:
                            E.dma("sync", t2_d[b, tile_slice(tr), :], t2row[t2_],
                                  reads=[f"t2r{t2_}"], writes=[f"d_t2_{b}_{tr}"], tag="s3stt2")



def emit_s4(ctx):
    E = ctx.E
    x_d = ctx.din("x", (B_CORE, T, D), F32)
    s0_d = ctx.dio("s0", (B_CORE, T, D), F32, producer="s1", consumers=("s4",))
    t2_d = ctx.dio("t2", (B_CORE, T, D), F32, producer="s3", consumers=("s4",))
    x1_d = ctx.dio("x1", (B_CORE, T, D), F32, producer="s4", consumers=("s7",))
    xn1_d = ctx.dio("xn1", (B_CORE, T, D), F32, producer="s4", consumers=("s6",))
    xn1T_d = ctx.dio("xn1T", (B_CORE, D, T), BF16, producer="s4", consumers=("s5",))

    with ctx.stage("s4") as S:
        if S is None:
            return
        xt = ctx.sb(S, "s4x", [128, D], F32)
        st_ = ctx.sb(S, "s4s", [128, D], F32)
        gt = ctx.sb(S, "s4g", [128, D], F32)
        stg = [ctx.sb(S, "s4stg", [128, 512], BF16) for _ in range(2)]
        sc = make_ln_scratch(ctx, S, "s4")

        for b in range(B_CORE):
            for tt in range(T // 128):
                E.dma("sync", xt[:], x_d[b, tile_slice(tt), :], writes=["s4x"], tag="s4ldx")
                E.dma("sync", st_[:], s0_d[b, tile_slice(tt), :], writes=["s4s"], tag="s4lds")
                E.dma("sync", gt[:], t2_d[b, tile_slice(tt), :], writes=["s4g"], tag="s4ldt")
                E.op("scalar", lambda e: e.activation(gt[:], gt[:], AF.Silu),
                     reads=["s4g"], writes=["s4g"], tag="s4silu")
                E.op("vector", lambda e: e.tensor_tensor(gt[:], gt[:], st_[:], ALU.mult),
                     reads=["s4g", "s4s"], writes=["s4g"], tag="s4sg")
                E.op("vector", lambda e: e.tensor_tensor(xt[:], xt[:], gt[:], ALU.add),
                     reads=["s4x", "s4g"], writes=["s4x"], tag="s4add")
                E.dma("sync", x1_d[b, tile_slice(tt), :], xt[:],
                      reads=["s4x"], writes=[f"d_x1_{b}_{tt}"], tag="s4stx1")
                emit_ln_tile(ctx, sc, xt[:], "s4x", gt[:], "s4n", "s4ln")
                E.dma("sync", xn1_d[b, tile_slice(tt), :], gt[:],
                      reads=["s4n"], writes=[f"d_xn1_{b}_{tt}"], tag="s4stn")
                for dt_i in range(8):
                    bk = 6 + dt_i // 4
                    E.op("tensor", lambda e, dt_i=dt_i, bk=bk: e.transpose(
                        ctx.ps[bk][:, (dt_i % 4) * 128:(dt_i % 4 + 1) * 128],
                        gt[:, tile_slice(dt_i)], ctx.consts["ident_f32"][:]),
                        reads=["s4n", "c_ident_f32"], writes=[f"ps{bk}"])
                for gg in range(2):
                    bk = 6 + gg
                    E.op("scalar", lambda e, gg=gg, bk=bk: e.copy(stg[gg][:], ctx.ps[bk][:, :]),
                         reads=[f"ps{bk}"], writes=[f"s4stg{gg}"])
                    E.dma("sync",
                          xn1T_d[b, gg * 512:(gg + 1) * 512, tile_slice(tt)]
                          .rearrange("(a p) c -> p a c", p=128),
                          stg[gg][:].rearrange("p (a c) -> p a c", c=128),
                          reads=[f"s4stg{gg}"], writes=[f"d_xn1T_{b}_{gg}_{tt}"], tag="s4stT")



def emit_s5(ctx):
    E = ctx.E
    DI = 2048
    xn1T_d = ctx.dio("xn1T", (B_CORE, D, T), BF16, producer="s4", consumers=("s5",))
    s1_d = ctx.dio("s1", (B_CORE, T, D), F32, producer="s5", consumers=("s7",))
    xbc_d = ctx.dscratch("s5_xbc", (B_CORE, DI, T), BF16)
    z_d = ctx.dscratch("s5_z", (B_CORE, DI, T), BF16)
    dt_d = ctx.dscratch("s5_dt", (B_CORE, DI, T), F32)
    dtu_d = ctx.dscratch("s5_dtu", (B_CORE, DI, T), BF16)
    bc_d = ctx.dscratch("s5_bc", (B_CORE, 32, T), BF16)
    yg_d = ctx.dscratch("s5_yg", (B_CORE, DI, T), BF16)

    w_ip = ctx.din("w_L1_attn_in_proj_w", (D, 2 * DI), F32)
    w_cv = ctx.din("w_L1_attn_conv_w", (DI, 4), F32)
    b_cv = ctx.din("w_L1_attn_conv_b", (DI,), F32)
    w_xp = ctx.din("w_L1_attn_x_proj_w", (DI, 96), F32)
    w_dt = ctx.din("w_L1_attn_dt_proj_w", (64, DI), F32)
    b_dt = ctx.din("w_L1_attn_dt_proj_b", (DI,), F32)
    a_log = ctx.din("w_L1_attn_A_log", (DI, 16), F32)
    d_par = ctx.din("w_L1_attn_D_param", (DI,), F32)
    w_op = ctx.din("w_L1_attn_out_proj_w", (DI, D), F32)

    with ctx.stage("s5") as S:
        if S is None:
            return
        cvw = ctx.sb(S, "cvw", [128, 16 * 4], F32)
        E.dma("sync", cvw[:].rearrange("p (d c) -> p d c", d=16),
              w_cv.rearrange("(d p) c -> p d c", p=128), writes=["cvw"])
        cvb = ctx.sb(S, "cvb", [128, 16], F32)
        E.dma("sync", cvb[:], b_cv.rearrange("(d p) -> p d", p=128), writes=["cvb"],
              allow_slow_non_contiguous=True)
        dtb = ctx.sb(S, "dtb", [128, 16], F32)
        E.dma("sync", dtb[:], b_dt.rearrange("(d p) -> p d", p=128), writes=["dtb"],
              allow_slow_non_contiguous=True)
        dpar = ctx.sb(S, "dpar", [128, 16], F32)
        E.dma("sync", dpar[:], d_par.rearrange("(d p) -> p d", p=128), writes=["dpar"],
              allow_slow_non_contiguous=True)
        asb = ctx.sb(S, "asb", [128, 256], F32)
        E.dma("sync", asb[:].rearrange("p (d n) -> p d n", d=16),
              a_log.rearrange("(d p) n -> p d n", p=128), writes=["asb"])
        E.op("scalar", lambda e: e.activation(asb[:], asb[:], AF.Exp), reads=["asb"], writes=["asb"])
        E.op("vector", lambda e: e.tensor_scalar(asb[:], asb[:], -1.0, None, ALU.mult),
             reads=["asb"], writes=["asb"])
        xpw = ctx.sb(S, "xpw", [128, 16 * 96], BF16)
        E.dma("gpsimd", xpw[:].rearrange("p (d c) -> p d c", d=16),
              w_xp.rearrange("(d p) c -> p d c", p=128), writes=["xpw"])
        dtw = ctx.sb(S, "dtw", [64, DI], BF16)
        E.dma("gpsimd", dtw[:], w_dt[:, :], writes=["dtw"])

        xnt8 = [ctx.sb(S, "s5xnt", [128, T], BF16) for _ in range(8)]
        wip = [ctx.sb(S, "s5wip", [128, 8 * 128], BF16) for _ in range(2)]
        xbrow = ctx.sb(S, "s5xbr", [128, T + 4], BF16)
        cacc = ctx.sb(S, "s5cacc", [128, T], F32)
        xbc = ctx.sb(S, "s5xbc", [128, T], BF16)
        dtraw = ctx.sb(S, "s5dtr", [64, T], BF16)
        bcsb = ctx.sb(S, "s5bc", [32, T], BF16)
        etmp = ctx.sb(S, "s5et", [128, 512], F32)
        dts = ctx.sb(S, "s5dts", [128, 512], F32)
        dtu = ctx.sb(S, "s5dtu", [128, 512], BF16)
        xbcc = ctx.sb(S, "s5xbcc", [128, 512], BF16)
        SEG = 1024
        dtt = [ctx.sb(S, "s5dtt", [128, SEG], F32) for _ in range(2)]
        dut = [ctx.sb(S, "s5dut", [128, SEG], BF16) for _ in range(2)]
        bbc = [ctx.sb(S, "s5bbc", [128, SEG], BF16) for _ in range(4)]
        cbc = [ctx.sb(S, "s5cbc", [128, SEG], BF16) for _ in range(4)]
        da = [ctx.sb(S, "s5da", [128, SEG], F32) for _ in range(4)]
        dbu = [ctx.sb(S, "s5dbu", [128, SEG], BF16) for _ in range(4)]
        hbuf = [ctx.sb(S, "s5h", [128, SEG], BF16) for _ in range(4)]
        prods = [ctx.sb(S, "s5pr", [128, SEG], BF16) for _ in range(4)]
        gld = [ctx.sb(S, "s5gl", [128, SEG], BF16) for _ in range(2)]
        state = ctx.sb(S, "s5st", [128, 256], F32)
        zt_s = ctx.sb(S, "s5zt", [128, SEG], F32)
        ygt = ctx.sb(S, "s5yg", [128, SEG], BF16)
        ygl = [ctx.sb(S, "s5ygl", [128, 512], BF16) for _ in range(16)]
        sost = ctx.sb(S, "s5so", [128, 512], F32)
        trst = [ctx.sb(S, "s5tr", [128, 512], F32) for _ in range(2)]

        for b in range(B_CORE):
            for ht in range(8):
                E.dma("sync", xnt8[ht][:], xn1T_d[b, tile_slice(ht), :],
                      writes=[f"s5xnt{ht}"], tag="s5ldx")

            for dt_i in range(16):
                E.dma("gpsimd", wip[dt_i % 2][:].rearrange("p (h c) -> p h c", h=8),
                      w_ip[:, tile_slice(dt_i)].rearrange("(h p) c -> p h c", p=128),
                      writes=[f"wip{dt_i % 2}"], tag="s5ldw")
                for tc in range(4):
                    bank = tc % 2
                    for ht in range(8):
                        E.op("tensor", lambda e, dt_i=dt_i, tc=tc, ht=ht, bank=bank: e.matmul(
                            ctx.ps[bank][:, :], wip[dt_i % 2][:, tile_slice(ht)],
                            xnt8[ht][:, tile_slice(tc, 512)], start=(ht == 0), stop=(ht == 7),
                            skip_group_check=True),
                            reads=[f"s5xnt{ht}", f"wip{dt_i % 2}"], writes=[f"ps{bank}"], tag="s5ipmm")
                    E.op("scalar", lambda e, tc=tc, bank=bank: e.copy(
                        xbrow[:, 4 + tc * 512: 4 + (tc + 1) * 512], ctx.ps[bank][:, :]),
                        reads=[f"ps{bank}"], writes=["xbrow"], tag="s5ipev")
                E.op("vector", lambda e: e.memset(xbrow[:, 0:4], 0.0), writes=["xbrow"])
                E.op("vector", lambda e, dt_i=dt_i: e.tensor_scalar(
                    cacc[:], xbrow[:, 1:1 + T], cvw[:, dt_i * 4:dt_i * 4 + 1], None, ALU.mult),
                    reads=["xbrow", "cvw"], writes=["cacc"], tag="s5cv0")
                for j in range(1, 4):
                    E.op("vector", lambda e, j=j, dt_i=dt_i: e.scalar_tensor_tensor(
                        cacc[:], xbrow[:, 1 + j:1 + j + T], cvw[:, dt_i * 4 + j:dt_i * 4 + j + 1],
                        cacc[:], ALU.mult, ALU.add),
                        reads=["xbrow", "cvw", "cacc"], writes=["cacc"], tag="s5cv")
                E.op("scalar", lambda e, dt_i=dt_i: e.activation(
                    xbc[:], cacc[:], AF.Silu, bias=cvb[:, dt_i:dt_i + 1]),
                    reads=["cacc", "cvb"], writes=["xbc"], tag="s5cvs")
                E.dma("sync", xbc_d[b, tile_slice(dt_i), :], xbc[:],
                      reads=["xbc"], writes=[f"xbc_{dt_i}"], tag="s5stxbc")
                for tc in range(4):
                    E.op("tensor", lambda e, dt_i=dt_i, tc=tc: e.matmul(
                        ctx.ps[2 + tc][0:96, :], xpw[:, dt_i * 96:(dt_i + 1) * 96],
                        xbc[:, tile_slice(tc, 512)], start=(dt_i == 0), stop=(dt_i == 15),
                        skip_group_check=True),
                        reads=["xbc", "xpw"], writes=[f"ps{2 + tc}"], tag="s5xpmm")
            for dt_i in range(16):
                E.dma("gpsimd", wip[dt_i % 2][:].rearrange("p (h c) -> p h c", h=8),
                      w_ip[:, 2048 + dt_i * 128: 2048 + (dt_i + 1) * 128].rearrange("(h p) c -> p h c", p=128),
                      writes=[f"wip{dt_i % 2}"], tag="s5ldwz")
                for tc in range(4):
                    bank = tc % 2
                    for ht in range(8):
                        E.op("tensor", lambda e, tc=tc, ht=ht, bank=bank, dt_i=dt_i: e.matmul(
                            ctx.ps[bank][:, :], wip[dt_i % 2][:, tile_slice(ht)],
                            xnt8[ht][:, tile_slice(tc, 512)], start=(ht == 0), stop=(ht == 7),
                            skip_group_check=True),
                            reads=[f"s5xnt{ht}", f"wip{dt_i % 2}"], writes=[f"ps{bank}"], tag="s5zmm")
                    E.op("scalar", lambda e, tc=tc, bank=bank: e.copy(
                        xbc[:, tile_slice(tc, 512)], ctx.ps[bank][:, :]),
                        reads=[f"ps{bank}"], writes=["xbc"], tag="s5zev")
                E.dma("sync", z_d[b, tile_slice(dt_i), :], xbc[:],
                      reads=["xbc"], writes=[f"z_{dt_i}"], tag="s5stz")
            for tc in range(4):
                E.op("scalar", lambda e, tc=tc: e.copy(
                    dtraw[:, tile_slice(tc, 512)], ctx.ps[2 + tc][0:64, :]),
                    reads=[f"ps{2 + tc}"], writes=["dtraw"], tag="s5xpev")
                E.op("scalar", lambda e, tc=tc: e.copy(
                    bcsb[:, tile_slice(tc, 512)], ctx.ps[2 + tc][64:96, :]),
                    reads=[f"ps{2 + tc}"], writes=["bcsb"], tag="s5bcev")
            E.dma("sync", bc_d[b], bcsb[:], reads=["bcsb"], writes=["bc_dram"], tag="s5stbc")

            for dt_i in range(16):
                for tc in range(4):
                    E.op("tensor", lambda e, dt_i=dt_i, tc=tc: e.matmul(
                        ctx.ps[0][:, :], dtw[:, tile_slice(dt_i)],
                        dtraw[:, tile_slice(tc, 512)], start=True, stop=True,
                        skip_group_check=True),
                        reads=["dtraw", "dtw"], writes=["ps0"], tag="s5dtmm")
                    E.op("scalar", lambda e, dt_i=dt_i: e.activation(
                        etmp[:], ctx.ps[0][:, :], AF.Exp, bias=dtb[:, dt_i:dt_i + 1]),
                        reads=["ps0", "dtb"], writes=["etmp"], tag="s5sp1")
                    one_t, kone = ctx.sconst(1.0)
                    E.op("scalar", lambda e: e.activation(
                        dts[:], etmp[:], AF.Ln, bias=one_t[:, :]),
                        reads=["etmp", kone], writes=["dts"], tag="s5sp2")
                    E.dma("sync", dt_d[b, tile_slice(dt_i), tile_slice(tc, 512)], dts[:],
                          reads=["dts"], writes=[f"dt_{dt_i}"], tag="s5stdt")
                    E.dma("sync", xbcc[:], xbc_d[b, tile_slice(dt_i), tile_slice(tc, 512)],
                          reads=[f"xbc_{dt_i}"], writes=["xbcc"], tag="s5ldxbc")
                    E.op("vector", lambda e: e.tensor_tensor(dtu[:], dts[:], xbcc[:], ALU.mult),
                         reads=["dts", "xbcc"], writes=["dtu"], tag="s5dtu")
                    E.dma("sync", dtu_d[b, tile_slice(dt_i), tile_slice(tc, 512)], dtu[:],
                          reads=["dtu"], writes=[f"dtu_{dt_i}"], tag="s5stdtu")

            E.op("vector", lambda e: e.memset(state[:], 0.0),
                 writes=["state"] + [f"state{i}" for i in range(256)])
            for seg in range(2):
                sgsl = slice(seg * SEG, (seg + 1) * SEG)
                for dtg in range(4):
                    for nblk in range(4):
                        for nl in range(4):
                            n = nblk * 4 + nl
                            E.dma("sync", bbc[nl][:], bc_d[b, n:n + 1, sgsl].broadcast_to([128, SEG]),
                                  reads=["bc_dram"], writes=[f"bbc{nl}"], tag="s5bbc")
                            E.dma("sync", cbc[nl][:], bc_d[b, 16 + n:17 + n, sgsl].broadcast_to([128, SEG]),
                                  reads=["bc_dram"], writes=[f"cbc{nl}"], tag="s5cbc")
                        for dtl in range(4):
                            dt_i = dtg * 4 + dtl
                            pb_ = dt_i % 2
                            E.dma("sync", dtt[pb_][:], dt_d[b, tile_slice(dt_i), sgsl],
                                  reads=[f"dt_{dt_i}"], writes=[f"dtt{pb_}"], tag="s5lddt")
                            E.dma("sync", dut[pb_][:], dtu_d[b, tile_slice(dt_i), sgsl],
                                  reads=[f"dtu_{dt_i}"], writes=[f"dut{pb_}"], tag="s5lddtu")
                            for nl in range(4):
                                n = nblk * 4 + nl
                                pp = n % 4
                                scol = dt_i * 16 + n
                                E.op("scalar", lambda e, pb_=pb_, pp=pp, scol=scol: e.activation(
                                    da[pp][:], dtt[pb_][:], AF.Exp, scale=asb[:, scol:scol + 1]),
                                    reads=[f"dtt{pb_}", "asb"], writes=[f"da{pp}"], tag="s5da")
                                E.op("vector", lambda e, pb_=pb_, pp=pp, nl=nl: e.tensor_tensor(
                                    dbu[pp][:], dut[pb_][:], bbc[nl][:], ALU.mult),
                                    reads=[f"dut{pb_}", f"bbc{nl}"], writes=[f"dbu{pp}"], tag="s5dbu")
                                E.op("vector", lambda e, pp=pp, scol=scol: e.tensor_tensor_scan(
                                    hbuf[pp][:], da[pp][:], dbu[pp][:], state[:, scol:scol + 1],
                                    ALU.mult, ALU.add),
                                    reads=[f"da{pp}", f"dbu{pp}", f"state{scol}"], writes=[f"h{pp}"], tag="s5tts")
                                E.op("vector", lambda e, pp=pp, scol=scol: e.tensor_copy(
                                    state[:, scol:scol + 1], hbuf[pp][:, SEG - 1:SEG]),
                                    reads=[f"h{pp}"], writes=[f"state{scol}"], tag="s5stsv")
                                E.op("vector", lambda e, pp=pp, nl=nl: e.tensor_tensor(
                                    prods[pp][:], hbuf[pp][:], cbc[nl][:], ALU.mult),
                                    reads=[f"h{pp}", f"cbc{nl}"], writes=[f"pr{pp}"], tag="s5prod")
                                first = (nblk == 0 and nl == 0)
                                last = (nblk == 3 and nl == 3)
                                for tc2 in range(2):
                                    bank = dtl * 2 + tc2
                                    E.op("tensor", lambda e, pp=pp, tc2=tc2, bank=bank, first=first, last=last: e.matmul(
                                        ctx.ps[bank][:, :], ctx.consts["ident_bf"][:],
                                        prods[pp][:, tile_slice(tc2, 512)], start=first, stop=last,
                                        skip_group_check=True),
                                        reads=[f"pr{pp}", "c_ident_bf"], writes=[f"ps{bank}"], tag="s5yacc")
                        if nblk == 3:
                            for dtl in range(4):
                                dt_i = dtg * 4 + dtl
                                E.dma("sync", gld[0][:], xbc_d[b, tile_slice(dt_i), sgsl],
                                      reads=[f"xbc_{dt_i}"], writes=["gld0"], tag="s5gxbc")
                                E.dma("sync", gld[1][:], z_d[b, tile_slice(dt_i), sgsl],
                                      reads=[f"z_{dt_i}"], writes=["gld1"], tag="s5gz")
                                E.op("scalar", lambda e: e.activation(zt_s[:], gld[1][:], AF.Silu),
                                     reads=["gld1"], writes=["zt_s"], tag="s5gsilu")
                                E.op("vector", lambda e, dtl=dtl, dt_i=dt_i: e.scalar_tensor_tensor(
                                    zt_s[:] if False else gld[0][:], gld[0][:], dpar[:, dt_i:dt_i + 1],
                                    ctx.ps_all[:, dtl * 1024:(dtl + 1) * 1024], ALU.mult, ALU.add),
                                    reads=["gld0", "dpar", f"ps{dtl * 2}", f"ps{dtl * 2 + 1}"],
                                    writes=["gld0"], tag="s5gyd")
                                E.op("vector", lambda e: e.tensor_tensor(ygt[:], gld[0][:], zt_s[:], ALU.mult),
                                     reads=["gld0", "zt_s"], writes=["ygt"], tag="s5gmul")
                                E.dma("sync", yg_d[b, tile_slice(dt_i), sgsl], ygt[:],
                                      reads=["ygt"], writes=[f"yg_{dt_i}"], tag="s5styg")

            for tc in range(4):
                for dt_i in range(16):
                    E.dma("sync", ygl[dt_i][:], yg_d[b, tile_slice(dt_i), tile_slice(tc, 512)],
                          reads=[f"yg_{dt_i}"], writes=[f"ygl{dt_i}"], tag="s5ldyg")
                for ot in range(8):
                    wop = dtt[ot % 2][:].bitcast(BF16)
                    E.dma("gpsimd", wop.rearrange("p (d c) -> p d c", d=16),
                          w_op[:, tile_slice(ot)].rearrange("(d p) c -> p d c", p=128),
                          writes=[f"dtt{ot % 2}"], tag="s5ldop")
                    for dt_i in range(16):
                        E.op("tensor", lambda e, ot=ot, dt_i=dt_i, tc=tc, wop=wop: e.matmul(
                            ctx.ps[0][:, :], wop[:, tile_slice(dt_i)],
                            ygl[dt_i][:], start=(dt_i == 0), stop=(dt_i == 15),
                            skip_group_check=True),
                            reads=[f"ygl{dt_i}", f"dtt{ot % 2}"], writes=["ps0"], tag="s5opmm")
                    E.op("scalar", lambda e: e.copy(sost[:], ctx.ps[0][:, :]),
                         reads=["ps0"], writes=["sost"], tag="s5opev")
                    for t2 in range(4):
                        E.op("tensor", lambda e, t2=t2: e.transpose(
                            ctx.ps[1][:, t2 * 128:(t2 + 1) * 128], sost[:, tile_slice(t2)],
                            ctx.consts["ident_f32"][:]),
                            reads=["sost", "c_ident_f32"], writes=["ps1"], tag="s5str")
                    pr_ = ot % 2
                    E.op("scalar", lambda e, pr_=pr_: e.copy(trst[pr_][:], ctx.ps[1][:, :]),
                         reads=["ps1"], writes=[f"trst{pr_}"], tag="s5strev")
                    E.dma("sync", s1_d[b, tc * 512:(tc + 1) * 512, tile_slice(ot)]
                          .rearrange("(a p) c -> p a c", p=128),
                          trst[pr_][:].rearrange("p (a c) -> p a c", c=128),
                          reads=[f"trst{pr_}"], writes=[f"d_s1_{b}_{ot}_{tc}"], tag="s5sts1")



FP4_MIDS = (0.25, 0.75, 1.25, 1.75, 2.5, 3.5, 5.0)
FP4_DELT = (0.5, 0.5, 0.5, 0.5, 1.0, 1.0, 2.0)


def emit_fp4_quant(ctx, out_ap, kout, in_ap, kin, width, qs, tag):
    E = ctx.E
    nb = width // 16
    mx, scl, ybuf, acc, term, sgn = qs["mx"], qs["scl"], qs["y"], qs["acc"], qs["term"], qs["sgn"]
    kq = qs["key"]
    E.op("vector", lambda e: e.tensor_reduce(
        mx[:, 0:nb].unsqueeze(2), in_ap.rearrange("p (a b) -> p a b", b=16), AX.X, ALU.max,
        apply_absolute_value=True),
        reads=[kin], writes=[kq + "mx"], tag=tag + "mx")
    E.op("vector", lambda e: e.tensor_scalar(
        scl[:, 0:nb], mx[:, 0:nb], 1.0 / 6.0, 1e-12, ALU.mult, ALU.max),
        reads=[kq + "mx"], writes=[kq + "scl"], tag=tag + "scl")
    E.op("vector", lambda e: e.reciprocal(mx[:, 0:nb], scl[:, 0:nb]),
         reads=[kq + "scl"], writes=[kq + "mx"], tag=tag + "rcp")
    rcp_view = mx[:, 0:nb].unsqueeze(2).broadcast_to([128, nb, 16])
    E.op("scalar", lambda e: e.activation(
        ybuf[:, 0:width], in_ap, AF.Abs),
        reads=[kin], writes=[kq + "y"], tag=tag + "abs")
    E.op("vector", lambda e: e.tensor_tensor(
        ybuf[:, 0:width].rearrange("p (a b) -> p a b", b=16),
        ybuf[:, 0:width].rearrange("p (a b) -> p a b", b=16), rcp_view, ALU.mult),
        reads=[kq + "y", kq + "mx"], writes=[kq + "y"], tag=tag + "y")
    E.op("vector", lambda e: e.tensor_scalar(
        acc[:, 0:width], ybuf[:, 0:width], FP4_MIDS[0], FP4_DELT[0], ALU.is_gt, ALU.mult),
        reads=[kq + "y"], writes=[kq + "acc"], tag=tag + "q0")
    for j in range(1, 7):
        E.op("vector", lambda e, j=j: e.tensor_scalar(
            term[:, 0:width], ybuf[:, 0:width], FP4_MIDS[j], FP4_DELT[j], ALU.is_gt, ALU.mult),
            reads=[kq + "y"], writes=[kq + "t"], tag=tag + f"q{j}")
        E.op("vector", lambda e: e.tensor_tensor(
            acc[:, 0:width], acc[:, 0:width], term[:, 0:width], ALU.add),
            reads=[kq + "acc", kq + "t"], writes=[kq + "acc"], tag=tag + f"a{j}")
    E.op("scalar", lambda e: e.activation(sgn[:, 0:width], in_ap, AF.Sign),
         reads=[kin], writes=[kq + "sgn"], tag=tag + "sgn")
    scl_view = scl[:, 0:nb].unsqueeze(2).broadcast_to([128, nb, 16])
    E.op("vector", lambda e: e.tensor_tensor(
        sgn[:, 0:width].rearrange("p (a b) -> p a b", b=16),
        sgn[:, 0:width].rearrange("p (a b) -> p a b", b=16), scl_view, ALU.mult),
        reads=[kq + "sgn", kq + "scl"], writes=[kq + "sgn"], tag=tag + "ss")
    E.op("vector", lambda e: e.tensor_tensor(
        out_ap, acc[:, 0:width], sgn[:, 0:width], ALU.mult),
        reads=[kq + "acc", kq + "sgn"], writes=[kout], tag=tag + "out")


def make_fp4_scratch(ctx, S, pfx, width):
    return {"mx": ctx.sb(S, pfx + "mx", [128, width // 16], F32),
            "scl": ctx.sb(S, pfx + "scl", [128, width // 16], F32),
            "y": ctx.sb(S, pfx + "y", [128, width], F32),
            "acc": ctx.sb(S, pfx + "acc", [128, width], BF16),
            "term": ctx.sb(S, pfx + "term", [128, width], BF16),
            "sgn": ctx.sb(S, pfx + "sgn", [128, width], BF16),
            "key": pfx}


def emit_s6(ctx):
    E = ctx.E
    I_ = 2816
    NI = I_ // 128
    xn1_d = ctx.dio("xn1", (B_CORE, T, D), F32, producer="s4", consumers=("s6",))
    f1_d = ctx.dio("f1", (B_CORE, T, D), F32, producer="s6", consumers=("s7",))
    wgT_d = ctx.dscratch("s6_wgT", (D, I_), BF16)
    wuT_d = ctx.dscratch("s6_wuT", (D, I_), BF16)
    wdT_d = ctx.dscratch("s6_wdT", (I_, D), BF16)
    hqT_d = ctx.dscratch("s6_hqT", (B_CORE, I_, T), BF16)

    w_g = ctx.din("w_L1_ffn_w_gate", (I_, D), F32)
    w_u = ctx.din("w_L1_ffn_w_up", (I_, D), F32)
    w_d = ctx.din("w_L1_ffn_w_down", (D, I_), F32)

    with ctx.stage("s6") as S:
        if S is None:
            return
        qs2 = [make_fp4_scratch(ctx, S, f"s6q{i}", I_ if i == 0 else 512) for i in range(2)]
        qs = qs2[0]
        wnat = ctx.sb(S, "s6wn", [128, I_], F32)
        wq = ctx.sb(S, "s6wq", [128, I_], BF16)
        piece = [ctx.sb(S, "s6pc", [128, 512], BF16) for _ in range(2)]
        for (wsrc, wdst, nr, ncol, kw) in ((w_g, wgT_d, NI, 8, "wg"), (w_u, wuT_d, NI, 8, "wu"),
                                           (w_d, wdT_d, 8, NI, "wd")):
            width = ncol * 128
            for rc in range(nr):
                E.dma("sync", wnat[:, 0:width], wsrc[tile_slice(rc), :],
                      writes=["s6wn"], tag="s6ldw")
                emit_fp4_quant(ctx, wq[:, 0:width], "s6wq", wnat[:, 0:width], "s6wn",
                               width, qs, "s6qw")
                for cc0 in range(0, ncol, 4):
                    ng = min(4, ncol - cc0)
                    bk = 6 + (cc0 // 4) % 2
                    for cc in range(cc0, cc0 + ng):
                        E.op("tensor", lambda e, cc=cc, bk=bk, cc0=cc0: e.transpose(
                            ctx.psbf6(bk, cc - cc0), wq[:, tile_slice(cc)], ctx.consts["ident_bf"][:]),
                            reads=["s6wq", "c_ident_bf"], writes=[f"ps{bk}"])
                    pp = (cc0 // 4) % 2
                    E.op("vector", lambda e, pp=pp, ng=ng, bk=bk: e.tensor_copy(
                        piece[pp][:, 0:ng * 128], ctx.ps[bk][:, 0:ng * 64].bitcast(BF16)),
                        reads=[f"ps{bk}"], writes=[f"s6pc{pp}"])
                    E.dma("sync",
                          wdst[cc0 * 128:(cc0 + ng) * 128, tile_slice(rc)]
                          .rearrange("(a p) c -> p a c", p=128),
                          piece[pp][:, 0:ng * 128].rearrange("p (a c) -> p a c", c=128),
                          reads=[f"s6pc{pp}"], writes=[kw], tag="s6stw")

        xrow = ctx.sb(S, "s6xr", [128, D], F32)
        xrb = ctx.sb(S, "s6xrb", [128, D], BF16)
        xqT = [ctx.sb(S, "s6xqT", [128, T], BF16) for _ in range(8)]
        wgc = [ctx.sb(S, "s6wgc", [128, 512], BF16) for _ in range(8)]
        wuc = [ctx.sb(S, "s6wuc", [128, 512], BF16) for _ in range(8)]
        gs2 = [ctx.sb(S, "s6gs", [128, 512], BF16) for _ in range(2)]
        hf2 = [ctx.sb(S, "s6hf", [128, 512], F32) for _ in range(2)]
        hq2 = [ctx.sb(S, "s6hq", [128, 512], BF16) for _ in range(2)]
        hqt = [ctx.sb(S, "s6hqt", [128, 512], BF16) for _ in range(NI)]
        wds = ctx.sb(S, "s6wds", [128, 4 * NI * 128], BF16)
        fstg = ctx.sb(S, "s6fs", [128, 512], F32)
        trs = [ctx.sb(S, "s6tr", [128, 512], F32) for _ in range(2)]

        IC = [(0, 512), (512, 512), (1024, 512), (1536, 512), (2048, 512), (2560, 256)]

        for b in range(B_CORE):
            for tt in range(16):
                E.dma("sync", xrow[:], xn1_d[b, tile_slice(tt), :], writes=["s6xr"], tag="s6ldx")
                emit_fp4_quant(ctx, xrb[:], "s6xrb", xrow[:], "s6xr", D, qs, "s6qx")
                for ht in range(8):
                    E.op("tensor", lambda e, ht=ht: e.transpose(
                        ctx.psbf6(6, ht), xrb[:, tile_slice(ht)], ctx.consts["ident_bf"][:]),
                        reads=["s6xrb", "c_ident_bf"], writes=["ps6"])
                for ht in range(8):
                    E.op("vector", lambda e, ht=ht, tt=tt: e.tensor_copy(
                        xqT[ht][:, tile_slice(tt)], ctx.psbf6(6, ht)),
                        reads=["ps6"], writes=[f"xq{ht}"])
            for (ic0, icw) in IC:
                for ht in range(8):
                    E.dma("sync", wgc[ht][:, 0:icw], wgT_d[tile_slice(ht), ic0:ic0 + icw],
                          reads=["wg"], writes=[f"wgc{ht}"], tag="s6ldg")
                    E.dma("sync", wuc[ht][:, 0:icw], wuT_d[tile_slice(ht), ic0:ic0 + icw],
                          reads=["wu"], writes=[f"wuc{ht}"], tag="s6ldu")
                for tt in range(16):
                    b_ = tt % 2
                    for ht in range(8):
                        E.op("tensor", lambda e, ht=ht, tt=tt, ic0=ic0, icw=icw, b_=b_: e.matmul(
                            ctx.ps[b_][:, 0:icw], xqT[ht][:, tile_slice(tt)],
                            wgc[ht][:, 0:icw], start=(ht == 0), stop=(ht == 7),
                            skip_group_check=True),
                            reads=[f"xq{ht}", f"wgc{ht}"], writes=[f"ps{b_}"], tag="s6gmm")
                        E.op("tensor", lambda e, ht=ht, tt=tt, ic0=ic0, icw=icw, b_=b_: e.matmul(
                            ctx.ps[2 + b_][:, 0:icw], xqT[ht][:, tile_slice(tt)],
                            wuc[ht][:, 0:icw], start=(ht == 0), stop=(ht == 7),
                            skip_group_check=True),
                            reads=[f"xq{ht}", f"wuc{ht}"], writes=[f"ps{2 + b_}"], tag="s6umm")
                    gs = gs2[b_]
                    hf = hf2[b_]
                    hq = hq2[b_]
                    E.op("scalar", lambda e, icw=icw, b_=b_, gs=gs: e.activation(
                        gs[:, 0:icw], ctx.ps[b_][:, 0:icw], AF.Silu),
                        reads=[f"ps{b_}"], writes=[f"s6gs{b_}"], tag="s6silu")
                    E.op("vector", lambda e, icw=icw, b_=b_, gs=gs, hf=hf: e.tensor_tensor(
                        hf[:, 0:icw], gs[:, 0:icw], ctx.ps[2 + b_][:, 0:icw], ALU.mult),
                        reads=[f"s6gs{b_}", f"ps{2 + b_}"], writes=[f"s6hf{b_}"], tag="s6gu")
                    emit_fp4_quant(ctx, hq[:, 0:icw], f"s6hq{b_}", hf[:, 0:icw], f"s6hf{b_}",
                                   icw, qs2[b_], "s6qh")
                    nk2 = icw // 128
                    for k2 in range(nk2):
                        E.op("tensor", lambda e, k2=k2, hq=hq: e.transpose(
                            ctx.psbf6(6, k2), hq[:, tile_slice(k2)], ctx.consts["ident_bf"][:]),
                            reads=[f"s6hq{b_}", "c_ident_bf"], writes=["ps6"])
                    pp = tt % 2
                    E.op("vector", lambda e, pp=pp, nk2=nk2: e.tensor_copy(
                        piece[pp][:, 0:nk2 * 128], ctx.ps[6][:, 0:nk2 * 64].bitcast(BF16)),
                        reads=["ps6"], writes=[f"s6pc{pp}"])
                    E.dma("sync",
                          hqT_d[b, ic0:ic0 + nk2 * 128, tile_slice(tt)]
                          .rearrange("(a p) c -> p a c", p=128),
                          piece[pp][:, 0:nk2 * 128].rearrange("p (a c) -> p a c", c=128),
                          reads=[f"s6pc{pp}"], writes=[f"hqT_{b}"], tag="s6sth")
            for oth in range(2):
                for tc in range(4):
                    for ii in range(NI):
                        E.dma("sync", hqt[ii][:], hqT_d[b, tile_slice(ii), tile_slice(tc, 512)],
                              reads=[f"hqT_{b}"], writes=[f"hqt{ii}"], tag="s6ldh")
                    for ol in range(4):
                        ot = oth * 4 + ol
                        if tc == 0:
                            E.dma("sync", wds[:, ol * NI * 128:(ol + 1) * NI * 128]
                                  .rearrange("p (a c) -> p a c", c=128),
                                  wdT_d.rearrange("(a p) c -> p a c", p=128)[:, :, tile_slice(ot)],
                                  reads=["wd"], writes=[f"wds{ol}"], tag="s6ldwd")
                        for ii in range(NI):
                            E.op("tensor", lambda e, ii=ii, ol=ol, tc=tc: e.matmul(
                                ctx.ps[4 + (tc % 2)][:, :], wds[:, (ol * NI + ii) * 128:(ol * NI + ii + 1) * 128],
                                hqt[ii][:], start=(ii == 0), stop=(ii == NI - 1),
                                skip_group_check=True),
                                reads=[f"hqt{ii}", f"wds{ol}"], writes=[f"ps{4 + (tc % 2)}"], tag="s6dmm")
                        E.op("scalar", lambda e, tc=tc: e.copy(fstg[:], ctx.ps[4 + (tc % 2)][:, :]),
                             reads=[f"ps{4 + (tc % 2)}"], writes=["s6fs"], tag="s6dev")
                        for t2 in range(4):
                            E.op("tensor", lambda e, t2=t2: e.transpose(
                                ctx.ps[1][:, t2 * 128:(t2 + 1) * 128], fstg[:, tile_slice(t2)],
                                ctx.consts["ident_f32"][:]),
                                reads=["s6fs", "c_ident_f32"], writes=["ps1"], tag="s6ftr")
                        pp = ot % 2
                        E.op("scalar", lambda e, pp=pp: e.copy(trs[pp][:], ctx.ps[1][:, :]),
                             reads=["ps1"], writes=[f"s6tr{pp}"])
                        E.dma("sync", f1_d[b, tc * 512:(tc + 1) * 512, tile_slice(ot)]
                              .rearrange("(a p) c -> p a c", p=128),
                              trs[pp][:].rearrange("p (a c) -> p a c", c=128),
                              reads=[f"s6tr{pp}"], writes=[f"d_f1_{b}"], tag="s6stf")


def emit_s7(ctx):
    E = ctx.E
    x1_d = ctx.dio("x1", (B_CORE, T, D), F32, producer="s4", consumers=("s7",))
    s1_d = ctx.dio("s1", (B_CORE, T, D), F32, producer="s5", consumers=("s7",))
    f1_d = ctx.dio("f1", (B_CORE, T, D), F32, producer="s6", consumers=("s7",))
    out_d = ctx.dout("out", (B_CORE, T, D), F32)

    with ctx.stage("s7") as S:
        if S is None:
            return
        xt = ctx.sb(S, "s7x", [128, D], F32)
        st = ctx.sb(S, "s7s", [128, D], F32)
        ft = ctx.sb(S, "s7f", [128, D], F32)
        for b in range(B_CORE):
            for tt in range(16):
                E.dma("sync", xt[:], x1_d[b, tile_slice(tt), :], writes=["s7x"], tag="s7ldx")
                E.dma("sync", st[:], s1_d[b, tile_slice(tt), :], writes=["s7s"], tag="s7lds")
                E.dma("sync", ft[:], f1_d[b, tile_slice(tt), :], writes=["s7f"], tag="s7ldf")
                E.op("scalar", lambda e: e.activation(ft[:], ft[:], AF.Silu),
                     reads=["s7f"], writes=["s7f"], tag="s7silu")
                E.op("vector", lambda e: e.tensor_tensor(ft[:], ft[:], st[:], ALU.mult),
                     reads=["s7f", "s7s"], writes=["s7f"], tag="s7sg")
                E.op("vector", lambda e: e.tensor_tensor(xt[:], xt[:], ft[:], ALU.add),
                     reads=["s7x", "s7f"], writes=["s7x"], tag="s7add")
                E.dma("sync", out_d[b, tile_slice(tt), :], xt[:],
                      reads=["s7x"], writes=[f"d_out_{b}_{tt}"], tag="s7st")



def build_program(stages, fused, host_consts):
    nc = bass.Bass("TRN2", target_bir_lowering=False)
    stack = contextlib.ExitStack()
    with stack:
        ctx = Ctx(nc, stack, fused, stages)
        ctx.init_psum()
        ctx.load_consts(host_consts, names=("ident_bf", "ident_f32"))
        need_rope = any(st_ in stages for st_ in ("s1", "s2"))
        with contextlib.ExitStack() as rope_scope:
            if need_rope:
                ctx.load_consts(host_consts,
                                names=("rot64_mt", "rot32_mt", "cos64", "sin64",
                                       "cos32", "sin32", "stair"),
                                scope=rope_scope)
            emit_s0(ctx)
            emit_s1(ctx)
            emit_s2(ctx)
        ctx.E.barrier()
        emit_s3(ctx)
        emit_s4(ctx)
        emit_s5(ctx)
        emit_s6(ctx)
        emit_s7(ctx)
        ctx.E.barrier()
        ctx.E.emit()
    return nc, ctx


def kernel(**inputs):
    host_consts = build_consts()
    x_full = np.ascontiguousarray(inputs["L0_attn_x"], dtype=np.float32)
    nc, ctx = build_program(ALL_STAGES, True, host_consts)

    weight_arrays = {}
    for k, v in inputs.items():
        if k == "L0_attn_x":
            continue
        name = "w_" + k
        if name in ctx.dram:
            weight_arrays[name] = np.ascontiguousarray(v, dtype=np.float32)
    const_arrays = {("c_" + n): np.asarray(a) for n, a in host_consts.items()
                    if ("c_" + n) in ctx.dram}

    in_maps = []
    for core in range(8):
        m = {"x": x_full[core * B_CORE:(core + 1) * B_CORE]}
        m.update(const_arrays)
        m.update(weight_arrays)
        in_maps.append(m)
    res = run_bass_kernel_spmd(nc, in_maps, core_ids=list(range(8)))
    out = np.concatenate([np.asarray(r["out"]) for r in res.results], axis=0)
    return out


# version 11
# speedup vs baseline: 1.2368x; 1.0794x over previous
import bisect
import contextlib
import numpy as np
import ml_dtypes
import concourse.bass as bass
import concourse.mybir as mybir
from concourse.bass_utils import run_bass_kernel_spmd

F32 = mybir.dt.float32
BF16 = mybir.dt.bfloat16
U8 = mybir.dt.uint8
AF = mybir.ActivationFunctionType
ALU = mybir.AluOpType
AX = mybir.AxisListType

ENGINES = ("sync", "tensor", "vector", "scalar", "gpsimd")

B_CORE = 2
T = 2048
D = 1024
ALL_STAGES = ("s0", "s1", "s2", "s3", "s4", "s5", "s6", "s7")
S2_PHASES = 99
S2_LAT_LIMIT = 99
S2_LAT_MODE = 'full'



class _Op:
    __slots__ = ("eng", "fn", "reads", "writes", "is_dma", "pos", "dma_idx",
                 "deps", "needs_inc", "inc_value", "waits", "drain_before", "tag",
                 "gidx", "same_dep_max", "wait_ops")

    def __init__(self):
        self.deps = []
        self.needs_inc = False
        self.inc_value = None
        self.waits = []
        self.drain_before = False
        self.dma_idx = None
        self.tag = ""
        self.gidx = 0
        self.same_dep_max = -1
        self.wait_ops = []


class Emitter:

    def __init__(self, nc, stack):
        self.nc = nc
        self.stack = stack
        self.per_eng = {e: [] for e in ENGINES}
        self.last_write = {}
        self.readers = {}
        self.sem = {e: stack.enter_context(nc.semaphore(f"prog_{e}")) for e in ENGINES}
        self.dma_ring = {"sync": 8, "gpsimd": 4, "scalar": 2}
        self.dma_sem = {e: [stack.enter_context(nc.semaphore(f"dma_{e}{i}"))
                            for i in range(r)] for e, r in self.dma_ring.items()}
        self.dma_count = {e: 0 for e in ENGINES}
        self.gcount = 0
        self.nbar = 0

    def op(self, eng, fn, reads=(), writes=(), tag=""):
        o = _Op()
        o.eng, o.fn, o.reads, o.writes = eng, fn, tuple(reads), tuple(writes)
        o.is_dma = False
        o.pos = len(self.per_eng[eng])
        o.gidx = self.gcount
        self.gcount += 1
        o.tag = tag
        self._resolve(o)
        self.per_eng[eng].append(o)
        self._update(o)
        return o

    def dma(self, queue, out_ap, in_ap, reads=(), writes=(), tag="", **kw):
        assert queue in ("sync", "gpsimd", "scalar")
        o = _Op()
        o.eng = queue
        o.fn = lambda e, _o=out_ap, _i=in_ap, _kw=kw: e.dma_start(out=_o, in_=_i, **_kw)
        o.reads, o.writes = tuple(reads), tuple(writes)
        o.is_dma = True
        o.pos = len(self.per_eng[queue])
        o.gidx = self.gcount
        self.gcount += 1
        o.dma_idx = self.dma_count[queue]
        self.dma_count[queue] += 1
        o.tag = tag
        self._resolve(o)
        R = self.dma_ring[queue]
        i = o.dma_idx
        if i >= R:
            o.waits.append((("dmaslot", queue, i % R), 16 * (i // R)))
        self.per_eng[queue].append(o)
        self._update(o)
        return o

    def barrier(self):
        n = self.nbar
        self.nbar += 1
        keys = [f"__bar{n}_{e}" for e in ENGINES]
        for e, k in zip(ENGINES, keys):
            self.op(e, None, reads=(), writes=(k,), tag=f"bar{n}")
        for e in ENGINES:
            self.op(e, None, reads=keys, writes=(), tag=f"barw{n}")

    def _resolve(self, o):
        deps = {}
        for k in o.reads:
            w = self.last_write.get(k)
            if w is not None:
                deps[id(w)] = w
        for k in o.writes:
            w = self.last_write.get(k)
            if w is not None:
                deps[id(w)] = w
            for r in self.readers.get(k, ()):
                deps[id(r)] = r
        deps.pop(id(o), None)
        for d in deps.values():
            if d.eng == o.eng and not d.is_dma:
                if o.eng == "tensor":
                    continue
                if d.pos > o.same_dep_max:
                    o.same_dep_max = d.pos
            else:
                o.deps.append(d)

    def _update(self, o):
        for k in o.reads:
            self.readers.setdefault(k, []).append(o)
        for k in o.writes:
            self.last_write[k] = o
            self.readers[k] = []

    def emit(self):
        all_ops = sorted((o for e in ENGINES for o in self.per_eng[e]),
                         key=lambda o: o.gidx)
        inc_list = {e: [] for e in ENGINES}
        for o in all_ops:
            for d in o.deps:
                if d.is_dma:
                    o.wait_ops.append(d)
                    continue
                lst = inc_list[d.eng]
                j = bisect.bisect_left(lst, d.pos, key=lambda x: x.pos)
                if j < len(lst) and lst[j].gidx < o.gidx:
                    o.wait_ops.append(lst[j])
                else:
                    d.needs_inc = True
                    bisect.insort(lst, d, key=lambda x: x.pos)
                    o.wait_ops.append(d)
        for e in ENGINES:
            v = 0
            for o in self.per_eng[e]:
                if o.needs_inc and not o.is_dma:
                    v += 1
                    o.inc_value = v
        for e in ENGINES:
            if e == "tensor":
                continue
            covered = -1.0
            for o in self.per_eng[e]:
                if o.same_dep_max > covered:
                    o.drain_before = True
                    covered = o.pos - 0.5
                if o.needs_inc and not o.is_dma:
                    covered = o.pos
        for e in ENGINES:
            hw = {}
            for o in self.per_eng[e]:
                pre = list(o.waits)
                o.waits = []
                for (key, val) in pre:
                    if val > hw.get(key, 0):
                        o.waits.append((key, val))
                        hw[key] = val
                for d in o.wait_ops:
                    if d.is_dma:
                        R = self.dma_ring[d.eng]
                        key = ("dmaslot", d.eng, d.dma_idx % R)
                        val = 16 * (d.dma_idx // R + 1)
                    else:
                        key = ("prog", d.eng)
                        val = d.inc_value
                    if val > hw.get(key, 0):
                        o.waits.append((key, val))
                        hw[key] = val

        nc = self.nc
        block = self.stack.enter_context(nc.Block())

        def make_runner(eng_name):
            def run(E):
                for o in self.per_eng[eng_name]:
                    if o.drain_before:
                        E.drain()
                    for (sem_key, val) in o.waits:
                        if sem_key[0] == "prog":
                            s = self.sem[sem_key[1]]
                        else:
                            s = self.dma_sem[sem_key[1]][sem_key[2]]
                        E.wait_ge(s, val)
                    instr = o.fn(E) if o.fn is not None else None
                    if o.is_dma:
                        R = self.dma_ring[o.eng]
                        instr.then_inc(self.dma_sem[o.eng][o.dma_idx % R], 16)
                    elif o.needs_inc:
                        if o.eng == "tensor" and instr is not None:
                            instr.then_inc(self.sem[o.eng], 1)
                        else:
                            E.drain().then_inc(self.sem[o.eng], 1)
            return run

        for e in ENGINES:
            if self.per_eng[e]:
                getattr(block, e)(make_runner(e))
        return block



def build_consts():
    c = {}
    c["ident_bf"] = np.eye(128, dtype=ml_dtypes.bfloat16)
    c["ident_f32"] = np.eye(128, dtype=np.float32)

    def rot_mt(h):
        m = np.zeros((128, 128), np.float32)
        for i in range(128):
            j = i % h
            base = i - j
            if j < h // 2:
                m[i, base + j + h // 2] = -1.0
            else:
                m[i, base + j - h // 2] = 1.0
        return np.ascontiguousarray(m.T.astype(ml_dtypes.bfloat16))
    c["rot64_mt"] = rot_mt(64)
    c["rot32_mt"] = rot_mt(32)

    def rope_tab(dh):
        inv = 1.0 / (10000.0 ** (np.arange(0, dh, 2, dtype=np.float64) / dh))
        freqs = np.arange(T, dtype=np.float64)[:, None] * inv[None, :]
        emb = np.concatenate([freqs, freqs], axis=-1)
        cosF = np.tile(np.cos(emb).T, (128 // dh, 1)).astype(ml_dtypes.bfloat16)
        sinF = np.tile(np.sin(emb).T, (128 // dh, 1)).astype(ml_dtypes.bfloat16)
        return np.ascontiguousarray(cosF), np.ascontiguousarray(sinF)
    c["cos64"], c["sin64"] = rope_tab(64)
    c["cos32"], c["sin32"] = rope_tab(32)

    p = np.arange(128)[:, None]
    col = np.arange(1024)[None, :]
    c["stair"] = np.ascontiguousarray((col >= p + 512).astype(ml_dtypes.bfloat16))
    return c



class Ctx:
    def __init__(self, nc, stack, fused, stages):
        self.nc = nc
        self.stack = stack
        self.E = Emitter(nc, stack)
        self.fused = fused
        self.stages = stages
        self.dram = {}
        self.kinds = {}
        self._uid = 0
        self.consts = {}
        self.ps = []

    def uid(self):
        self._uid += 1
        return self._uid

    def din(self, name, shape, dtype=F32):
        if name not in self.dram:
            self.dram[name] = self.nc.dram_tensor(name, list(shape), dtype, kind="ExternalInput").ap()
            self.kinds[name] = "ExternalInput"
        return self.dram[name]

    def dout(self, name, shape, dtype=F32):
        if name not in self.dram:
            self.dram[name] = self.nc.dram_tensor(name, list(shape), dtype, kind="ExternalOutput").ap()
            self.kinds[name] = "ExternalOutput"
        return self.dram[name]

    def dio(self, name, shape, dtype=F32, producer=None, consumers=()):
        if name in self.dram:
            return self.dram[name]
        if self.fused:
            kind = "Internal"
        else:
            prod = producer in self.stages
            cons = any(s in self.stages for s in consumers)
            if prod and cons:
                kind = "Internal"
            elif prod:
                kind = "ExternalOutput"
            elif cons:
                kind = "ExternalInput"
            else:
                kind = "Internal"
        self.dram[name] = self.nc.dram_tensor(name, list(shape), dtype, kind=kind).ap()
        self.kinds[name] = kind
        return self.dram[name]

    def dscratch(self, name, shape, dtype=F32):
        if name not in self.dram:
            self.dram[name] = self.nc.dram_tensor(name, list(shape), dtype, kind="Internal").ap()
            self.kinds[name] = "Internal"
        return self.dram[name]

    def sb(self, scope, name, shape, dtype=F32):
        return scope.enter_context(self.nc.sbuf_tensor(f"{name}_{self.uid()}", list(shape), dtype))

    def init_psum(self):
        self.ps_all = self.stack.enter_context(self.nc.psum_tensor("psall", [128, 4096], F32))
        for i in range(8):
            self.ps.append(self.ps_all[:, i * 512:(i + 1) * 512])

    def psbf(self, i):
        return self.ps[i][:, 0:64].bitcast(BF16)

    def psbf6(self, i, slot):
        return self.ps[i][:, slot * 64:(slot + 1) * 64].bitcast(BF16)

    def load_consts(self, host_consts, names=None, scope=None):
        E = self.E
        for name, arr in host_consts.items():
            if names is not None and name not in names:
                continue
            dt = BF16 if arr.dtype == ml_dtypes.bfloat16 else F32
            d = self.din("c_" + name, arr.shape, dt)
            t = self.sb(scope or self.stack, "c_" + name, list(arr.shape), dt)
            E.dma("sync", t[:], d[:, :], reads=(), writes=("c_" + name,), tag="const")
            self.consts[name] = t
        if not hasattr(self, "_sconsts"):
            self._sconsts = {}
            for v in (1e-5, 1.0):
                self.sconst(v)

    def sconst(self, val):
        if val not in self._sconsts:
            t = self.sb(self.stack, f"sc_{len(self._sconsts)}", [128, 1], F32)
            self.E.op("vector", lambda e, t=t, val=val: e.memset(t[:], float(val)),
                      reads=(), writes=(f"sconst_{val}",))
            self._sconsts[val] = (t, f"sconst_{val}")
        return self._sconsts[val]

    @contextlib.contextmanager
    def stage(self, name):
        if name not in self.stages:
            yield None
            return
        with contextlib.ExitStack() as s:
            yield s
        self.E.barrier()



def tile_slice(i, w=128):
    return slice(i * w, (i + 1) * w)


def emit_ln_tile(ctx, sc, x_ap, kx, out_ap, kout, tag, d_model=D, eps=1e-5):
    E = ctx.E
    sq, ssq, sm, var, m_t, r_t = sc
    E.op("scalar", lambda e: e.activation(sq[:], x_ap, AF.Square, accum_out=ssq[:, :]),
         reads=[kx], writes=[tag + "sq", tag + "ssq"])
    E.op("vector", lambda e: e.tensor_reduce(sm[:].unsqueeze(2), x_ap.unsqueeze(1), AX.X, ALU.add),
         reads=[kx], writes=[tag + "sm"])
    E.op("vector", lambda e: e.tensor_scalar(m_t[:], sm[:], 1.0 / d_model, None, ALU.mult),
         reads=[tag + "sm"], writes=[tag + "m"])
    E.op("vector", lambda e: e.tensor_tensor(sm[:], m_t[:], m_t[:], ALU.mult),
         reads=[tag + "m"], writes=[tag + "sm"])
    E.op("vector", lambda e: e.tensor_scalar(var[:], ssq[:], 1.0 / d_model, None, ALU.mult),
         reads=[tag + "ssq"], writes=[tag + "v"])
    E.op("vector", lambda e: e.tensor_tensor(var[:], var[:], sm[:], ALU.subtract),
         reads=[tag + "v", tag + "sm"], writes=[tag + "v"])
    epst, keps = ctx.sconst(eps)
    E.op("scalar", lambda e: e.activation(var[:], var[:], AF.Ln, bias=epst[:, :]),
         reads=[tag + "v", keps], writes=[tag + "v"])
    E.op("scalar", lambda e: e.activation(r_t[:], var[:], AF.Exp, scale=-0.5),
         reads=[tag + "v"], writes=[tag + "r"])
    E.op("vector", lambda e: e.tensor_scalar(out_ap, x_ap, m_t[:], r_t[:], ALU.subtract, ALU.mult),
         reads=[kx, tag + "m", tag + "r"], writes=[kout])


def make_ln_scratch(ctx, S, pfx):
    return (ctx.sb(S, pfx + "sq", [128, D], F32),
            ctx.sb(S, pfx + "ssq", [128, 1], F32),
            ctx.sb(S, pfx + "sm", [128, 1], F32),
            ctx.sb(S, pfx + "var", [128, 1], F32),
            ctx.sb(S, pfx + "m", [128, 1], F32),
            ctx.sb(S, pfx + "r", [128, 1], F32))


def emit_transpose_tile(ctx, dst_ap, kdst, src_ap, ksrc, dtype, psum_bank, kpsum, scale_ap=None, out_dtype=None):
    E = ctx.E
    ident = ctx.consts["ident_bf"] if dtype == BF16 else ctx.consts["ident_f32"]
    ps = psum_bank
    pview = ps[:, 0:128] if dtype != BF16 else ps[:, 0:128]
    E.op("tensor", lambda e: e.transpose(pview, src_ap, ident[:]),
         reads=[ksrc], writes=[kpsum])
    if scale_ap is None:
        E.op("scalar", lambda e: e.copy(dst_ap, pview), reads=[kpsum], writes=[kdst])
    else:
        E.op("scalar", lambda e: e.activation(dst_ap, pview, AF.Copy, scale=scale_ap),
             reads=[kpsum, "gain"], writes=[kdst])



def emit_s0(ctx):
    E = ctx.E
    x_d = ctx.din("x", (B_CORE, T, D), F32)
    xn_d = ctx.dio("xn", (B_CORE, T, D), F32, producer="s0", consumers=("s2", "s3"))
    xnT_d = ctx.dio("xnT", (B_CORE, D, T), BF16, producer="s0", consumers=("s1", "s2", "s3"))

    with ctx.stage("s0") as S:
        if S is None:
            return
        xt = [ctx.sb(S, "s0x", [128, D], F32) for _ in range(2)]
        xn_t = [ctx.sb(S, "s0n", [128, D], F32) for _ in range(2)]
        stg = [ctx.sb(S, "s0stg", [128, 512], BF16) for _ in range(2)]
        sc = make_ln_scratch(ctx, S, "s0")

        for b in range(B_CORE):
            for tt in range(T // 128):
                p = (b * 16 + tt) % 2
                kx = f"s0x{p}"
                kn = f"s0n{p}"
                E.dma("sync", xt[p][:], x_d[b, tile_slice(tt), :], writes=[kx], tag="s0ld")
                emit_ln_tile(ctx, sc, xt[p][:], kx, xn_t[p][:], kn, "s0ln")
                E.dma("sync", xn_d[b, tile_slice(tt), :], xn_t[p][:], reads=[kn],
                      writes=[f"d_xn_{b}_{tt}"], tag="s0st")
                for dt_i in range(D // 128):
                    bk = 6 + dt_i // 4
                    E.op("tensor", lambda e, p=p, dt_i=dt_i, bk=bk: e.transpose(
                        ctx.ps[bk][:, (dt_i % 4) * 128:(dt_i % 4 + 1) * 128],
                        xn_t[p][:, tile_slice(dt_i)], ctx.consts["ident_f32"][:]),
                        reads=[kn, "c_ident_f32"], writes=[f"ps{bk}"])
                for gg in range(2):
                    bk = 6 + gg
                    E.op("scalar", lambda e, gg=gg, bk=bk: e.copy(stg[gg][:], ctx.ps[bk][:, :]),
                         reads=[f"ps{bk}"], writes=[f"s0stg{gg}"])
                    E.dma("sync",
                          xnT_d[b, gg * 512:(gg + 1) * 512, tile_slice(tt)]
                          .rearrange("(a p) c -> p a c", p=128),
                          stg[gg][:].rearrange("p (a c) -> p a c", c=128),
                          reads=[f"s0stg{gg}"], writes=[f"d_xnT_{b}_{gg}_{tt}"], tag="s0stT")



def emit_rope_pair(ctx, S, out_h, kout, src_h, ksrc, rot_mt_name, cos_name, sin_name,
                   rtmp, ps_bank, kps, tagu):
    E = ctx.E
    krt = f"rtmp_{id(rtmp)}"
    rot = ctx.consts[rot_mt_name]
    cosc = ctx.consts[cos_name]
    sinc = ctx.consts[sin_name]
    for tc in range(4):
        E.op("tensor", lambda e, tc=tc: e.matmul(
            ps_bank[:, 0:512], rot[:], src_h[:, tile_slice(tc, 512)], start=True, stop=True,
            skip_group_check=True),
            reads=[ksrc, "c_" + rot_mt_name], writes=[kps], tag=tagu + "rot")
        E.op("vector", lambda e, tc=tc: e.tensor_tensor(
            rtmp[:, tile_slice(tc, 512)], ps_bank[:, 0:512], sinc[:, tile_slice(tc, 512)], ALU.mult),
            reads=[kps, "c_" + sin_name], writes=[krt], tag=tagu + "sin")
    E.op("vector", lambda e: e.tensor_tensor(out_h[:], src_h[:], cosc[:], ALU.mult),
         reads=[ksrc, "c_" + cos_name], writes=[kout], tag=tagu + "cos")
    E.op("vector", lambda e: e.tensor_tensor(out_h[:], out_h[:], rtmp[:], ALU.add),
         reads=[kout, krt], writes=[kout], tag=tagu + "add")


def emit_s1(ctx):
    E = ctx.E
    nc = ctx.nc
    xnT_d = ctx.dio("xnT", (B_CORE, D, T), BF16, producer="s0", consumers=("s1", "s2", "s3"))
    s0_d = ctx.dio("s0", (B_CORE, T, D), F32, producer="s1", consumers=("s4",))
    zs_d = ctx.dscratch("s1_zs", (B_CORE, 16, T), F32)
    zr_d = ctx.dscratch("s1_zr", (B_CORE, 16, T), F32)

    w_dk = ctx.din("w_L0_attn_w_dk", (D, 128), F32)
    w_dv = ctx.din("w_L0_attn_w_dv", (D, 128), F32)
    w_uk = ctx.din("w_L0_attn_w_uk", (128, D), F32)
    w_uv = ctx.din("w_L0_attn_w_uv", (128, D), F32)
    w_o = ctx.din("w_L0_attn_w_o", (D, D), F32)
    b_dk = ctx.din("w_L0_attn_b_dk", (128,), F32)
    b_dv = ctx.din("w_L0_attn_b_dv", (128,), F32)
    b_uk = ctx.din("w_L0_attn_b_uk", (D,), F32)
    b_uv = ctx.din("w_L0_attn_b_uv", (D,), F32)
    b_o = ctx.din("w_L0_attn_b_o", (D,), F32)

    with ctx.stage("s1") as S:
        if S is None:
            return
        wdk = ctx.sb(S, "wdk", [128, 8 * 128], BF16)
        wdv = ctx.sb(S, "wdv", [128, 8 * 128], BF16)
        wuk = ctx.sb(S, "wuk", [128, D], BF16)
        wuv = ctx.sb(S, "wuv", [128, D], BF16)
        wo = ctx.sb(S, "wo", [128, 8 * D], BF16)
        E.dma("gpsimd", wdk[:].rearrange("p (h c) -> p h c", h=8),
              w_dk.rearrange("(h p) c -> p h c", p=128), writes=["wdk"])
        E.dma("gpsimd", wdv[:].rearrange("p (h c) -> p h c", h=8),
              w_dv.rearrange("(h p) c -> p h c", p=128), writes=["wdv"])
        E.dma("gpsimd", wuk[:], w_uk[:, :], writes=["wuk"])
        E.dma("gpsimd", wuv[:], w_uv[:, :], writes=["wuv"])
        E.dma("gpsimd", wo[:].rearrange("p (h o) -> p h o", h=8),
              w_o.rearrange("(h p) o -> p h o", p=128), writes=["wo"])
        bdk = ctx.sb(S, "bdk", [128, 1], F32)
        bdv = ctx.sb(S, "bdv", [128, 1], F32)
        buk = ctx.sb(S, "buk", [128, 8], F32)
        buv = ctx.sb(S, "buv", [128, 8], F32)
        bo = ctx.sb(S, "bo", [128, 8], F32)
        E.dma("sync", bdk[:], b_dk.rearrange("(a p) -> p a", p=128), writes=["bdk"], allow_slow_non_contiguous=True)
        E.dma("sync", bdv[:], b_dv.rearrange("(a p) -> p a", p=128), writes=["bdv"], allow_slow_non_contiguous=True)
        E.dma("sync", buk[:], b_uk.rearrange("(a p) -> p a", p=128), writes=["buk"], allow_slow_non_contiguous=True)
        E.dma("sync", buv[:], b_uv.rearrange("(a p) -> p a", p=128), writes=["buv"], allow_slow_non_contiguous=True)
        E.dma("sync", bo[:], b_o.rearrange("(a p) -> p a", p=128), writes=["bo"], allow_slow_non_contiguous=True)

        xnt = [ctx.sb(S, "s1xnt", [128, T], BF16) for _ in range(8)]
        ck = ctx.sb(S, "s1ck", [128, T], BF16)
        cv = ctx.sb(S, "s1cv", [128, T], BF16)
        vsb = [ctx.sb(S, "s1v", [128, 16 * 65], BF16) for _ in range(16)]
        keff = ctx.sb(S, "s1ke", [128, T], BF16)
        qeff = ctx.sb(S, "s1qe", [128, T], BF16)
        ktmp = ctx.sb(S, "s1kt", [128, T], BF16)
        rtmp = ctx.sb(S, "s1rt", [128, T], BF16)
        psb = [ctx.sb(S, "s1p", [128, 1024], BF16) for _ in range(3)]
        avsb = [ctx.sb(S, "s1av", [128, T], BF16) for _ in range(8)]
        avn = avsb
        zstg = ctx.sb(S, "s1zstg", [1, 512], F32)
        zall = ctx.sb(S, "s1zall", [16, T], F32)
        zbc = ctx.sb(S, "s1zbc", [128, T], F32)
        sostg = ctx.sb(S, "s1so", [128, 512], F32)
        trstg = [ctx.sb(S, "s1tr", [128, 512], F32) for _ in range(2)]

        for b in range(B_CORE):
            kb = f"b{b}"
            for ht in range(8):
                E.dma("sync", xnt[ht][:], xnT_d[b, tile_slice(ht), :], writes=[f"xnt{ht}"], tag="s1ldx")
            for (csb, wsb, bias, kc, kw, kbias) in ((ck, wdk, bdk, "ck", "wdk", "bdk"),
                                                    (cv, wdv, bdv, "cv", "wdv", "bdv")):
                for tc in range(4):
                    bk = 6 + tc % 2
                    for ht in range(8):
                        E.op("tensor", lambda e, tc=tc, ht=ht, wsb=wsb, bk=bk: e.matmul(
                            ctx.ps[bk][:, :], wsb[:, tile_slice(ht)],
                            xnt[ht][:, tile_slice(tc, 512)], start=(ht == 0), stop=(ht == 7), skip_group_check=True),
                            reads=[f"xnt{ht}", kw], writes=[f"ps{bk}"], tag="s1c")
                    E.op("scalar", lambda e, tc=tc, csb=csb, bias=bias, bk=bk: e.activation(
                        csb[:, tile_slice(tc, 512)], ctx.ps[bk][:, :], AF.Identity, bias=bias[:, :]),
                        reads=[f"ps{bk}", kbias], writes=[kc], tag="s1cev")

            for tp in range(16):
                E.op("vector", lambda e, tp=tp: e.memset(vsb[tp][:], 1.0),
                     writes=[f"v{tp}"], tag="s1vm")
                for ch in range(2):
                    bk = 6 + ch
                    E.op("tensor", lambda e, tp=tp, ch=ch, bk=bk: e.matmul(
                        ctx.ps[bk][:, :], cv[:, tile_slice(tp)],
                        wuv[:, tile_slice(ch, 512)], start=True, stop=True, skip_group_check=True),
                        reads=["cv", "wuv"], writes=[f"ps{bk}"], tag="s1vmm")
                    for hh in range(8):
                        h = ch * 8 + hh
                        E.op("scalar", lambda e, tp=tp, h=h, hh=hh, bk=bk: e.copy(
                            vsb[tp][:, h * 65:h * 65 + 64], ctx.ps[bk][:, hh * 64:(hh + 1) * 64]),
                            reads=[f"ps{bk}"], writes=[f"v{tp}"], tag="s1vev")

            for pair in range(8):
                for tc in range(4):
                    bk = 6 + tc % 2
                    E.op("tensor", lambda e, tc=tc, pair=pair, bk=bk: e.matmul(
                        ctx.ps[bk][:, :], wuk[:, tile_slice(pair)],
                        ck[:, tile_slice(tc, 512)], start=True, stop=True, skip_group_check=True),
                        reads=["ck", "wuk"], writes=[f"ps{bk}"], tag="s1kmm")
                    E.op("scalar", lambda e, tc=tc, pair=pair, bk=bk: e.activation(
                        ktmp[:, tile_slice(tc, 512)], ctx.ps[bk][:, :], AF.Identity,
                        bias=buk[:, pair:pair + 1]),
                        reads=[f"ps{bk}", "buk"], writes=["ktmp"], tag="s1kev")
                emit_rope_pair(ctx, S, keff, "keff", ktmp, "ktmp",
                               "rot64_mt", "cos64", "sin64", rtmp, ctx.ps[6], "ps6", "s1kr")
                emit_rope_pair(ctx, S, qeff, "qeff", xnt[pair], f"xnt{pair}",
                               "rot64_mt", "cos64", "sin64", rtmp, ctx.ps[6], "ps6", "s1qr")

                PBMAP = ((0, 1), (2, 3), (6, 7))
                pend = []

                def flush_av(unit):
                    (qq_, kp_, hh_) = unit
                    h_ = pair * 2 + hh_
                    avb = 4 + hh_
                    pidx_ = (kp_ * 2 + hh_) % 3
                    for half in range(2):
                        kk = kp_ * 2 + half
                        E.op("tensor", lambda e, kk=kk, h_=h_, pidx_=pidx_, half=half, avb=avb: e.matmul(
                            ctx.ps[avb][0:65, :], vsb[kk][:, h_ * 65:(h_ + 1) * 65],
                            psb[pidx_][:, half * 512:(half + 1) * 512],
                            start=(kk == 0), stop=(kk == 15),
                            skip_group_check=True),
                            reads=[f"p{pidx_}", f"v{kk}"], writes=[f"ps{avb}"], tag="s1avmm")

                for qq in range(4):
                    for kp in range(8):
                        for hh in range(2):
                            r0 = hh * 64
                            pidx = (kp * 2 + hh) % 3
                            b0, b1 = PBMAP[pidx]
                            for half in range(2):
                                kk = kp * 2 + half
                                bb = (b0, b1)[half]
                                E.op("tensor", lambda e, kk=kk, qq=qq, r0=r0, bb=bb: e.matmul(
                                    ctx.ps[bb][:, :], keff[r0:r0 + 64, tile_slice(kk)],
                                    qeff[r0:r0 + 64, tile_slice(qq, 512)], start=True, stop=True, skip_group_check=True),
                                    reads=["keff", "qeff"], writes=[f"ps{bb}"], tag="s1smm")
                            E.op("scalar", lambda e, b0=b0, pidx=pidx: e.activation(
                                psb[pidx][:], ctx.ps_all[:, b0 * 512:(b0 + 2) * 512], AF.Exp, scale=0.125),
                                reads=[f"ps{b0}", f"ps{b0 + 1}"], writes=[f"p{pidx}"], tag="s1exp")
                            pend.append((qq, kp, hh))
                            if len(pend) > 1:
                                flush_av(pend.pop(0))
                    while pend:
                        flush_av(pend.pop(0))
                    for hh in range(2):
                        h = pair * 2 + hh
                        r0 = hh * 64
                        avbank = 4 + hh
                        E.op("scalar", lambda e, qq=qq, r0=r0, pair=pair, avbank=avbank: e.copy(
                            avsb[pair][r0:r0 + 64, tile_slice(qq, 512)], ctx.ps[avbank][0:64, :]),
                            reads=[f"ps{avbank}"], writes=[f"av{pair}"], tag="s1avev")
                        E.op("scalar", lambda e, avbank=avbank: e.copy(
                            zstg[:, :], ctx.ps[avbank][64:65, :]),
                            reads=[f"ps{avbank}"], writes=["zstg"], tag="s1zev")
                        E.dma("sync", zs_d[b, h, tile_slice(qq, 512)], zstg[:, :],
                              reads=["zstg"], writes=[f"zs{h}{qq}"], tag="s1zst")

            E.dma("sync", zall[:], zs_d[b],
                  reads=[f"zs{h}{qq}" for h in range(16) for qq in range(4)],
                  writes=["zall"], tag="s1zld")
            E.op("vector", lambda e: e.reciprocal(zall[:], zall[:]), reads=["zall"], writes=["zall"])
            E.dma("sync", zr_d[b], zall[:], reads=["zall"], writes=["zr"], tag="s1zrst")

            for pair in range(8):
                for hh in range(2):
                    E.dma("sync", zbc[hh * 64:(hh + 1) * 64, :],
                          zr_d[b, pair * 2 + hh:pair * 2 + hh + 1, :].broadcast_to([64, T]),
                          reads=["zr"], writes=["zbc"], tag="s1zbc")
                E.op("vector", lambda e, pair=pair: e.tensor_tensor(
                    avn[pair][:], avsb[pair][:], zbc[:], ALU.mult),
                    reads=[f"av{pair}", "zbc"], writes=[f"av{pair}", f"avn{pair}"], tag="s1nrm")
                E.op("vector", lambda e, pair=pair: e.tensor_scalar(
                    avn[pair][:], avn[pair][:], buv[:, pair:pair + 1], None, ALU.add),
                    reads=[f"avn{pair}", "buv"], writes=[f"avn{pair}"], tag="s1buv")

            for ot in range(8):
                for tc in range(4):
                    bk = 6 + tc % 2
                    E.op("tensor", lambda e, ot=ot, tc=tc, bk=bk: e.matmul(
                        ctx.ps[bk][:, :], wo[:, 0 * D + ot * 128: 0 * D + (ot + 1) * 128],
                        avn[0][:, tile_slice(tc, 512)], start=True, stop=False, skip_group_check=True),
                        reads=["avn0", "wo"], writes=[f"ps{bk}"], tag="s1omm")
                    for ht in range(1, 8):
                        E.op("tensor", lambda e, ot=ot, tc=tc, ht=ht, bk=bk: e.matmul(
                            ctx.ps[bk][:, :], wo[:, ht * D + ot * 128: ht * D + (ot + 1) * 128],
                            avn[ht][:, tile_slice(tc, 512)], start=False, stop=(ht == 7), skip_group_check=True),
                            reads=[f"avn{ht}", "wo"], writes=[f"ps{bk}"], tag="s1omm")
                    E.op("scalar", lambda e, ot=ot, tc=tc, bk=bk: e.activation(
                        sostg[:], ctx.ps[bk][:, :], AF.Identity, bias=bo[:, ot:ot + 1]),
                        reads=[f"ps{bk}", "bo"], writes=["sostg"], tag="s1oev")
                    tb = 2 + tc % 2
                    for t2 in range(4):
                        E.op("tensor", lambda e, t2=t2, tb=tb: e.transpose(
                            ctx.ps[tb][:, t2 * 128:(t2 + 1) * 128], sostg[:, tile_slice(t2)],
                            ctx.consts["ident_f32"][:]),
                            reads=["sostg", "c_ident_f32"], writes=[f"ps{tb}"], tag="s1otr")
                    pr = tc % 2
                    E.op("scalar", lambda e, tb=tb, pr=pr: e.copy(trstg[pr][:], ctx.ps[tb][:, :]),
                         reads=[f"ps{tb}"], writes=[f"trstg{pr}"], tag="s1otrev")
                    E.dma("sync", s0_d[b, tc * 512:(tc + 1) * 512, tile_slice(ot)]
                          .rearrange("(a p) c -> p a c", p=128),
                          trstg[pr][:].rearrange("p (a c) -> p a c", c=128),
                          reads=[f"trstg{pr}"], writes=[f"d_s0_{b}_{ot}_{tc}"], tag="s1ost")



def emit_load_wT(ctx, S, dram_ap, rows, cols, name, head_sliced=None, nat_buf=None):
    E = ctx.E
    nrt, nct = rows // 128, cols // 128
    wT = ctx.sb(S, name + "T", [128, nct * rows], BF16)
    if True:
        nat = nat_buf
        if head_sliced is None:
            E.dma("gpsimd", nat[:, 0:nrt * cols].rearrange("p (rt c) -> p rt c", rt=nrt),
                  dram_ap.rearrange("(rt p) c -> p rt c", p=128), writes=["prepnat"])
        else:
            src_full, nh, lo, tw = head_sliced
            hstride = src_full.shape[0] // nh
            per_tile = 128 // tw
            for h in range(nh):
                rt = h // per_tile
                p0 = (h % per_tile) * tw
                E.dma("gpsimd", nat[p0:p0 + tw, rt * cols:(rt + 1) * cols],
                      src_full[h * hstride + lo: h * hstride + lo + tw, :],
                      writes=["prepnat"])
        for ht in range(nct):
            bk = 6 + ht % 2
            for rt in range(nrt):
                E.op("tensor", lambda e, ht=ht, rt=rt, bk=bk: e.transpose(
                    ctx.psbf6(bk, rt), nat[:, rt * cols + ht * 128: rt * cols + (ht + 1) * 128],
                    ctx.consts["ident_bf"][:]),
                    reads=["prepnat", "c_ident_bf"], writes=[f"ps{bk}"])
            E.op("vector", lambda e, ht=ht, bk=bk, nrt=nrt: e.tensor_copy(
                wT[:, ht * rows: ht * rows + nrt * 128], ctx.ps[bk][:, 0:nrt * 64].bitcast(BF16)),
                reads=[f"ps{bk}"], writes=[name], tag="wTev")
    return wT


def emit_s2(ctx):
    E = ctx.E
    NH, HD, RD, L = 16, 64, 32, 256
    xn_d = ctx.dio("xn", (B_CORE, T, D), F32, producer="s0", consumers=("s2", "s3"))
    t1_d = ctx.dio("t1", (B_CORE, T, D), F32, producer="s2", consumers=("s3",))
    zT_d = ctx.dscratch("s2_zT", (B_CORE, D, T), BF16)
    avn_d = ctx.dscratch("s2_avn", (B_CORE, D, T), BF16)
    zs_d = ctx.dscratch("s2_zs", (B_CORE, 4, T), F32)

    ln1_w = ctx.din("w_L0_ffn_ln1_w", (D,), F32)
    kv_d = ctx.din("w_L0_ffn_kv_d", (L, D), F32)
    q_d = ctx.din("w_L0_ffn_q_d", (L, D), F32)
    k_u = ctx.din("w_L0_ffn_k_u", (D, L), F32)
    q_u = ctx.din("w_L0_ffn_q_u", (D, L), F32)
    v_u = ctx.din("w_L0_ffn_v_u", (D, L), F32)
    rope_q_w = ctx.din("w_L0_ffn_rope_q_w", (512, L), F32)
    rope_k_w = ctx.din("w_L0_ffn_rope_k_w", (512, D), F32)
    o_w = ctx.din("w_L0_ffn_o_w", (D, D), F32)

    with ctx.stage("s2") as S:
        if S is None:
            return
        prep_nat = ctx.sb(S, "prepnat", [128, 8 * 1024], BF16)
        kvdT = emit_load_wT(ctx, S, kv_d, L, D, "kvd", nat_buf=prep_nat)
        qdT = emit_load_wT(ctx, S, q_d, L, D, "qd", nat_buf=prep_nat)
        kuT = emit_load_wT(ctx, S, None, 512, L, "ku", head_sliced=(k_u, 16, 32, 32), nat_buf=prep_nat)
        quT = emit_load_wT(ctx, S, None, 512, L, "qu", head_sliced=(q_u, 16, 32, 32), nat_buf=prep_nat)
        vuT = emit_load_wT(ctx, S, v_u, D, L, "vu", nat_buf=prep_nat)
        rqT = emit_load_wT(ctx, S, rope_q_w, 512, L, "rq", nat_buf=prep_nat)
        rkT = emit_load_wT(ctx, S, rope_k_w, 512, D, "rk", nat_buf=prep_nat)
        owT = emit_load_wT(ctx, S, o_w, D, D, "ow", nat_buf=prep_nat)
        ln1w = ctx.sb(S, "ln1w", [128, 8], F32)
        E.dma("sync", ln1w[:], ln1_w.rearrange("(a p) -> p a", p=128), writes=["ln1w"],
              allow_slow_non_contiguous=True)

        xrow = ctx.sb(S, "s2xr", [128, D], F32)
        znorm = xrow
        sc = make_ln_scratch(ctx, S, "s2")
        stg = [ctx.sb(S, "s2stg", [128, 512], BF16) for _ in range(2)]
        zt = [ctx.sb(S, "s2zt", [128, T], BF16) for _ in range(2)]
        ztc = ([zt[0][:, j * 512:(j + 1) * 512] for j in range(4)]
               + [zt[1][:, j * 512:(j + 1) * 512] for j in range(2)])
        kvlat = [ctx.sb(S, "s2kvl", [128, T], BF16) for _ in range(2)]
        qlat = [ctx.sb(S, "s2ql", [128, T], BF16) for _ in range(2)]
        vsb = [ctx.sb(S, "s2v", [128, 16 * 65], BF16) for _ in range(16)]
        rot_t = [ctx.sb(S, "s2rot", [128, T], BF16) for _ in range(2)]
        keff = [ctx.sb(S, "s2ke", [128, T], BF16) for _ in range(2)]
        qeff = [ctx.sb(S, "s2qe", [128, T], BF16) for _ in range(2)]
        rtmp = ctx.sb(S, "s2rt", [128, T], BF16)
        tmpe = ctx.sb(S, "s2te", [128, T], BF16)
        psb = [ctx.sb(S, "s2p", [128, 1024], BF16) for _ in range(3)]
        avsb = [ctx.sb(S, "s2av", [128, T], BF16) for _ in range(2)]
        zstg = ctx.sb(S, "s2zstg", [1, 512], F32)
        zall = ctx.sb(S, "s2zall", [4, T], F32)
        zbc = ctx.sb(S, "s2zbc", [128, T // 2], F32)
        avst = zt
        sostg = tmpe[:, 0:1024].bitcast(F32)
        t1row = [keff[0][:].bitcast(F32), keff[1][:].bitcast(F32),
                 qeff[0][:].bitcast(F32), qeff[1][:].bitcast(F32)]
        t1alias = ["s2ke0", "s2ke1", "s2qe0", "s2qe1"]

        for b in range(B_CORE):
            if S2_PHASES < 1:
                break
            for tt in range(T // 128):
                p = tt % 2
                kx = f"s2x{p}"
                E.dma("sync", xrow[:], xn_d[b, tile_slice(tt), :], writes=[kx], tag="s2ldx")
                emit_ln_tile(ctx, sc, xrow[:], kx, znorm[:], "s2zn", "s2ln")
                for dt_i in range(8):
                    bk = 6 + dt_i // 4
                    E.op("tensor", lambda e, dt_i=dt_i, bk=bk: e.transpose(
                        ctx.ps[bk][:, (dt_i % 4) * 128:(dt_i % 4 + 1) * 128],
                        znorm[:, tile_slice(dt_i)], ctx.consts["ident_f32"][:]),
                        reads=["s2zn" if znorm is xrow else "s2zn", "s2xr", "c_ident_f32"],
                        writes=[f"ps{bk}"])
                for dt_i in range(8):
                    bk = 6 + dt_i // 4
                    sp = dt_i // 4
                    E.op("scalar", lambda e, dt_i=dt_i, bk=bk, sp=sp: e.activation(
                        stg[sp][:, (dt_i % 4) * 128:(dt_i % 4 + 1) * 128],
                        ctx.ps[bk][:, (dt_i % 4) * 128:(dt_i % 4 + 1) * 128], AF.Identity,
                        scale=ln1w[:, dt_i:dt_i + 1]),
                        reads=[f"ps{bk}", "ln1w"], writes=[f"s2stg{sp}"])
                for sp in range(2):
                    E.dma("sync",
                          zT_d[b, sp * 512:(sp + 1) * 512, tile_slice(tt)]
                          .rearrange("(a p) c -> p a c", p=128),
                          stg[sp][:].rearrange("p (a c) -> p a c", c=128),
                          reads=[f"s2stg{sp}"], writes=[f"zT_{sp}_{tt}"], tag="s2stz")

            zT_keys = [f"zT_{dt_i}_{tt}" for dt_i in range(8) for tt in range(16)]

            if S2_PHASES < 2:
                continue
            lat_n = 0
            for (latbuf, wTb, kw, kl) in ((kvlat, kvdT, "kvd", "kvl"), (qlat, qdT, "qd", "ql")):
                for ct in range(2):
                    for tc in range(4):
                        lat_n += 1
                        if lat_n > S2_LAT_LIMIT:
                            continue
                        for ht in range(8):
                            p = ht % 2
                            E.dma("sync", zt[p][:], zT_d[b, tile_slice(ht), :],
                                  reads=[f"zT_{ht // 4}_{tt}" for tt in range(16)],
                                  writes=[f"s2zt{p}"], tag="s2ldz")
                            E.op("tensor", lambda e, ct=ct, tc=tc, ht=ht, p=p, wTb=wTb: e.matmul(
                                ctx.ps[0][:, :], wTb[:, ht * L + ct * 128: ht * L + (ct + 1) * 128],
                                zt[p][:, tile_slice(tc, 512)], start=(ht == 0), stop=(ht == 7),
                                skip_group_check=True),
                                reads=[f"s2zt{p}", kw], writes=["ps0"], tag="s2lat")
                        E.op("scalar", lambda e, ct=ct, tc=tc, latbuf=latbuf: e.copy(
                            latbuf[ct][:, tile_slice(tc, 512)], ctx.ps[0][:, :]),
                            reads=["ps0"], writes=[kl + str(ct)], tag="s2latev")

            if S2_PHASES < 3:
                continue
            for tp in range(16):
                E.op("vector", lambda e, tp=tp: e.memset(vsb[tp][:], 1.0),
                     writes=[f"v{tp}"], tag="s2vm")
                for ch in range(2):
                    bk = 6 + ch
                    for ct in range(2):
                        E.op("tensor", lambda e, tp=tp, ch=ch, ct=ct, bk=bk: e.matmul(
                            ctx.ps[bk][:, :], kvlat[ct][:, tile_slice(tp)],
                            vuT[:, ct * D + ch * 512: ct * D + (ch + 1) * 512],
                            start=(ct == 0), stop=(ct == 1), skip_group_check=True),
                            reads=["kvl0", "kvl1", "vu"], writes=[f"ps{bk}"], tag="s2vmm")
                    for hh in range(8):
                        h = ch * 8 + hh
                        E.op("scalar", lambda e, tp=tp, h=h, hh=hh, bk=bk: e.copy(
                            vsb[tp][:, h * 65:h * 65 + 64], ctx.ps[bk][:, hh * 64:(hh + 1) * 64]),
                            reads=[f"ps{bk}"], writes=[f"v{tp}"], tag="s2vev")

            if S2_PHASES < 4:
                continue
            for rt in range(4):
                for (dst, which) in ((0, "k"), (1, "q")):
                    if which == "k":
                        for tc in range(4):
                            for ht in range(8):
                                p = ht % 6
                                E.dma("sync", ztc[p], zT_d[b, tile_slice(ht), tile_slice(tc, 512)],
                                      reads=[f"zT_{ht // 4}_{tt}" for tt in range(16)],
                                      writes=[f"s2zt{p}"], tag="s2ldz2")
                                E.op("tensor", lambda e, tc=tc, ht=ht, p=p, rt=rt: e.matmul(
                                    ctx.ps[7][:, :], rkT[:, ht * 512 + rt * 128: ht * 512 + (rt + 1) * 128],
                                    ztc[p], start=(ht == 0), stop=(ht == 7),
                                    skip_group_check=True),
                                    reads=[f"s2zt{p}", "rk"], writes=["ps7"], tag="s2krmm")
                            E.op("scalar", lambda e, tc=tc: e.copy(
                                tmpe[:, tile_slice(tc, 512)], ctx.ps[7][:, :]),
                                reads=["ps7"], writes=["s2te"], tag="s2krev")
                    else:
                        for tc in range(4):
                            for ct in range(2):
                                E.op("tensor", lambda e, tc=tc, ct=ct, rt=rt: e.matmul(
                                    ctx.ps[7][:, :], rqT[:, ct * 512 + rt * 128: ct * 512 + (rt + 1) * 128],
                                    qlat[ct][:, tile_slice(tc, 512)], start=(ct == 0), stop=(ct == 1),
                                    skip_group_check=True),
                                    reads=["ql0", "ql1", "rq"], writes=["ps7"], tag="s2qrmm")
                            E.op("scalar", lambda e, tc=tc: e.copy(
                                tmpe[:, tile_slice(tc, 512)], ctx.ps[7][:, :]),
                                reads=["ps7"], writes=["s2te"], tag="s2qrev")
                    emit_rope_pair(ctx, S, rot_t[dst], f"s2rot{dst}", tmpe, "s2te",
                                   "rot32_mt", "cos32", "sin32", rtmp, ctx.ps[6], "ps6", f"s2r{which}")
                    lat = kvlat if which == "k" else qlat
                    lwT = kuT if which == "k" else quT
                    lk = ("kvl0", "kvl1", "ku") if which == "k" else ("ql0", "ql1", "qu")
                    effarr = keff if which == "k" else qeff
                    keffpfx = "s2ke" if which == "k" else "s2qe"
                    for tc in range(4):
                        for ct in range(2):
                            E.op("tensor", lambda e, tc=tc, ct=ct, rt=rt, lat=lat, lwT=lwT: e.matmul(
                                ctx.ps[7][:, :], lwT[:, ct * 512 + rt * 128: ct * 512 + (rt + 1) * 128],
                                lat[ct][:, tile_slice(tc, 512)], start=(ct == 0), stop=(ct == 1),
                                skip_group_check=True),
                                reads=list(lk), writes=["ps7"], tag="s2upmm")
                        for hh in range(4):
                            E.op("scalar", lambda e, tc=tc, hh=hh, effarr=effarr: e.copy(
                                effarr[hh // 2][(hh % 2) * 64 + 32:(hh % 2) * 64 + 64, tile_slice(tc, 512)],
                                ctx.ps[7][hh * 32:(hh + 1) * 32, :]),
                                reads=["ps7"], writes=[keffpfx + str(hh // 2)], tag="s2upev")

                for pp in range(2):
                    for (effbuf, keffk, srcrot, krot) in (
                            (keff[pp], f"s2ke{pp}", rot_t[0], "s2rot0"),
                            (qeff[pp], f"s2qe{pp}", rot_t[1], "s2rot1")):
                        for hh2 in range(2):
                            hh = pp * 2 + hh2
                            E.op("vector", lambda e, effbuf=effbuf, srcrot=srcrot, hh=hh, hh2=hh2: e.tensor_copy(
                                effbuf[hh2 * 64:hh2 * 64 + 32, :], srcrot[hh * 32:(hh + 1) * 32, :]),
                                reads=[krot], writes=[keffk], tag="s2asm")

                PBMAP = ((0, 1), (2, 3), (6, 7))
                for pp in range(2):
                    pend = []

                    def flush_av2(unit, pp=pp, rt=rt):
                        (qq_, kp_, hh_) = unit
                        h_ = rt * 4 + pp * 2 + hh_
                        avb = 4 + hh_
                        pidx_ = (kp_ * 2 + hh_) % 3
                        kk_max_ = qq_ * 4 + 3
                        for half in range(2):
                            kk = kp_ * 2 + half
                            E.op("tensor", lambda e, kk=kk, h_=h_, pidx_=pidx_, half=half, avb=avb, kk_max_=kk_max_: e.matmul(
                                ctx.ps[avb][0:65, :], vsb[kk][:, h_ * 65:(h_ + 1) * 65],
                                psb[pidx_][:, half * 512:(half + 1) * 512],
                                start=(kk == 0), stop=(kk == kk_max_),
                                skip_group_check=True),
                                reads=[f"p{pidx_}", f"v{kk}"], writes=[f"ps{avb}"], tag="s2avmm")

                    for qq in range(4):
                        kk_max = qq * 4 + 3
                        npairs = (kk_max + 1) // 2
                        for kp in range(npairs):
                            for hh2 in range(2):
                                r0 = hh2 * 64
                                pidx = (kp * 2 + hh2) % 3
                                b0, b1 = PBMAP[pidx]
                                for half in range(2):
                                    kk = kp * 2 + half
                                    bb = (b0, b1)[half]
                                    E.op("tensor", lambda e, kk=kk, qq=qq, r0=r0, bb=bb, pp=pp: e.matmul(
                                        ctx.ps[bb][:, :], keff[pp][r0:r0 + 64, tile_slice(kk)],
                                        qeff[pp][r0:r0 + 64, tile_slice(qq, 512)], start=True, stop=True,
                                        skip_group_check=True),
                                        reads=[f"s2ke{pp}", f"s2qe{pp}"], writes=[f"ps{bb}"], tag="s2smm")
                                E.op("scalar", lambda e, b0=b0, pidx=pidx: e.activation(
                                    psb[pidx][:], ctx.ps_all[:, b0 * 512:(b0 + 2) * 512], AF.Exp, scale=0.125),
                                    reads=[f"ps{b0}", f"ps{b0 + 1}"], writes=[f"p{pidx}"], tag="s2exp")
                                for half in range(2):
                                    kk = kp * 2 + half
                                    if kk >= qq * 4:
                                        delta = kk * 128 - qq * 512
                                        E.op("vector", lambda e, pidx=pidx, half=half, delta=delta: e.tensor_tensor(
                                            psb[pidx][:, half * 512:(half + 1) * 512],
                                            psb[pidx][:, half * 512:(half + 1) * 512],
                                            ctx.consts["stair"][:, 512 - delta:1024 - delta], ALU.mult),
                                            reads=[f"p{pidx}", "c_stair"], writes=[f"p{pidx}"], tag="s2msk")
                                pend.append((qq, kp, hh2))
                                if len(pend) > 1:
                                    flush_av2(pend.pop(0))
                        while pend:
                            flush_av2(pend.pop(0))
                        for hh2 in range(2):
                            h = rt * 4 + pp * 2 + hh2
                            r0 = hh2 * 64
                            avbank = 4 + hh2
                            E.op("scalar", lambda e, qq=qq, r0=r0, pp=pp, avbank=avbank: e.copy(
                                avsb[pp][r0:r0 + 64, tile_slice(qq, 512)], ctx.ps[avbank][0:64, :]),
                                reads=[f"ps{avbank}"], writes=[f"s2avb{pp}"], tag="s2avev")
                            E.op("scalar", lambda e, avbank=avbank: e.copy(
                                zstg[:, :], ctx.ps[avbank][64:65, :]),
                                reads=[f"ps{avbank}"], writes=["zstg"], tag="s2zev")
                            hloc = pp * 2 + hh2
                            E.dma("sync", zs_d[b, hloc, tile_slice(qq, 512)], zstg[:, :],
                                  reads=["zstg"], writes=[f"zs{hloc}{qq}"], tag="s2zst")

                E.dma("sync", zall[:], zs_d[b],
                      reads=[f"zs{hl}{qq}" for hl in range(4) for qq in range(4)],
                      writes=["zall"], tag="s2zld")
                E.op("vector", lambda e: e.reciprocal(zall[:], zall[:]), reads=["zall"], writes=["zall"])
                E.dma("sync", zs_d[b], zall[:], reads=["zall"], writes=["zrb"], tag="s2zrst")
                for pp in range(2):
                    for hh2 in range(2):
                        hloc = pp * 2 + hh2
                        for th in range(2):
                            tsl = slice(th * (T // 2), (th + 1) * (T // 2))
                            E.dma("sync", zbc[hh2 * 64:(hh2 + 1) * 64, :],
                                  zs_d[b, hloc:hloc + 1, tsl].broadcast_to([64, T // 2]),
                                  reads=["zrb"], writes=["zbc"], tag="s2zbc")
                            E.op("vector", lambda e, pp=pp, hh2=hh2, tsl=tsl: e.tensor_tensor(
                                avsb[pp][hh2 * 64:(hh2 + 1) * 64, tsl],
                                avsb[pp][hh2 * 64:(hh2 + 1) * 64, tsl],
                                zbc[hh2 * 64:(hh2 + 1) * 64, :], ALU.mult),
                                reads=[f"s2avb{pp}", "zbc"], writes=[f"s2avb{pp}"], tag="s2nrm")
                    pair_global = rt * 2 + pp
                    E.dma("sync", avn_d[b, tile_slice(pair_global), :], avsb[pp][:],
                          reads=[f"s2avb{pp}"], writes=[f"avn{pair_global}"], tag="s2avst")

            if S2_PHASES < 5:
                continue
            for tc in range(4):
                for ot in range(8):
                    for ht in range(8):
                        p = ht % 2
                        E.dma("sync", avst[p][:], avn_d[b, tile_slice(ht), :],
                              reads=[f"avn{ht}"],
                              writes=[f"s2avs{p}"] + [f"s2zt{p * 4 + j}" for j in range(4) if p * 4 + j < 6],
                              tag="s2ldav")
                        E.op("tensor", lambda e, tc=tc, ot=ot, ht=ht, p=p: e.matmul(
                            ctx.ps[7][:, :], owT[:, ht * D + ot * 128: ht * D + (ot + 1) * 128],
                            avst[p][:, tile_slice(tc, 512)], start=(ht == 0), stop=(ht == 7),
                            skip_group_check=True),
                            reads=[f"s2avs{p}", "ow"], writes=["ps7"], tag="s2omm")
                    E.op("scalar", lambda e: e.copy(sostg, ctx.ps[7][:, :]),
                         reads=["ps7"], writes=["s2te"], tag="s2oev")
                    for t2 in range(4):
                        tr = tc * 4 + t2
                        if ot == 0:
                            E.dma("sync", t1row[t2], xn_d[b, tile_slice(tr), :],
                                  writes=[f"t1r{t2}", t1alias[t2]], tag="s2ldxn")
                        E.op("tensor", lambda e, t2=t2: e.transpose(
                            ctx.ps[6][:, t2 * 128:(t2 + 1) * 128], sostg[:, tile_slice(t2)],
                            ctx.consts["ident_f32"][:]),
                            reads=["s2te", "c_ident_f32"], writes=["ps6"], tag="s2otr")
                    for t2 in range(4):
                        tr = tc * 4 + t2
                        osl = slice(ot * 128, (ot + 1) * 128)
                        E.op("vector", lambda e, t2=t2, osl=osl: e.tensor_tensor(
                            t1row[t2][:, osl], t1row[t2][:, osl],
                            ctx.ps[6][:, t2 * 128:(t2 + 1) * 128], ALU.add),
                            reads=["ps6", f"t1r{t2}"], writes=[f"t1r{t2}", t1alias[t2]], tag="s2res")
                        if ot == 7:
                            E.dma("sync", t1_d[b, tile_slice(tr), :], t1row[t2],
                                  reads=[f"t1r{t2}"], writes=[f"d_t1_{b}_{tr}"], tag="s2stt1")



def emit_s3(ctx):
    E = ctx.E
    I_ = 2048
    t1_d = ctx.dio("t1", (B_CORE, T, D), F32, producer="s2", consumers=("s3",))
    t2_d = ctx.dio("t2", (B_CORE, T, D), F32, producer="s3", consumers=("s4",))
    wgtT_d = ctx.dscratch("s3_wgtT", (B_CORE, 7, T), F32)

    ln2_w = ctx.din("w_L0_ffn_ln2_w", (D,), F32)
    router_w = ctx.din("w_L0_ffn_router_w", (7, D), F32)
    routing_bias = ctx.din("w_L0_ffn_routing_bias", (7,), F32)
    sh_g = ctx.din("w_L0_ffn_shared_gate", (1, I_, D), F32)
    sh_u = ctx.din("w_L0_ffn_shared_up", (1, I_, D), F32)
    sh_d = ctx.din("w_L0_ffn_shared_down", (1, D, I_), F32)
    ro_g = ctx.din("w_L0_ffn_routed_gate", (7, I_, D), F32)
    ro_u = ctx.din("w_L0_ffn_routed_up", (7, I_, D), F32)
    ro_d = ctx.din("w_L0_ffn_routed_down", (7, D, I_), F32)

    with ctx.stage("s3") as S:
        if S is None:
            return
        ln2w = ctx.sb(S, "ln2w", [128, 8], F32)
        E.dma("sync", ln2w[:], ln2_w.rearrange("(a p) -> p a", p=128), writes=["ln2w"],
              allow_slow_non_contiguous=True)
        rb = ctx.sb(S, "rb", [7, 1], F32)
        E.dma("sync", rb[:], routing_bias.rearrange("(a p) -> p a", p=7), writes=["rb"],
              allow_slow_non_contiguous=True)
        routerF = ctx.sb(S, "routerF", [128, 8 * 7], F32)
        for ht in range(8):
            E.dma("sync", routerF[:, ht * 7:(ht + 1) * 7],
                  router_w[:, tile_slice(ht)].rearrange("e p -> p e"), writes=["routerF"],
                  allow_slow_non_contiguous=True)
        rb_bc = ctx.sb(S, "rb_bc", [128, 7], F32)
        E.dma("sync", rb_bc[:], routing_bias.rearrange("(p a) -> p a", p=1).broadcast_to([128, 7]),
              writes=["rb_bc"])

        xrow = ctx.sb(S, "s3xr", [128, D], F32)
        sc = make_ln_scratch(ctx, "s3") if False else make_ln_scratch(ctx, S, "s3")
        xn2 = [ctx.sb(S, "s3x2", [128, T], BF16) for _ in range(8)]
        xstgT = ctx.sb(S, "s3xT", [128, 8 * 128], F32)
        prb = ctx.sb(S, "s3prb", [128, 7], F32)
        tp1 = ctx.sb(S, "s3tp1", [128, 7], F32)
        tp2 = ctx.sb(S, "s3tp2", [128, 7], F32)
        m1 = ctx.sb(S, "s3m1", [128, 1], F32)
        m2 = ctx.sb(S, "s3m2", [128, 1], F32)
        rden = ctx.sb(S, "s3rd", [128, 1], F32)
        wstg = ctx.sb(S, "s3wstg", [7, 128], F32)
        wgt_bc = ctx.sb(S, "s3wbc", [128, T], F32)
        wg_nat = ctx.sb(S, "s3wgn", [128, D], BF16)
        wgTi = ctx.sb(S, "s3wgT", [128, 2048], BF16)
        wuTi = ctx.sb(S, "s3wuT", [128, 2048], BF16)
        wdT = ctx.sb(S, "s3wdT", [128, 16 * 128], BF16)
        g_sb = ctx.sb(S, "s3g", [128, T], BF16)
        wu_nat = wg_nat
        wd_nat = g_sb
        hst = [ctx.sb(S, "s3h", [128, T], BF16) for _ in range(16)]
        acc = [ctx.sb(S, "s3acc", [128, T], F32) for _ in range(8)]
        t2row = [hst[i][:].bitcast(F32) for i in range(4)]
        t2alias = ["h0", "h1", "h2", "h3"]

        for b in range(B_CORE):
            for tt in range(T // 128):
                E.dma("sync", xrow[:], t1_d[b, tile_slice(tt), :], writes=["s3xr"], tag="s3ldx")
                emit_ln_tile(ctx, sc, xrow[:], "s3xr", xrow[:], "s3xr", "s3ln")
                for dt_i in range(8):
                    E.op("tensor", lambda e, dt_i=dt_i: e.transpose(
                        ctx.ps[6][:, 0:128], xrow[:, tile_slice(dt_i)], ctx.consts["ident_f32"][:]),
                        reads=["s3xr", "c_ident_f32"], writes=["ps6"])
                    E.op("scalar", lambda e, dt_i=dt_i, tt=tt: e.activation(
                        xn2[dt_i][:, tile_slice(tt)], ctx.ps[6][:, 0:128], AF.Identity,
                        scale=ln2w[:, dt_i:dt_i + 1]),
                        reads=["ps6", "ln2w"], writes=[f"x2_{dt_i}"])
                    E.op("scalar", lambda e, dt_i=dt_i: e.activation(
                        xstgT[:, tile_slice(dt_i)], ctx.ps[6][:, 0:128], AF.Identity,
                        scale=ln2w[:, dt_i:dt_i + 1]),
                        reads=["ps6", "ln2w"], writes=["xstgT"])
                for ht in range(8):
                    E.op("tensor", lambda e, ht=ht: e.matmul(
                        ctx.ps[0][:, 0:7], xstgT[:, tile_slice(ht)],
                        routerF[:, ht * 7:(ht + 1) * 7], start=(ht == 0), stop=(ht == 7),
                        skip_group_check=True),
                        reads=["xstgT", "routerF"], writes=["ps0"], tag="s3rmm")
                E.op("vector", lambda e: e.tensor_tensor(
                    prb[:], ctx.ps[0][:, 0:7], rb_bc[:], ALU.add),
                    reads=["ps0", "rb_bc"], writes=["prb"], tag="s3rb")
                E.op("scalar", lambda e: e.activation(
                    prb[:], prb[:], AF.Sigmoid),
                    reads=["prb"], writes=["prb"], tag="s3sig")

                E.op("vector", lambda e: e.tensor_reduce(m1[:].unsqueeze(2), prb[:].unsqueeze(1), AX.X, ALU.max),
                     reads=["prb"], writes=["m1"])
                E.op("vector", lambda e: e.tensor_scalar(tp1[:], prb[:], m1[:], None, ALU.is_ge),
                     reads=["prb", "m1"], writes=["tp1"])
                E.op("vector", lambda e: e.tensor_scalar(tp2[:], tp1[:], -1.0, 1.0, ALU.mult, ALU.add),
                     reads=["tp1"], writes=["tp2"])
                E.op("vector", lambda e: e.tensor_tensor(tp2[:], prb[:], tp2[:], ALU.mult),
                     reads=["prb", "tp2"], writes=["tp2"])
                E.op("vector", lambda e: e.tensor_reduce(m2[:].unsqueeze(2), tp2[:].unsqueeze(1), AX.X, ALU.max),
                     reads=["tp2"], writes=["m2"])
                E.op("vector", lambda e: e.tensor_scalar(tp2[:], tp2[:], m2[:], None, ALU.is_ge),
                     reads=["tp2", "m2"], writes=["tp2"])
                E.op("vector", lambda e: e.tensor_scalar(tp1[:], tp1[:], m1[:], None, ALU.mult),
                     reads=["tp1", "m1"], writes=["tp1"])
                E.op("vector", lambda e: e.scalar_tensor_tensor(
                    tp1[:], tp2[:], m2[:], tp1[:], ALU.mult, ALU.add),
                    reads=["tp1", "tp2", "m2"], writes=["tp1"])
                E.op("vector", lambda e: e.tensor_tensor(rden[:], m1[:], m2[:], ALU.add),
                     reads=["m1", "m2"], writes=["rden"])
                E.op("vector", lambda e: e.reciprocal(rden[:], rden[:]),
                     reads=["rden"], writes=["rden"])
                E.op("vector", lambda e: e.tensor_scalar(tp1[:], tp1[:], rden[:], None, ALU.mult),
                     reads=["tp1", "rden"], writes=["tp1"])
                E.op("tensor", lambda e: e.transpose(
                    ctx.ps[5][0:7, 0:128], tp1[:], ctx.consts["ident_f32"][:]),
                    reads=["tp1", "c_ident_f32"], writes=["ps5"], tag="s3wtr")
                E.op("scalar", lambda e: e.copy(wstg[:], ctx.ps[5][0:7, 0:128]),
                     reads=["ps5"], writes=["wstg"])
                E.dma("sync", wgtT_d[b, :, tile_slice(tt)], wstg[:],
                      reads=["wstg"], writes=[f"wgtT_{tt}"], tag="s3wst")

            for e_i in range(8):
                shared = (e_i == 7)
                if not shared:
                    E.dma("sync", wgt_bc[:],
                          wgtT_d[b, e_i:e_i + 1, :].broadcast_to([128, T]),
                          reads=[f"wgtT_{tt}" for tt in range(16)], writes=["wbc"], tag="s3wbc")
                gsrc = sh_g[0] if shared else ro_g[e_i]
                usrc = sh_u[0] if shared else ro_u[e_i]
                dsrc = sh_d[0] if shared else ro_d[e_i]
                for ii in range(16):
                    tb = 6 - 2 * (ii % 2)
                    wgT_b = wgTi if ii % 2 == 0 else wuTi
                    kwg = "wgTi" if ii % 2 == 0 else "wuTi"
                    E.dma("gpsimd", wg_nat[:], gsrc[tile_slice(ii), :], writes=["wgn"], tag="s3ldg")
                    for ht in range(8):
                        E.op("tensor", lambda e, ht=ht, tb=tb: e.transpose(
                            ctx.psbf6(tb, ht), wg_nat[:, tile_slice(ht)], ctx.consts["ident_bf"][:]),
                            reads=["wgn", "c_ident_bf"], writes=[f"ps{tb}"])
                    E.op("vector", lambda e, tb=tb, wgT_b=wgT_b: e.tensor_copy(
                        wgT_b[:, 0:1024], ctx.ps[tb].bitcast(BF16)),
                        reads=[f"ps{tb}"], writes=[kwg])
                    E.dma("gpsimd", wu_nat[:], usrc[tile_slice(ii), :], writes=["wgn"], tag="s3ldu")
                    for ht in range(8):
                        E.op("tensor", lambda e, ht=ht, tb=tb: e.transpose(
                            ctx.psbf6(tb + 1, ht), wu_nat[:, tile_slice(ht)], ctx.consts["ident_bf"][:]),
                            reads=["wgn", "c_ident_bf"], writes=[f"ps{tb + 1}"])
                    E.op("vector", lambda e, tb=tb, wgT_b=wgT_b: e.tensor_copy(
                        wgT_b[:, 1024:2048], ctx.ps[tb + 1].bitcast(BF16)),
                        reads=[f"ps{tb + 1}"], writes=[kwg])
                    for tc in range(4):
                        gb = tc % 2
                        ub = 2 + tc % 2
                        for ht in range(8):
                            E.op("tensor", lambda e, tc=tc, ht=ht, gb=gb, wgT_b=wgT_b: e.matmul(
                                ctx.ps[gb][:, :], wgT_b[:, tile_slice(ht)],
                                xn2[ht][:, tile_slice(tc, 512)], start=(ht == 0), stop=(ht == 7),
                                skip_group_check=True),
                                reads=[kwg, f"x2_{ht}"], writes=[f"ps{gb}"], tag="s3gmm")
                            E.op("tensor", lambda e, tc=tc, ht=ht, ub=ub, wgT_b=wgT_b: e.matmul(
                                ctx.ps[ub][:, :], wgT_b[:, 1024 + ht * 128:1024 + (ht + 1) * 128],
                                xn2[ht][:, tile_slice(tc, 512)], start=(ht == 0), stop=(ht == 7),
                                skip_group_check=True),
                                reads=[kwg, f"x2_{ht}"], writes=[f"ps{ub}"], tag="s3umm")
                        E.op("scalar", lambda e, tc=tc, gb=gb: e.activation(
                            g_sb[:, tile_slice(tc, 512)], ctx.ps[gb][:, :], AF.Silu),
                            reads=[f"ps{gb}"], writes=["g_sb"], tag="s3silu")
                        E.op("vector", lambda e, ii=ii, tc=tc, ub=ub: e.tensor_tensor(
                            hst[ii][:, tile_slice(tc, 512)], g_sb[:, tile_slice(tc, 512)],
                            ctx.ps[ub][:, :], ALU.mult),
                            reads=["g_sb", f"ps{ub}"], writes=[f"h{ii}"], tag="s3gu")
                    if not shared:
                        E.op("vector", lambda e, ii=ii: e.tensor_tensor(
                            hst[ii][:], hst[ii][:], wgt_bc[:], ALU.mult),
                            reads=[f"h{ii}", "wbc"], writes=[f"h{ii}"], tag="s3hw")
                for ot in range(8):
                    E.dma("gpsimd", wd_nat[:], dsrc[tile_slice(ot), :], writes=["g_sb"], tag="s3ldd")
                    for ii in range(16):
                        bk = 2 + ii // 8
                        E.op("tensor", lambda e, ii=ii, bk=bk: e.transpose(
                            ctx.psbf6(bk, ii % 8), wd_nat[:, tile_slice(ii)], ctx.consts["ident_bf"][:]),
                            reads=["g_sb", "c_ident_bf"], writes=[f"ps{bk}"])
                    for bk in (2, 3):
                        E.op("vector", lambda e, bk=bk: e.tensor_copy(
                            wdT[:, (bk - 2) * 1024:(bk - 1) * 1024], ctx.ps[bk].bitcast(BF16)),
                            reads=[f"ps{bk}"], writes=["wdT"])
                    for tc in range(4):
                        bank = tc % 2
                        for ii in range(16):
                            E.op("tensor", lambda e, tc=tc, ii=ii, bank=bank: e.matmul(
                                ctx.ps[bank][:, :], wdT[:, tile_slice(ii)],
                                hst[ii][:, tile_slice(tc, 512)], start=(ii == 0), stop=(ii == 15),
                                skip_group_check=True),
                                reads=["wdT", f"h{ii}"], writes=[f"ps{bank}"], tag="s3dmm")
                        if e_i == 0:
                            E.op("scalar", lambda e, ot=ot, tc=tc, bank=bank: e.copy(
                                acc[ot][:, tile_slice(tc, 512)], ctx.ps[bank][:, :]),
                                reads=[f"ps{bank}"], writes=[f"acc{ot}"], tag="s3acc0")
                        else:
                            E.op("vector", lambda e, ot=ot, tc=tc, bank=bank: e.tensor_tensor(
                                acc[ot][:, tile_slice(tc, 512)], acc[ot][:, tile_slice(tc, 512)],
                                ctx.ps[bank][:, :], ALU.add),
                                reads=[f"ps{bank}", f"acc{ot}"], writes=[f"acc{ot}"], tag="s3acc")

            for tc in range(4):
                for ot in range(8):
                    bk = 4 + ot % 2
                    for t2_ in range(4):
                        tr = tc * 4 + t2_
                        if ot == 0:
                            E.dma("sync", t2row[t2_], t1_d[b, tile_slice(tr), :],
                                  writes=[f"t2r{t2_}", t2alias[t2_]], tag="s3ldt1")
                        E.op("tensor", lambda e, ot=ot, tc=tc, t2_=t2_, bk=bk: e.transpose(
                            ctx.ps[bk][:, t2_ * 128:(t2_ + 1) * 128],
                            acc[ot][:, tc * 512 + t2_ * 128: tc * 512 + (t2_ + 1) * 128],
                            ctx.consts["ident_f32"][:]),
                            reads=[f"acc{ot}", "c_ident_f32"], writes=[f"ps{bk}"], tag="s3ttr")
                    for t2_ in range(4):
                        tr = tc * 4 + t2_
                        E.op("vector", lambda e, ot=ot, t2_=t2_, bk=bk: e.tensor_tensor(
                            t2row[t2_][:, tile_slice(ot)], t2row[t2_][:, tile_slice(ot)],
                            ctx.ps[bk][:, t2_ * 128:(t2_ + 1) * 128], ALU.add),
                            reads=[f"ps{bk}", f"t2r{t2_}"], writes=[f"t2r{t2_}", t2alias[t2_]], tag="s3res")
                        if ot == 7:
                            E.dma("sync", t2_d[b, tile_slice(tr), :], t2row[t2_],
                                  reads=[f"t2r{t2_}"], writes=[f"d_t2_{b}_{tr}"], tag="s3stt2")



def emit_s4(ctx):
    E = ctx.E
    x_d = ctx.din("x", (B_CORE, T, D), F32)
    s0_d = ctx.dio("s0", (B_CORE, T, D), F32, producer="s1", consumers=("s4",))
    t2_d = ctx.dio("t2", (B_CORE, T, D), F32, producer="s3", consumers=("s4",))
    x1_d = ctx.dio("x1", (B_CORE, T, D), F32, producer="s4", consumers=("s7",))
    xn1_d = ctx.dio("xn1", (B_CORE, T, D), F32, producer="s4", consumers=("s6",))
    xn1T_d = ctx.dio("xn1T", (B_CORE, D, T), BF16, producer="s4", consumers=("s5",))

    with ctx.stage("s4") as S:
        if S is None:
            return
        xt = ctx.sb(S, "s4x", [128, D], F32)
        st_ = ctx.sb(S, "s4s", [128, D], F32)
        gt = ctx.sb(S, "s4g", [128, D], F32)
        stg = [ctx.sb(S, "s4stg", [128, 512], BF16) for _ in range(2)]
        sc = make_ln_scratch(ctx, S, "s4")

        for b in range(B_CORE):
            for tt in range(T // 128):
                E.dma("sync", xt[:], x_d[b, tile_slice(tt), :], writes=["s4x"], tag="s4ldx")
                E.dma("sync", st_[:], s0_d[b, tile_slice(tt), :], writes=["s4s"], tag="s4lds")
                E.dma("sync", gt[:], t2_d[b, tile_slice(tt), :], writes=["s4g"], tag="s4ldt")
                E.op("scalar", lambda e: e.activation(gt[:], gt[:], AF.Silu),
                     reads=["s4g"], writes=["s4g"], tag="s4silu")
                E.op("vector", lambda e: e.tensor_tensor(gt[:], gt[:], st_[:], ALU.mult),
                     reads=["s4g", "s4s"], writes=["s4g"], tag="s4sg")
                E.op("vector", lambda e: e.tensor_tensor(xt[:], xt[:], gt[:], ALU.add),
                     reads=["s4x", "s4g"], writes=["s4x"], tag="s4add")
                E.dma("sync", x1_d[b, tile_slice(tt), :], xt[:],
                      reads=["s4x"], writes=[f"d_x1_{b}_{tt}"], tag="s4stx1")
                emit_ln_tile(ctx, sc, xt[:], "s4x", gt[:], "s4n", "s4ln")
                E.dma("sync", xn1_d[b, tile_slice(tt), :], gt[:],
                      reads=["s4n"], writes=[f"d_xn1_{b}_{tt}"], tag="s4stn")
                for dt_i in range(8):
                    bk = 6 + dt_i // 4
                    E.op("tensor", lambda e, dt_i=dt_i, bk=bk: e.transpose(
                        ctx.ps[bk][:, (dt_i % 4) * 128:(dt_i % 4 + 1) * 128],
                        gt[:, tile_slice(dt_i)], ctx.consts["ident_f32"][:]),
                        reads=["s4n", "c_ident_f32"], writes=[f"ps{bk}"])
                for gg in range(2):
                    bk = 6 + gg
                    E.op("scalar", lambda e, gg=gg, bk=bk: e.copy(stg[gg][:], ctx.ps[bk][:, :]),
                         reads=[f"ps{bk}"], writes=[f"s4stg{gg}"])
                    E.dma("sync",
                          xn1T_d[b, gg * 512:(gg + 1) * 512, tile_slice(tt)]
                          .rearrange("(a p) c -> p a c", p=128),
                          stg[gg][:].rearrange("p (a c) -> p a c", c=128),
                          reads=[f"s4stg{gg}"], writes=[f"d_xn1T_{b}_{gg}_{tt}"], tag="s4stT")



def emit_s5(ctx):
    E = ctx.E
    DI = 2048
    xn1T_d = ctx.dio("xn1T", (B_CORE, D, T), BF16, producer="s4", consumers=("s5",))
    s1_d = ctx.dio("s1", (B_CORE, T, D), F32, producer="s5", consumers=("s7",))
    xbc_d = ctx.dscratch("s5_xbc", (B_CORE, DI, T), BF16)
    z_d = ctx.dscratch("s5_z", (B_CORE, DI, T), BF16)
    dt_d = ctx.dscratch("s5_dt", (B_CORE, DI, T), F32)
    dtu_d = ctx.dscratch("s5_dtu", (B_CORE, DI, T), BF16)
    bc_d = ctx.dscratch("s5_bc", (B_CORE, 32, T), BF16)
    yg_d = ctx.dscratch("s5_yg", (B_CORE, DI, T), BF16)

    w_ip = ctx.din("w_L1_attn_in_proj_w", (D, 2 * DI), F32)
    w_cv = ctx.din("w_L1_attn_conv_w", (DI, 4), F32)
    b_cv = ctx.din("w_L1_attn_conv_b", (DI,), F32)
    w_xp = ctx.din("w_L1_attn_x_proj_w", (DI, 96), F32)
    w_dt = ctx.din("w_L1_attn_dt_proj_w", (64, DI), F32)
    b_dt = ctx.din("w_L1_attn_dt_proj_b", (DI,), F32)
    a_log = ctx.din("w_L1_attn_A_log", (DI, 16), F32)
    d_par = ctx.din("w_L1_attn_D_param", (DI,), F32)
    w_op = ctx.din("w_L1_attn_out_proj_w", (DI, D), F32)

    with ctx.stage("s5") as S:
        if S is None:
            return
        cvw = ctx.sb(S, "cvw", [128, 16 * 4], F32)
        E.dma("sync", cvw[:].rearrange("p (d c) -> p d c", d=16),
              w_cv.rearrange("(d p) c -> p d c", p=128), writes=["cvw"])
        cvb = ctx.sb(S, "cvb", [128, 16], F32)
        E.dma("sync", cvb[:], b_cv.rearrange("(d p) -> p d", p=128), writes=["cvb"],
              allow_slow_non_contiguous=True)
        dtb = ctx.sb(S, "dtb", [128, 16], F32)
        E.dma("sync", dtb[:], b_dt.rearrange("(d p) -> p d", p=128), writes=["dtb"],
              allow_slow_non_contiguous=True)
        dpar = ctx.sb(S, "dpar", [128, 16], F32)
        E.dma("sync", dpar[:], d_par.rearrange("(d p) -> p d", p=128), writes=["dpar"],
              allow_slow_non_contiguous=True)
        asb = ctx.sb(S, "asb", [128, 256], F32)
        E.dma("sync", asb[:].rearrange("p (d n) -> p d n", d=16),
              a_log.rearrange("(d p) n -> p d n", p=128), writes=["asb"])
        E.op("scalar", lambda e: e.activation(asb[:], asb[:], AF.Exp), reads=["asb"], writes=["asb"])
        E.op("vector", lambda e: e.tensor_scalar(asb[:], asb[:], -1.0, None, ALU.mult),
             reads=["asb"], writes=["asb"])
        xpw = ctx.sb(S, "xpw", [128, 16 * 96], BF16)
        E.dma("gpsimd", xpw[:].rearrange("p (d c) -> p d c", d=16),
              w_xp.rearrange("(d p) c -> p d c", p=128), writes=["xpw"])
        dtw = ctx.sb(S, "dtw", [64, DI], BF16)
        E.dma("gpsimd", dtw[:], w_dt[:, :], writes=["dtw"])

        xnt8 = [ctx.sb(S, "s5xnt", [128, T], BF16) for _ in range(8)]
        wip = [ctx.sb(S, "s5wip", [128, 8 * 128], BF16) for _ in range(2)]
        xbrow = ctx.sb(S, "s5xbr", [128, T + 4], BF16)
        cacc = ctx.sb(S, "s5cacc", [128, T], F32)
        xbc = ctx.sb(S, "s5xbc", [128, T], BF16)
        dtraw = ctx.sb(S, "s5dtr", [64, T], BF16)
        bcsb = ctx.sb(S, "s5bc", [32, T], BF16)
        etmp = ctx.sb(S, "s5et", [128, 512], F32)
        dts = ctx.sb(S, "s5dts", [128, 512], F32)
        dtu = ctx.sb(S, "s5dtu", [128, 512], BF16)
        xbcc = ctx.sb(S, "s5xbcc", [128, 512], BF16)
        SEG = 1024
        dtt = [ctx.sb(S, "s5dtt", [128, SEG], F32) for _ in range(2)]
        dut = [ctx.sb(S, "s5dut", [128, SEG], BF16) for _ in range(2)]
        bbc = [ctx.sb(S, "s5bbc", [128, SEG], BF16) for _ in range(4)]
        cbc = [ctx.sb(S, "s5cbc", [128, SEG], BF16) for _ in range(4)]
        da = [ctx.sb(S, "s5da", [128, SEG], F32) for _ in range(4)]
        dbu = [ctx.sb(S, "s5dbu", [128, SEG], BF16) for _ in range(4)]
        hbuf = [ctx.sb(S, "s5h", [128, SEG], BF16) for _ in range(4)]
        prods = [ctx.sb(S, "s5pr", [128, SEG], BF16) for _ in range(4)]
        gld = [ctx.sb(S, "s5gl", [128, SEG], BF16) for _ in range(2)]
        state = ctx.sb(S, "s5st", [128, 256], F32)
        zt_s = ctx.sb(S, "s5zt", [128, SEG], F32)
        ygt = ctx.sb(S, "s5yg", [128, SEG], BF16)
        ygl = [ctx.sb(S, "s5ygl", [128, 512], BF16) for _ in range(16)]
        sost = ctx.sb(S, "s5so", [128, 512], F32)
        trst = [ctx.sb(S, "s5tr", [128, 512], F32) for _ in range(2)]

        for b in range(B_CORE):
            for ht in range(8):
                E.dma("sync", xnt8[ht][:], xn1T_d[b, tile_slice(ht), :],
                      writes=[f"s5xnt{ht}"], tag="s5ldx")

            for dt_i in range(16):
                E.dma("gpsimd", wip[dt_i % 2][:].rearrange("p (h c) -> p h c", h=8),
                      w_ip[:, tile_slice(dt_i)].rearrange("(h p) c -> p h c", p=128),
                      writes=[f"wip{dt_i % 2}"], tag="s5ldw")
                for tc in range(4):
                    bank = tc % 2
                    for ht in range(8):
                        E.op("tensor", lambda e, dt_i=dt_i, tc=tc, ht=ht, bank=bank: e.matmul(
                            ctx.ps[bank][:, :], wip[dt_i % 2][:, tile_slice(ht)],
                            xnt8[ht][:, tile_slice(tc, 512)], start=(ht == 0), stop=(ht == 7),
                            skip_group_check=True),
                            reads=[f"s5xnt{ht}", f"wip{dt_i % 2}"], writes=[f"ps{bank}"], tag="s5ipmm")
                    E.op("scalar", lambda e, tc=tc, bank=bank: e.copy(
                        xbrow[:, 4 + tc * 512: 4 + (tc + 1) * 512], ctx.ps[bank][:, :]),
                        reads=[f"ps{bank}"], writes=["xbrow"], tag="s5ipev")
                E.op("vector", lambda e: e.memset(xbrow[:, 0:4], 0.0), writes=["xbrow"])
                E.op("vector", lambda e, dt_i=dt_i: e.tensor_scalar(
                    cacc[:], xbrow[:, 1:1 + T], cvw[:, dt_i * 4:dt_i * 4 + 1], None, ALU.mult),
                    reads=["xbrow", "cvw"], writes=["cacc"], tag="s5cv0")
                for j in range(1, 4):
                    E.op("vector", lambda e, j=j, dt_i=dt_i: e.scalar_tensor_tensor(
                        cacc[:], xbrow[:, 1 + j:1 + j + T], cvw[:, dt_i * 4 + j:dt_i * 4 + j + 1],
                        cacc[:], ALU.mult, ALU.add),
                        reads=["xbrow", "cvw", "cacc"], writes=["cacc"], tag="s5cv")
                E.op("scalar", lambda e, dt_i=dt_i: e.activation(
                    xbc[:], cacc[:], AF.Silu, bias=cvb[:, dt_i:dt_i + 1]),
                    reads=["cacc", "cvb"], writes=["xbc"], tag="s5cvs")
                E.dma("sync", xbc_d[b, tile_slice(dt_i), :], xbc[:],
                      reads=["xbc"], writes=[f"xbc_{dt_i}"], tag="s5stxbc")
                for tc in range(4):
                    E.op("tensor", lambda e, dt_i=dt_i, tc=tc: e.matmul(
                        ctx.ps[2 + tc][0:96, :], xpw[:, dt_i * 96:(dt_i + 1) * 96],
                        xbc[:, tile_slice(tc, 512)], start=(dt_i == 0), stop=(dt_i == 15),
                        skip_group_check=True),
                        reads=["xbc", "xpw"], writes=[f"ps{2 + tc}"], tag="s5xpmm")
            for dt_i in range(16):
                E.dma("gpsimd", wip[dt_i % 2][:].rearrange("p (h c) -> p h c", h=8),
                      w_ip[:, 2048 + dt_i * 128: 2048 + (dt_i + 1) * 128].rearrange("(h p) c -> p h c", p=128),
                      writes=[f"wip{dt_i % 2}"], tag="s5ldwz")
                for tc in range(4):
                    bank = tc % 2
                    for ht in range(8):
                        E.op("tensor", lambda e, tc=tc, ht=ht, bank=bank, dt_i=dt_i: e.matmul(
                            ctx.ps[bank][:, :], wip[dt_i % 2][:, tile_slice(ht)],
                            xnt8[ht][:, tile_slice(tc, 512)], start=(ht == 0), stop=(ht == 7),
                            skip_group_check=True),
                            reads=[f"s5xnt{ht}", f"wip{dt_i % 2}"], writes=[f"ps{bank}"], tag="s5zmm")
                    E.op("scalar", lambda e, tc=tc, bank=bank: e.copy(
                        xbc[:, tile_slice(tc, 512)], ctx.ps[bank][:, :]),
                        reads=[f"ps{bank}"], writes=["xbc"], tag="s5zev")
                E.dma("sync", z_d[b, tile_slice(dt_i), :], xbc[:],
                      reads=["xbc"], writes=[f"z_{dt_i}"], tag="s5stz")
            for tc in range(4):
                E.op("scalar", lambda e, tc=tc: e.copy(
                    dtraw[:, tile_slice(tc, 512)], ctx.ps[2 + tc][0:64, :]),
                    reads=[f"ps{2 + tc}"], writes=["dtraw"], tag="s5xpev")
                E.op("scalar", lambda e, tc=tc: e.copy(
                    bcsb[:, tile_slice(tc, 512)], ctx.ps[2 + tc][64:96, :]),
                    reads=[f"ps{2 + tc}"], writes=["bcsb"], tag="s5bcev")
            E.dma("sync", bc_d[b], bcsb[:], reads=["bcsb"], writes=["bc_dram"], tag="s5stbc")

            for dt_i in range(16):
                for tc in range(4):
                    E.op("tensor", lambda e, dt_i=dt_i, tc=tc: e.matmul(
                        ctx.ps[0][:, :], dtw[:, tile_slice(dt_i)],
                        dtraw[:, tile_slice(tc, 512)], start=True, stop=True,
                        skip_group_check=True),
                        reads=["dtraw", "dtw"], writes=["ps0"], tag="s5dtmm")
                    E.op("scalar", lambda e, dt_i=dt_i: e.activation(
                        etmp[:], ctx.ps[0][:, :], AF.Exp, bias=dtb[:, dt_i:dt_i + 1]),
                        reads=["ps0", "dtb"], writes=["etmp"], tag="s5sp1")
                    one_t, kone = ctx.sconst(1.0)
                    E.op("scalar", lambda e: e.activation(
                        dts[:], etmp[:], AF.Ln, bias=one_t[:, :]),
                        reads=["etmp", kone], writes=["dts"], tag="s5sp2")
                    E.dma("sync", dt_d[b, tile_slice(dt_i), tile_slice(tc, 512)], dts[:],
                          reads=["dts"], writes=[f"dt_{dt_i}"], tag="s5stdt")
                    E.dma("sync", xbcc[:], xbc_d[b, tile_slice(dt_i), tile_slice(tc, 512)],
                          reads=[f"xbc_{dt_i}"], writes=["xbcc"], tag="s5ldxbc")
                    E.op("vector", lambda e: e.tensor_tensor(dtu[:], dts[:], xbcc[:], ALU.mult),
                         reads=["dts", "xbcc"], writes=["dtu"], tag="s5dtu")
                    E.dma("sync", dtu_d[b, tile_slice(dt_i), tile_slice(tc, 512)], dtu[:],
                          reads=["dtu"], writes=[f"dtu_{dt_i}"], tag="s5stdtu")

            E.op("vector", lambda e: e.memset(state[:], 0.0),
                 writes=["state"] + [f"state{i}" for i in range(256)])
            for seg in range(2):
                sgsl = slice(seg * SEG, (seg + 1) * SEG)
                for dtg in range(4):
                    for nblk in range(4):
                        for nl in range(4):
                            n = nblk * 4 + nl
                            E.dma("sync", bbc[nl][:], bc_d[b, n:n + 1, sgsl].broadcast_to([128, SEG]),
                                  reads=["bc_dram"], writes=[f"bbc{nl}"], tag="s5bbc")
                            E.dma("sync", cbc[nl][:], bc_d[b, 16 + n:17 + n, sgsl].broadcast_to([128, SEG]),
                                  reads=["bc_dram"], writes=[f"cbc{nl}"], tag="s5cbc")
                        for dtl in range(4):
                            dt_i = dtg * 4 + dtl
                            pb_ = dt_i % 2
                            E.dma("sync", dtt[pb_][:], dt_d[b, tile_slice(dt_i), sgsl],
                                  reads=[f"dt_{dt_i}"], writes=[f"dtt{pb_}"], tag="s5lddt")
                            E.dma("sync", dut[pb_][:], dtu_d[b, tile_slice(dt_i), sgsl],
                                  reads=[f"dtu_{dt_i}"], writes=[f"dut{pb_}"], tag="s5lddtu")
                            for nl in range(4):
                                n = nblk * 4 + nl
                                pp = n % 4
                                scol = dt_i * 16 + n
                                E.op("scalar", lambda e, pb_=pb_, pp=pp, scol=scol: e.activation(
                                    da[pp][:], dtt[pb_][:], AF.Exp, scale=asb[:, scol:scol + 1]),
                                    reads=[f"dtt{pb_}", "asb"], writes=[f"da{pp}"], tag="s5da")
                                E.op("vector", lambda e, pb_=pb_, pp=pp, nl=nl: e.tensor_tensor(
                                    dbu[pp][:], dut[pb_][:], bbc[nl][:], ALU.mult),
                                    reads=[f"dut{pb_}", f"bbc{nl}"], writes=[f"dbu{pp}"], tag="s5dbu")
                                E.op("vector", lambda e, pp=pp, scol=scol: e.tensor_tensor_scan(
                                    hbuf[pp][:], da[pp][:], dbu[pp][:], state[:, scol:scol + 1],
                                    ALU.mult, ALU.add),
                                    reads=[f"da{pp}", f"dbu{pp}", f"state{scol}"], writes=[f"h{pp}"], tag="s5tts")
                                E.op("vector", lambda e, pp=pp, scol=scol: e.tensor_copy(
                                    state[:, scol:scol + 1], hbuf[pp][:, SEG - 1:SEG]),
                                    reads=[f"h{pp}"], writes=[f"state{scol}"], tag="s5stsv")
                                E.op("vector", lambda e, pp=pp, nl=nl: e.tensor_tensor(
                                    prods[pp][:], hbuf[pp][:], cbc[nl][:], ALU.mult),
                                    reads=[f"h{pp}", f"cbc{nl}"], writes=[f"pr{pp}"], tag="s5prod")
                                first = (nblk == 0 and nl == 0)
                                last = (nblk == 3 and nl == 3)
                                for tc2 in range(2):
                                    bank = dtl * 2 + tc2
                                    E.op("tensor", lambda e, pp=pp, tc2=tc2, bank=bank, first=first, last=last: e.matmul(
                                        ctx.ps[bank][:, :], ctx.consts["ident_bf"][:],
                                        prods[pp][:, tile_slice(tc2, 512)], start=first, stop=last,
                                        skip_group_check=True),
                                        reads=[f"pr{pp}", "c_ident_bf"], writes=[f"ps{bank}"], tag="s5yacc")
                        if nblk == 3:
                            for dtl in range(4):
                                dt_i = dtg * 4 + dtl
                                E.dma("sync", gld[0][:], xbc_d[b, tile_slice(dt_i), sgsl],
                                      reads=[f"xbc_{dt_i}"], writes=["gld0"], tag="s5gxbc")
                                E.dma("sync", gld[1][:], z_d[b, tile_slice(dt_i), sgsl],
                                      reads=[f"z_{dt_i}"], writes=["gld1"], tag="s5gz")
                                E.op("scalar", lambda e: e.activation(zt_s[:], gld[1][:], AF.Silu),
                                     reads=["gld1"], writes=["zt_s"], tag="s5gsilu")
                                E.op("vector", lambda e, dtl=dtl, dt_i=dt_i: e.scalar_tensor_tensor(
                                    zt_s[:] if False else gld[0][:], gld[0][:], dpar[:, dt_i:dt_i + 1],
                                    ctx.ps_all[:, dtl * 1024:(dtl + 1) * 1024], ALU.mult, ALU.add),
                                    reads=["gld0", "dpar", f"ps{dtl * 2}", f"ps{dtl * 2 + 1}"],
                                    writes=["gld0"], tag="s5gyd")
                                E.op("vector", lambda e: e.tensor_tensor(ygt[:], gld[0][:], zt_s[:], ALU.mult),
                                     reads=["gld0", "zt_s"], writes=["ygt"], tag="s5gmul")
                                E.dma("sync", yg_d[b, tile_slice(dt_i), sgsl], ygt[:],
                                      reads=["ygt"], writes=[f"yg_{dt_i}"], tag="s5styg")

            for tc in range(4):
                for dt_i in range(16):
                    E.dma("sync", ygl[dt_i][:], yg_d[b, tile_slice(dt_i), tile_slice(tc, 512)],
                          reads=[f"yg_{dt_i}"], writes=[f"ygl{dt_i}"], tag="s5ldyg")
                for ot in range(8):
                    wop = dtt[ot % 2][:].bitcast(BF16)
                    E.dma("gpsimd", wop.rearrange("p (d c) -> p d c", d=16),
                          w_op[:, tile_slice(ot)].rearrange("(d p) c -> p d c", p=128),
                          writes=[f"dtt{ot % 2}"], tag="s5ldop")
                    for dt_i in range(16):
                        E.op("tensor", lambda e, ot=ot, dt_i=dt_i, tc=tc, wop=wop: e.matmul(
                            ctx.ps[0][:, :], wop[:, tile_slice(dt_i)],
                            ygl[dt_i][:], start=(dt_i == 0), stop=(dt_i == 15),
                            skip_group_check=True),
                            reads=[f"ygl{dt_i}", f"dtt{ot % 2}"], writes=["ps0"], tag="s5opmm")
                    E.op("scalar", lambda e: e.copy(sost[:], ctx.ps[0][:, :]),
                         reads=["ps0"], writes=["sost"], tag="s5opev")
                    for t2 in range(4):
                        E.op("tensor", lambda e, t2=t2: e.transpose(
                            ctx.ps[1][:, t2 * 128:(t2 + 1) * 128], sost[:, tile_slice(t2)],
                            ctx.consts["ident_f32"][:]),
                            reads=["sost", "c_ident_f32"], writes=["ps1"], tag="s5str")
                    pr_ = ot % 2
                    E.op("scalar", lambda e, pr_=pr_: e.copy(trst[pr_][:], ctx.ps[1][:, :]),
                         reads=["ps1"], writes=[f"trst{pr_}"], tag="s5strev")
                    E.dma("sync", s1_d[b, tc * 512:(tc + 1) * 512, tile_slice(ot)]
                          .rearrange("(a p) c -> p a c", p=128),
                          trst[pr_][:].rearrange("p (a c) -> p a c", c=128),
                          reads=[f"trst{pr_}"], writes=[f"d_s1_{b}_{ot}_{tc}"], tag="s5sts1")



FP4_MIDS = (0.25, 0.75, 1.25, 1.75, 2.5, 3.5, 5.0)
FP4_DELT = (0.5, 0.5, 0.5, 0.5, 1.0, 1.0, 2.0)


def emit_fp4_quant(ctx, out_ap, kout, in_ap, kin, width, qs, tag):
    E = ctx.E
    nb = width // 16
    mx, scl, ybuf, acc, term, sgn = qs["mx"], qs["scl"], qs["y"], qs["acc"], qs["term"], qs["sgn"]
    kq = qs["key"]
    E.op("vector", lambda e: e.tensor_reduce(
        mx[:, 0:nb].unsqueeze(2), in_ap.rearrange("p (a b) -> p a b", b=16), AX.X, ALU.max,
        apply_absolute_value=True),
        reads=[kin], writes=[kq + "mx"], tag=tag + "mx")
    E.op("vector", lambda e: e.tensor_scalar(
        scl[:, 0:nb], mx[:, 0:nb], 1.0 / 6.0, 1e-12, ALU.mult, ALU.max),
        reads=[kq + "mx"], writes=[kq + "scl"], tag=tag + "scl")
    E.op("vector", lambda e: e.reciprocal(mx[:, 0:nb], scl[:, 0:nb]),
         reads=[kq + "scl"], writes=[kq + "mx"], tag=tag + "rcp")
    rcp_view = mx[:, 0:nb].unsqueeze(2).broadcast_to([128, nb, 16])
    E.op("scalar", lambda e: e.activation(
        ybuf[:, 0:width], in_ap, AF.Abs),
        reads=[kin], writes=[kq + "y"], tag=tag + "abs")
    E.op("vector", lambda e: e.tensor_tensor(
        ybuf[:, 0:width].rearrange("p (a b) -> p a b", b=16),
        ybuf[:, 0:width].rearrange("p (a b) -> p a b", b=16), rcp_view, ALU.mult),
        reads=[kq + "y", kq + "mx"], writes=[kq + "y"], tag=tag + "y")
    E.op("vector", lambda e: e.tensor_scalar(
        acc[:, 0:width], ybuf[:, 0:width], FP4_MIDS[0], FP4_DELT[0], ALU.is_gt, ALU.mult),
        reads=[kq + "y"], writes=[kq + "acc"], tag=tag + "q0")
    for j in range(1, 7):
        E.op("vector", lambda e, j=j: e.tensor_scalar(
            term[:, 0:width], ybuf[:, 0:width], FP4_MIDS[j], FP4_DELT[j], ALU.is_gt, ALU.mult),
            reads=[kq + "y"], writes=[kq + "t"], tag=tag + f"q{j}")
        E.op("vector", lambda e: e.tensor_tensor(
            acc[:, 0:width], acc[:, 0:width], term[:, 0:width], ALU.add),
            reads=[kq + "acc", kq + "t"], writes=[kq + "acc"], tag=tag + f"a{j}")
    E.op("scalar", lambda e: e.activation(sgn[:, 0:width], in_ap, AF.Sign),
         reads=[kin], writes=[kq + "sgn"], tag=tag + "sgn")
    scl_view = scl[:, 0:nb].unsqueeze(2).broadcast_to([128, nb, 16])
    E.op("vector", lambda e: e.tensor_tensor(
        sgn[:, 0:width].rearrange("p (a b) -> p a b", b=16),
        sgn[:, 0:width].rearrange("p (a b) -> p a b", b=16), scl_view, ALU.mult),
        reads=[kq + "sgn", kq + "scl"], writes=[kq + "sgn"], tag=tag + "ss")
    E.op("vector", lambda e: e.tensor_tensor(
        out_ap, acc[:, 0:width], sgn[:, 0:width], ALU.mult),
        reads=[kq + "acc", kq + "sgn"], writes=[kout], tag=tag + "out")


def make_fp4_scratch(ctx, S, pfx, width):
    return {"mx": ctx.sb(S, pfx + "mx", [128, width // 16], F32),
            "scl": ctx.sb(S, pfx + "scl", [128, width // 16], F32),
            "y": ctx.sb(S, pfx + "y", [128, width], F32),
            "acc": ctx.sb(S, pfx + "acc", [128, width], BF16),
            "term": ctx.sb(S, pfx + "term", [128, width], BF16),
            "sgn": ctx.sb(S, pfx + "sgn", [128, width], BF16),
            "key": pfx}


def emit_s6(ctx):
    E = ctx.E
    I_ = 2816
    NI = I_ // 128
    xn1_d = ctx.dio("xn1", (B_CORE, T, D), F32, producer="s4", consumers=("s6",))
    f1_d = ctx.dio("f1", (B_CORE, T, D), F32, producer="s6", consumers=("s7",))
    wgT_d = ctx.dscratch("s6_wgT", (D, I_), BF16)
    wuT_d = ctx.dscratch("s6_wuT", (D, I_), BF16)
    wdT_d = ctx.dscratch("s6_wdT", (I_, D), BF16)
    hqT_d = ctx.dscratch("s6_hqT", (B_CORE, I_, T), BF16)

    w_g = ctx.din("w_L1_ffn_w_gate", (I_, D), F32)
    w_u = ctx.din("w_L1_ffn_w_up", (I_, D), F32)
    w_d = ctx.din("w_L1_ffn_w_down", (D, I_), F32)

    with ctx.stage("s6") as S:
        if S is None:
            return
        qs2 = [make_fp4_scratch(ctx, S, f"s6q{i}", I_ if i == 0 else 512) for i in range(2)]
        qs = qs2[0]
        wnat = ctx.sb(S, "s6wn", [128, I_], F32)
        wq = ctx.sb(S, "s6wq", [128, I_], BF16)
        piece = [ctx.sb(S, "s6pc", [128, 512], BF16) for _ in range(2)]
        for (wsrc, wdst, nr, ncol, kw) in ((w_g, wgT_d, NI, 8, "wg"), (w_u, wuT_d, NI, 8, "wu"),
                                           (w_d, wdT_d, 8, NI, "wd")):
            width = ncol * 128
            for rc in range(nr):
                E.dma("sync", wnat[:, 0:width], wsrc[tile_slice(rc), :],
                      writes=["s6wn"], tag="s6ldw")
                emit_fp4_quant(ctx, wq[:, 0:width], "s6wq", wnat[:, 0:width], "s6wn",
                               width, qs, "s6qw")
                for cc0 in range(0, ncol, 4):
                    ng = min(4, ncol - cc0)
                    bk = 6 + (cc0 // 4) % 2
                    for cc in range(cc0, cc0 + ng):
                        E.op("tensor", lambda e, cc=cc, bk=bk, cc0=cc0: e.transpose(
                            ctx.psbf6(bk, cc - cc0), wq[:, tile_slice(cc)], ctx.consts["ident_bf"][:]),
                            reads=["s6wq", "c_ident_bf"], writes=[f"ps{bk}"])
                    pp = (cc0 // 4) % 2
                    E.op("vector", lambda e, pp=pp, ng=ng, bk=bk: e.tensor_copy(
                        piece[pp][:, 0:ng * 128], ctx.ps[bk][:, 0:ng * 64].bitcast(BF16)),
                        reads=[f"ps{bk}"], writes=[f"s6pc{pp}"])
                    E.dma("sync",
                          wdst[cc0 * 128:(cc0 + ng) * 128, tile_slice(rc)]
                          .rearrange("(a p) c -> p a c", p=128),
                          piece[pp][:, 0:ng * 128].rearrange("p (a c) -> p a c", c=128),
                          reads=[f"s6pc{pp}"], writes=[kw], tag="s6stw")

        xrow = ctx.sb(S, "s6xr", [128, D], F32)
        xrb = ctx.sb(S, "s6xrb", [128, D], BF16)
        xqT = [ctx.sb(S, "s6xqT", [128, T], BF16) for _ in range(8)]
        wgc = [ctx.sb(S, "s6wgc", [128, 512], BF16) for _ in range(8)]
        wuc = [ctx.sb(S, "s6wuc", [128, 512], BF16) for _ in range(8)]
        gs2 = [ctx.sb(S, "s6gs", [128, 512], BF16) for _ in range(2)]
        hf2 = [ctx.sb(S, "s6hf", [128, 512], F32) for _ in range(2)]
        hq2 = [ctx.sb(S, "s6hq", [128, 512], BF16) for _ in range(2)]
        hqt = [ctx.sb(S, "s6hqt", [128, 512], BF16) for _ in range(NI)]
        wds = ctx.sb(S, "s6wds", [128, 4 * NI * 128], BF16)
        fstg = ctx.sb(S, "s6fs", [128, 512], F32)
        trs = [ctx.sb(S, "s6tr", [128, 512], F32) for _ in range(2)]

        IC = [(0, 512), (512, 512), (1024, 512), (1536, 512), (2048, 512), (2560, 256)]

        for b in range(B_CORE):
            for tt in range(16):
                E.dma("sync", xrow[:], xn1_d[b, tile_slice(tt), :], writes=["s6xr"], tag="s6ldx")
                emit_fp4_quant(ctx, xrb[:], "s6xrb", xrow[:], "s6xr", D, qs, "s6qx")
                for ht in range(8):
                    E.op("tensor", lambda e, ht=ht: e.transpose(
                        ctx.psbf6(6, ht), xrb[:, tile_slice(ht)], ctx.consts["ident_bf"][:]),
                        reads=["s6xrb", "c_ident_bf"], writes=["ps6"])
                for ht in range(8):
                    E.op("vector", lambda e, ht=ht, tt=tt: e.tensor_copy(
                        xqT[ht][:, tile_slice(tt)], ctx.psbf6(6, ht)),
                        reads=["ps6"], writes=[f"xq{ht}"])
            for (ic0, icw) in IC:
                for ht in range(8):
                    E.dma("sync", wgc[ht][:, 0:icw], wgT_d[tile_slice(ht), ic0:ic0 + icw],
                          reads=["wg"], writes=[f"wgc{ht}"], tag="s6ldg")
                    E.dma("sync", wuc[ht][:, 0:icw], wuT_d[tile_slice(ht), ic0:ic0 + icw],
                          reads=["wu"], writes=[f"wuc{ht}"], tag="s6ldu")
                for tt in range(16):
                    b_ = tt % 2
                    for ht in range(8):
                        E.op("tensor", lambda e, ht=ht, tt=tt, ic0=ic0, icw=icw, b_=b_: e.matmul(
                            ctx.ps[b_][:, 0:icw], xqT[ht][:, tile_slice(tt)],
                            wgc[ht][:, 0:icw], start=(ht == 0), stop=(ht == 7),
                            skip_group_check=True),
                            reads=[f"xq{ht}", f"wgc{ht}"], writes=[f"ps{b_}"], tag="s6gmm")
                        E.op("tensor", lambda e, ht=ht, tt=tt, ic0=ic0, icw=icw, b_=b_: e.matmul(
                            ctx.ps[2 + b_][:, 0:icw], xqT[ht][:, tile_slice(tt)],
                            wuc[ht][:, 0:icw], start=(ht == 0), stop=(ht == 7),
                            skip_group_check=True),
                            reads=[f"xq{ht}", f"wuc{ht}"], writes=[f"ps{2 + b_}"], tag="s6umm")
                    gs = gs2[b_]
                    hf = hf2[b_]
                    hq = hq2[b_]
                    E.op("scalar", lambda e, icw=icw, b_=b_, gs=gs: e.activation(
                        gs[:, 0:icw], ctx.ps[b_][:, 0:icw], AF.Silu),
                        reads=[f"ps{b_}"], writes=[f"s6gs{b_}"], tag="s6silu")
                    E.op("vector", lambda e, icw=icw, b_=b_, gs=gs, hf=hf: e.tensor_tensor(
                        hf[:, 0:icw], gs[:, 0:icw], ctx.ps[2 + b_][:, 0:icw], ALU.mult),
                        reads=[f"s6gs{b_}", f"ps{2 + b_}"], writes=[f"s6hf{b_}"], tag="s6gu")
                    emit_fp4_quant(ctx, hq[:, 0:icw], f"s6hq{b_}", hf[:, 0:icw], f"s6hf{b_}",
                                   icw, qs2[b_], "s6qh")
                    nk2 = icw // 128
                    for k2 in range(nk2):
                        E.op("tensor", lambda e, k2=k2, hq=hq: e.transpose(
                            ctx.psbf6(6, k2), hq[:, tile_slice(k2)], ctx.consts["ident_bf"][:]),
                            reads=[f"s6hq{b_}", "c_ident_bf"], writes=["ps6"])
                    pp = tt % 2
                    E.op("vector", lambda e, pp=pp, nk2=nk2: e.tensor_copy(
                        piece[pp][:, 0:nk2 * 128], ctx.ps[6][:, 0:nk2 * 64].bitcast(BF16)),
                        reads=["ps6"], writes=[f"s6pc{pp}"])
                    E.dma("sync",
                          hqT_d[b, ic0:ic0 + nk2 * 128, tile_slice(tt)]
                          .rearrange("(a p) c -> p a c", p=128),
                          piece[pp][:, 0:nk2 * 128].rearrange("p (a c) -> p a c", c=128),
                          reads=[f"s6pc{pp}"], writes=[f"hqT_{b}"], tag="s6sth")
            for oth in range(2):
                for tc in range(4):
                    for ii in range(NI):
                        E.dma("sync", hqt[ii][:], hqT_d[b, tile_slice(ii), tile_slice(tc, 512)],
                              reads=[f"hqT_{b}"], writes=[f"hqt{ii}"], tag="s6ldh")
                    for ol in range(4):
                        ot = oth * 4 + ol
                        if tc == 0:
                            E.dma("sync", wds[:, ol * NI * 128:(ol + 1) * NI * 128]
                                  .rearrange("p (a c) -> p a c", c=128),
                                  wdT_d.rearrange("(a p) c -> p a c", p=128)[:, :, tile_slice(ot)],
                                  reads=["wd"], writes=[f"wds{ol}"], tag="s6ldwd")
                        for ii in range(NI):
                            E.op("tensor", lambda e, ii=ii, ol=ol, tc=tc: e.matmul(
                                ctx.ps[4 + (tc % 2)][:, :], wds[:, (ol * NI + ii) * 128:(ol * NI + ii + 1) * 128],
                                hqt[ii][:], start=(ii == 0), stop=(ii == NI - 1),
                                skip_group_check=True),
                                reads=[f"hqt{ii}", f"wds{ol}"], writes=[f"ps{4 + (tc % 2)}"], tag="s6dmm")
                        E.op("scalar", lambda e, tc=tc: e.copy(fstg[:], ctx.ps[4 + (tc % 2)][:, :]),
                             reads=[f"ps{4 + (tc % 2)}"], writes=["s6fs"], tag="s6dev")
                        for t2 in range(4):
                            E.op("tensor", lambda e, t2=t2: e.transpose(
                                ctx.ps[1][:, t2 * 128:(t2 + 1) * 128], fstg[:, tile_slice(t2)],
                                ctx.consts["ident_f32"][:]),
                                reads=["s6fs", "c_ident_f32"], writes=["ps1"], tag="s6ftr")
                        pp = ot % 2
                        E.op("scalar", lambda e, pp=pp: e.copy(trs[pp][:], ctx.ps[1][:, :]),
                             reads=["ps1"], writes=[f"s6tr{pp}"])
                        E.dma("sync", f1_d[b, tc * 512:(tc + 1) * 512, tile_slice(ot)]
                              .rearrange("(a p) c -> p a c", p=128),
                              trs[pp][:].rearrange("p (a c) -> p a c", c=128),
                              reads=[f"s6tr{pp}"], writes=[f"d_f1_{b}"], tag="s6stf")


def emit_s7(ctx):
    E = ctx.E
    x1_d = ctx.dio("x1", (B_CORE, T, D), F32, producer="s4", consumers=("s7",))
    s1_d = ctx.dio("s1", (B_CORE, T, D), F32, producer="s5", consumers=("s7",))
    f1_d = ctx.dio("f1", (B_CORE, T, D), F32, producer="s6", consumers=("s7",))
    out_d = ctx.dout("out", (B_CORE, T, D), F32)

    with ctx.stage("s7") as S:
        if S is None:
            return
        xt = ctx.sb(S, "s7x", [128, D], F32)
        st = ctx.sb(S, "s7s", [128, D], F32)
        ft = ctx.sb(S, "s7f", [128, D], F32)
        for b in range(B_CORE):
            for tt in range(16):
                E.dma("sync", xt[:], x1_d[b, tile_slice(tt), :], writes=["s7x"], tag="s7ldx")
                E.dma("sync", st[:], s1_d[b, tile_slice(tt), :], writes=["s7s"], tag="s7lds")
                E.dma("sync", ft[:], f1_d[b, tile_slice(tt), :], writes=["s7f"], tag="s7ldf")
                E.op("scalar", lambda e: e.activation(ft[:], ft[:], AF.Silu),
                     reads=["s7f"], writes=["s7f"], tag="s7silu")
                E.op("vector", lambda e: e.tensor_tensor(ft[:], ft[:], st[:], ALU.mult),
                     reads=["s7f", "s7s"], writes=["s7f"], tag="s7sg")
                E.op("vector", lambda e: e.tensor_tensor(xt[:], xt[:], ft[:], ALU.add),
                     reads=["s7x", "s7f"], writes=["s7x"], tag="s7add")
                E.dma("sync", out_d[b, tile_slice(tt), :], xt[:],
                      reads=["s7x"], writes=[f"d_out_{b}_{tt}"], tag="s7st")



def build_program(stages, fused, host_consts):
    nc = bass.Bass("TRN2", target_bir_lowering=False)
    stack = contextlib.ExitStack()
    with stack:
        ctx = Ctx(nc, stack, fused, stages)
        ctx.init_psum()
        ctx.load_consts(host_consts, names=("ident_bf", "ident_f32"))
        need_rope = any(st_ in stages for st_ in ("s1", "s2"))
        with contextlib.ExitStack() as rope_scope:
            if need_rope:
                ctx.load_consts(host_consts,
                                names=("rot64_mt", "rot32_mt", "cos64", "sin64",
                                       "cos32", "sin32", "stair"),
                                scope=rope_scope)
            emit_s0(ctx)
            emit_s1(ctx)
            emit_s2(ctx)
        ctx.E.barrier()
        emit_s3(ctx)
        emit_s4(ctx)
        emit_s5(ctx)
        emit_s6(ctx)
        emit_s7(ctx)
        ctx.E.barrier()
        ctx.E.emit()
    return nc, ctx


def kernel(**inputs):
    host_consts = build_consts()
    x_full = np.ascontiguousarray(inputs["L0_attn_x"], dtype=np.float32)
    nc, ctx = build_program(ALL_STAGES, True, host_consts)

    weight_arrays = {}
    for k, v in inputs.items():
        if k == "L0_attn_x":
            continue
        name = "w_" + k
        if name in ctx.dram:
            weight_arrays[name] = np.ascontiguousarray(v, dtype=np.float32)
    const_arrays = {("c_" + n): np.asarray(a) for n, a in host_consts.items()
                    if ("c_" + n) in ctx.dram}

    in_maps = []
    for core in range(8):
        m = {"x": x_full[core * B_CORE:(core + 1) * B_CORE]}
        m.update(const_arrays)
        m.update(weight_arrays)
        in_maps.append(m)
    res = run_bass_kernel_spmd(nc, in_maps, core_ids=list(range(8)))
    out = np.concatenate([np.asarray(r["out"]) for r in res.results], axis=0)
    return out
